# version 05
# baseline (speedup 1.0000x reference)
import numpy as np
from contextlib import ExitStack
import concourse.bass as bass
import concourse.mybir as mybir

F32 = mybir.dt.float32
BF16 = mybir.dt.bfloat16
I32 = mybir.dt.int32
U8 = mybir.dt.uint8
AF = mybir.ActivationFunctionType
ALU = mybir.AluOpType
AX = mybir.AxisListType

T = 4096
D = 1024
NTT = T // 128
NDT = D // 128
ENGS = ("sync", "vector", "scalar", "tensor", "gpsimd")


class Sem:
    _uid = [0]

    def __init__(self, prog, name):
        Sem._uid[0] += 1
        name = f"{name}_{Sem._uid[0]}"
        self.h = prog.es.enter_context(prog.nc.semaphore(name))
        self.n = 0
        self.base = 0
        self.name = name


DETECT_RACES = True

class Prog:
    def __init__(self, name):
        self.nc = bass.Bass("TRN2", target_bir_lowering=False, name=name,
                            detect_race_conditions=DETECT_RACES)
        self.es = ExitStack()
        self.ins = {}
        self.outs = {}
        self.q = {e: [] for e in ENGS}
        self.consts = {}
        self.cst = None
        self._ord = {}
        self._last = {}
        self.io_map = {}
        self._scopes = []
        self._sem_free = []
        self._sem_free_swdge = []
        self._phase_sems = []
        self._pending = {}
        self._seen = {e: {} for e in ENGS}
        self._phb = None
        self._global_sems = False

    def inp(self, name, shape, dtype=F32):
        if name in self.io_map:
            return self.io_map[name]
        if name in self.ins:
            return self.ins[name]
        ap = self.nc.dram_tensor(name, list(shape), dtype, kind="ExternalInput").ap()
        self.ins[name] = ap
        return ap

    def out(self, name, shape, dtype=F32):
        if name in self.io_map:
            return self.io_map[name]
        ap = self.nc.dram_tensor(name, list(shape), dtype, kind="ExternalOutput").ap()
        self.outs[name] = ap
        return ap

    def sb(self, name, shape, dtype):
        if self._scopes:
            name = f"p{self._phase_no}_{name}"
            return self._scopes[-1].enter_context(
                self.nc.sbuf_tensor(name, list(shape), dtype))
        return self.es.enter_context(self.nc.sbuf_tensor(name, list(shape), dtype))

    def ps(self, name, shape, dtype=F32):
        if self._scopes:
            name = f"p{self._phase_no}_{name}"
            return self._scopes[-1].enter_context(
                self.nc.psum_tensor(name, list(shape), dtype))
        return self.es.enter_context(self.nc.psum_tensor(name, list(shape), dtype))

    def sem(self, name, _global=False):
        if self._scopes and not _global:
            if self._sem_free:
                h, cum = self._sem_free.pop()
                s = Sem.__new__(Sem)
                s.h, s.n, s.base, s.name = h, 0, cum, name
            else:
                s = Sem(self, name + "_pool")
            self._phase_sems.append(s)
            return s
        return Sem(self, name)

    def begin_phase(self):
        self._phase_no = getattr(self, "_phase_no", -1) + 1
        self._scopes.append(ExitStack())

    def end_phase(self):
        if self._phb is None:
            self._phb = Sem(self, "phb")
        waits = [(s, v) for (s, v) in self._last.values()]
        self.do("sync", lambda e: e.nop(), wait=waits, inc=self._phb, chain=False)
        tgt = self._phb.n
        self._last = {}
        self._ord = {}
        for eng in ENGS:
            self._pending.setdefault(eng, []).append((self._phb, tgt))
        for s in self._phase_sems:
            if getattr(s, "swdge", False):
                self._sem_free_swdge.append((s.h, s.base + s.n))
            else:
                self._sem_free.append((s.h, s.base + s.n))
        self._phase_sems = []
        self._scopes.pop().close()

    def cbias(self, value, dtype=F32):
        key = (dtype, float(value))
        if key not in self.consts:
            t = self.es.enter_context(self.nc.sbuf_tensor(
                f"c_{len(self.consts)}", [128, 1], dtype))
            self.consts[key] = t
            self.nc.const_aps.register(dtype, float(value), t[:])
        if self.cst is None:
            self.cst = self.sem("cst", _global=True)
            self.cst.n = 1
        return self.consts[key][:]

    def cwait(self):
        return [(self.cst, 1)] if self.cst is not None else []

    def do(self, eng, fn, wait=(), inc=None, inc_k=1, chain=True):
        if self._pending.get(eng):
            wait = list(wait) + self._pending.pop(eng)
        seen = self._seen[eng]
        w = []
        for (s, v) in wait:
            tv = s.base + v
            hid = id(s.h)
            if tv <= 0 or seen.get(hid, -1) >= tv:
                continue
            seen[hid] = tv
            w.append((s.h, tv))
        if eng in ("vector", "scalar", "gpsimd") and chain:
            if eng in self._last:
                ls, lv = self._last[eng]
                tv = ls.base + lv
                hid = id(ls.h)
                if seen.get(hid, -1) < tv:
                    seen[hid] = tv
                    w.append((ls.h, tv))
            if inc is None:
                if eng not in self._ord:
                    self._ord[eng] = self.sem("ord_" + eng)
                inc = self._ord[eng]
        if inc is not None:
            inc.n += inc_k
            tgt = (inc.h, inc_k)
            if eng in ("vector", "scalar", "gpsimd") and chain:
                self._last[eng] = (inc, inc.n)
        else:
            tgt = None
        self.q[eng].append((w, fn, tgt))

    def dma(self, eng, dst, src, wait=(), inc=None, **kw):
        if eng == "gpsimd" and inc is not None and self._scopes:
            self._make_swdge(inc)
        self.do(eng, lambda e: e.dma_start(out=dst, in_=src, **kw),
                wait=wait, inc=inc, inc_k=16, chain=False)

    def _make_swdge(self, s):
        if getattr(s, "swdge", False):
            return
        assert s.n == 0, f"sem {s.name} got non-swdge incs before gpsimd dma"
        if s in self._phase_sems:
            self._sem_free.append((s.h, s.base))
        if self._sem_free_swdge:
            h, cum = self._sem_free_swdge.pop()
            s.h, s.base = h, cum
        else:
            tmp = Sem(self, s.name + "_sw")
            s.h, s.base = tmp.h, 0
        s.swdge = True

    def finalize(self):
        if self.consts:
            items = list(self.consts.items())
            pre = []
            for i, ((dt_, v), t) in enumerate(items):
                inc = (self.cst.h, 1) if i == len(items) - 1 else None
                pre.append(([], (lambda t, v: lambda e: e.memset(t[:], v))(t, v), inc))
            self.q["gpsimd"] = pre + self.q["gpsimd"]
        blk = self.es.enter_context(self.nc.Block())
        for e in ENGS:
            items = self.q[e]
            if not items:
                continue

            def mk(items):
                def body(eng):
                    for w, fn, tgt in items:
                        for (h, v) in w:
                            eng.wait_ge(h, v)
                        inst = fn(eng)
                        if tgt is not None:
                            assert inst is not None, "inc on op that returned None"
                            inst.then_inc(*tgt)
                return body
            getattr(blk, e)(mk(items))
        self.es.close()
        return self.nc


def ceil_div(a, b):
    return -(-a) // b


def mm_sweep(p, name, nk, nots, nfs, lhsT_ap, rhs_ap, psum_banks, evac,
             pe_sem, wait_first=(), evac_done_sem=None, pwidth=128, fwidth=512):
    nb = len(psum_banks)
    gi = 0
    for ot in range(nots):
        for f in range(nfs):
            bank = psum_banks[gi % nb]
            for k in range(nk):
                w = list(wait_first(ot, f, k) if callable(wait_first) else wait_first) if k == 0 else []
                if k == 0 and gi >= nb and evac_done_sem is not None:
                    w.append((evac_done_sem, gi - nb + 1))
                inc = pe_sem if k == nk - 1 else None
                p.do("tensor",
                     (lambda l, r, b, s0, s1: lambda e: e.matmul(
                         b, l, r, start=s0, stop=s1))(
                         lhsT_ap(k, ot), rhs_ap(k, f), bank[:, 0:fwidth],
                         k == 0, k == nk - 1),
                     wait=w, inc=inc)
            evac(ot, f, bank, pe_sem.n)
            gi += 1


def build_l0a(p=None):
    if p is None:
        p = Prog("l0a")
    x = p.inp("x", [T, D])
    wT = p.inp("in_proj_wT", [D, 4096])
    idn_in = p.inp("idn", [128, 128], BF16)
    xn_tok = p.out("xn_tok", [T, D], F32)
    xc_d = p.out("xc_d", [2048, T], BF16)
    zsil_d = p.out("zsil_d", [2048, T], BF16)

    xbuf = [p.sb(f"xbuf{i}", [128, D], F32) for i in range(2)]
    sqscr = [p.sb(f"sqscr{i}", [128, D], F32) for i in range(2)]
    s1 = p.sb("s1", [128, NTT], F32)
    s2 = p.sb("s2", [128, NTT], F32)
    mean = p.sb("mean", [128, NTT], F32)
    var = p.sb("var", [128, NTT], F32)
    sd = p.sb("sd", [128, NTT], F32)
    rstd = p.sb("rstd", [128, NTT], F32)
    xnt32 = [p.sb(f"xnt32{i}", [128, D], F32) for i in range(2)]
    xnt = [p.sb(f"xnt{i}", [128, D], BF16) for i in range(2)]
    xn_d = p.sb("xn_d", [128, NDT * T], BF16)
    idn = p.sb("idn_s", [128, 128], BF16)
    wbuf = [p.sb(f"wbuf{i}", [128, 8 * 128], F32) for i in range(2)]
    wbb = [p.sb(f"wbb{i}", [128, 8 * 128], BF16) for i in range(2)]
    stg = [p.sb(f"stg{i}", [128, 512], BF16) for i in range(4)]
    ps_t = [p.ps(f"ps_t{i}", [128, 1024], BF16) for i in range(2)]
    ps_m = [p.ps(f"ps_m{i}", [128, 512], F32) for i in range(4)]

    ldx = [p.sem(f"ldx{i}") for i in range(2)]
    li = p.sem("li")
    lw = [p.sem(f"lw{i}") for i in range(2)]
    vst = p.sem("vst"); sst = p.sem("sst"); vrs = p.sem("vrs")
    srs = p.sem("srs"); rrs = p.sem("rrs")
    xnv = p.sem("xnv"); pet = p.sem("pet"); evq = p.sem("evq")
    xns = [p.sem(f"xns{i}") for i in range(2)]
    gct = p.sem("gct"); pem = p.sem("pem"); evm = p.sem("evm")
    stm = [p.sem(f"stm{i}") for i in range(4)]

    p.dma("sync", idn[:], idn_in, inc=li)

    ldx_after = []
    for i in range(NTT):
        w = [(vst, i - 1), (sst, i - 1)] if i >= 2 else []
        p.dma("sync", _ap(xbuf[i % 2]), x[i * 128:(i + 1) * 128, :], wait=w, inc=ldx[i % 2])
        ldx_after.append(ldx[i % 2].n)
    for i in range(NTT):
        p.do("vector", (lambda i: lambda e: e.tensor_reduce(
            s1[:, i:i + 1], _ap(xbuf[i % 2]), AX.X, ALU.add))(i),
            wait=[(ldx[i % 2], ldx_after[i])], inc=vst)
        p.do("scalar", (lambda i: lambda e: e.activation(
            _ap(sqscr[i % 2]), _ap(xbuf[i % 2]), AF.Square, accum_out=s2[:, i:i + 1]))(i),
            wait=[(ldx[i % 2], ldx_after[i])], inc=sst)
    stc = p.sem("stc")
    p.do("vector", lambda e: e.tensor_scalar(mean[:], s1[:], 1.0 / D, None, ALU.mult),
         wait=[(sst, NTT), (vst, NTT)], inc=stc)
    p.do("vector", lambda e: e.tensor_tensor(var[:], mean[:], mean[:], ALU.mult),
         wait=[(stc, 1)], inc=stc)
    p.do("vector", lambda e: e.scalar_tensor_tensor(
        var[:], s2[:], 1.0 / D, var[:], ALU.mult, ALU.subtract),
        wait=[(stc, 2)], inc=vrs)
    eps_ap = p.cbias(1e-5)
    p.do("scalar", lambda e: e.activation(sd[:], var[:], AF.Sqrt, bias=eps_ap, scale=1.0),
         wait=[(vrs, 1)] + p.cwait(), inc=srs)
    p.do("vector", lambda e: e.reciprocal(rstd[:], sd[:]), wait=[(srs, 1)], inc=rrs)

    xnv_after = []; xns_after = []; pet_after = []; evq_after = []
    ldx2_after = []
    for i in range(NTT):
        w = [(rrs, 1)] if i < 2 else [(xnv, i - 1)]
        p.dma("sync", _ap(xbuf[i % 2]), x[i * 128:(i + 1) * 128, :], wait=w, inc=ldx[i % 2])
        ldx2_after.append(ldx[i % 2].n)
    for i in range(NTT):
        w = [(ldx[i % 2], ldx2_after[i]), (rrs, 1), (stc, 1)]
        if i >= 2:
            w += [(xns[i % 2], xns_after[i - 2]), (pet, pet_after[i - 2])]
        p.do("vector", (lambda i: lambda e: e.tensor_scalar(
            xnt32[i % 2][:], xbuf[i % 2][:], mean[:, i:i + 1], rstd[:, i:i + 1],
            ALU.subtract, ALU.mult))(i), wait=w)
        p.do("vector", (lambda i: lambda e: e.tensor_copy(
            xnt[i % 2][:], xnt32[i % 2][:]))(i), inc=xnv)
        xnv_after.append(xnv.n)
        p.dma("gpsimd", xn_tok[i * 128:(i + 1) * 128, :], xnt32[i % 2][:],
              wait=[(xnv, xnv_after[i])], inc=xns[i % 2])
        xns_after.append(xns[i % 2].n)
        for j in range(NDT):
            w = []
            if j == 0:
                w = [(xnv, xnv_after[i]), (li, 16)]
                if i >= 2:
                    w.append((evq, evq_after[i - 2]))
            p.do("tensor", (lambda i, j: lambda e: e.transpose(
                ps_t[i % 2][:, j * 128:(j + 1) * 128], xnt[i % 2][:, j * 128:(j + 1) * 128],
                idn[:]))(i, j), wait=w, inc=(pet if j == NDT - 1 else None))
        pet_after.append(pet.n)
        p.do("vector", (lambda i: lambda e: e.tensor_copy(
            xn_d[:].rearrange("p (dt t) -> p dt t", t=T)[:, :, i * 128:(i + 1) * 128],
            ps_t[i % 2][:].rearrange("p (dt c) -> p dt c", c=128)))(i),
            wait=[(pet, pet_after[i])], inc=evq)
        evq_after.append(evq.n)

    lw_after = {}; gct_after = {}; pem_after = {}; evm_after = {}; stm_after = {}
    xnd = xn_d[:].rearrange("p (dt t) -> p dt t", t=T)
    g = 0
    for ot in range(32):
        w = [(gct, ot - 1)] if ot >= 2 else []
        p.dma("sync", wbuf[ot % 2][:].rearrange("p (kt o) -> p kt o", o=128),
              wT[:, ot * 128:(ot + 1) * 128].rearrange("(kt p) o -> p kt o", p=128),
              wait=w, inc=lw[ot % 2])
        lw_after[ot] = lw[ot % 2].n
    for ot in range(32):
        w = [(lw[ot % 2], lw_after[ot])]
        if ot >= 2:
            w.append((pem, 8 * (ot - 1)))
        p.do("gpsimd", (lambda ot: lambda e: e.tensor_copy(wbb[ot % 2][:], wbuf[ot % 2][:]))(ot),
             wait=w, inc=gct)
        gct_after[ot] = gct.n
        for f in range(8):
            bank = ps_m[g % 4]
            for k in range(8):
                w = []
                if k == 0:
                    w = [(gct, gct_after[ot])]
                    if g == 0:
                        w.append((evq, NTT))
                    if g >= 4:
                        w.append((evm, evm_after[g - 4]))
                p.do("tensor", (lambda ot, f, k, bank, s0, s1_: lambda e: e.matmul(
                    bank[:], wbb[ot % 2][:, k * 128:(k + 1) * 128],
                    xnd[:, k, f * 512:(f + 1) * 512], start=s0, stop=s1_))(
                    ot, f, k, bank, k == 0, k == 7),
                    wait=w, inc=(pem if k == 7 else None))
            pem_after[(ot, f)] = pem.n
            w = [(pem, pem_after[(ot, f)])]
            if g >= 4:
                w.append((stm[g % 4], stm_after[g - 4]))
            func = AF.Copy if ot < 16 else AF.Silu
            p.do("scalar", (lambda bank, g, func: lambda e: e.activation(
                stg[g % 4][:], bank[:], func))(bank, g, func), wait=w, inc=evm)
            evm_after[g] = evm.n
            if ot < 16:
                dst = xc_d[ot * 128:(ot + 1) * 128, f * 512:(f + 1) * 512]
            else:
                dst = zsil_d[(ot - 16) * 128:(ot - 16 + 1) * 128, f * 512:(f + 1) * 512]
            p.dma("gpsimd", dst, stg[g % 4][:], wait=[(evm, evm_after[g])], inc=stm[g % 4])
            stm_after[g] = stm[g % 4].n
            g += 1

    p.do("sync", lambda e: e.nop(),
         wait=[(s, s.n) for s in stm] + [(s, s.n) for s in xns])
    return p


def build_l0b1(p=None):
    if p is None:
        p = Prog("l0b1")
    xc = p.inp("xc_d", [2048, T], BF16)
    cw = p.inp("conv_wT", [4, 2048, 2048])
    cb = p.inp("conv_b_re", [128, 16])
    u_d = p.out("u_d", [2048, T], BF16)

    XW = 4104
    xcb = p.sb("xcb", [128, 16 * XW], BF16)
    slab = [p.sb(f"slab{i}", [128, 4 * 128], F32) for i in range(4)]
    wcb = [p.sb(f"wcb{i}", [128, 64 * 128], BF16) for i in range(2)]
    cbs = p.sb("cbs", [128, 16], F32)
    stg = [p.sb(f"stg{i}", [128, 512], BF16) for i in range(4)]
    ps_m = [p.ps(f"ps_m{i}", [128, 512], F32) for i in range(4)]

    ldc = p.sem("ldc"); ldb = p.sem("ldb")
    ldw = [p.sem(f"ldw{i}") for i in range(4)]
    cst = p.sem("cstw"); pem = p.sem("pem"); evm = p.sem("evm")
    stm = [p.sem(f"stm{i}") for i in range(4)]
    msz = p.sem("msz")

    p.do("vector", lambda e: e.memset(
        xcb[:].rearrange("p (t w) -> p t w", w=XW)[:, :, 0:8], 0.0), inc=msz)
    for it in range(16):
        p.dma("sync", xcb[:, it * XW + 8: it * XW + 8 + T],
              xc[it * 128:(it + 1) * 128, :], inc=ldc)
    p.dma("sync", cbs[:], cb, inc=ldb)

    ldw_after = {}; cst_after = {}; pem_after = {}; evm_after = {}; stm_after = {}
    g = 0
    si = 0
    for oct in range(16):
        for it in range(16):
            s = si % 4
            w = []
            if si >= 4:
                w.append((cst, cst_after[si - 4]))
            p.dma("sync", slab[s][:].rearrange("p (k c) -> p k c", c=128),
                  cw[:, it * 128:(it + 1) * 128, oct * 128:(oct + 1) * 128]
                  .rearrange("k p c -> p k c"), wait=w, inc=ldw[s])
            ldw_after[si] = ldw[s].n
            w = [(ldw[s], ldw_after[si])]
            if oct >= 2 and it == 0:
                w.append((pem, 8 * (oct - 1)))
            p.do("gpsimd", (lambda oct, it, s: lambda e: e.tensor_copy(
                wcb[oct % 2][:, (it * 4) * 128:(it * 4 + 4) * 128], slab[s][:]))(oct, it, s),
                wait=w, inc=cst)
            cst_after[si] = cst.n
            si += 1
        for f in range(8):
            bank = ps_m[g % 4]
            ki = 0
            for it in range(16):
                for k in range(4):
                    w = []
                    if ki == 0:
                        w = [(cst, cst_after[oct * 16 + 15]), (msz, 1)]
                        if g == 0:
                            w.append((ldc, 16 * 16))
                        if g >= 4:
                            w.append((evm, evm_after[g - 4]))
                    src = xcb[:, it * XW + 8 + (k - 3) + f * 512: it * XW + 8 + (k - 3) + f * 512 + 512]
                    p.do("tensor", (lambda lt, rt, bank, s0, s1_: lambda e: e.matmul(
                        bank[:], lt, rt, start=s0, stop=s1_))(
                        wcb[oct % 2][:, (it * 4 + k) * 128:(it * 4 + k + 1) * 128],
                        src, bank, ki == 0, ki == 63),
                        wait=w, inc=(pem if ki == 63 else None))
                    ki += 1
            pem_after[g] = pem.n
            w = [(pem, pem_after[g]), (ldb, 16)]
            if g >= 4:
                w.append((stm[g % 4], stm_after[g - 4]))
            p.do("vector", (lambda bank, g, oct: lambda e: e.tensor_scalar(
                stg[g % 4][:], bank[:], cbs[:, oct:oct + 1], None, ALU.add))(bank, g, oct),
                wait=w, inc=evm)
            evm_after[g] = evm.n
            p.dma("gpsimd", u_d[oct * 128:(oct + 1) * 128, f * 512:(f + 1) * 512],
                  stg[g % 4][:], wait=[(evm, evm_after[g])], inc=stm[g % 4])
            stm_after[g] = stm[g % 4].n
            g += 1
    p.do("sync", lambda e: e.nop(), wait=[(s, s.n) for s in stm])
    return p


def build_l0b2(p=None):
    if p is None:
        p = Prog("l0b2")
    u = p.inp("u_d", [2048, T], BF16)
    dwT = p.inp("dt_wT", [2048, 2048])
    dbr = p.inp("dt_b_re", [128, 16])
    bwT = p.inp("B_wT", [2048, 16])
    cwT = p.inp("C_wT", [2048, 16])
    dt_d = p.out("dt_d", [2048, T], BF16)
    dtu_d = p.out("dtu_d", [2048, T], BF16)
    bm = p.out("Bm", [16, T])
    cm = p.out("Cm", [16, T])

    ub = p.sb("ub", [128, 16 * T], BF16)
    dwf = [p.sb(f"dwf{i}", [128, 16 * 128], F32) for i in range(2)]
    dwb = [p.sb(f"dwb{i}", [128, 16 * 128], BF16) for i in range(2)]
    dbs = p.sb("dbs", [128, 16], F32)
    bcf = p.sb("bcf", [128, 512], F32)
    bcb = p.sb("bcb", [128, 512], BF16)
    scr = [p.sb(f"scr{i}", [128, 512], F32) for i in range(2)]
    stg = [p.sb(f"stg{i}", [128, 512], BF16) for i in range(4)]
    stg2 = [p.sb(f"stg2{i}", [128, 512], BF16) for i in range(4)]
    stgb = [p.sb(f"stgb{i}", [16, 512], F32) for i in range(4)]
    ps_m = [p.ps(f"ps_m{i}", [128, 512], F32) for i in range(4)]
    ps_b = [p.ps(f"ps_b{i}", [16, 512], F32) for i in range(2)]

    ldu = p.sem("ldu"); ldb = p.sem("ldb")
    ldw = [p.sem(f"ldw{i}") for i in range(2)]
    cstw = p.sem("cstw2"); pem = p.sem("pem"); evs = p.sem("evs"); evv = p.sem("evv")
    evu = p.sem("evu")
    stm = [p.sem(f"stm{i}") for i in range(4)]
    stm2 = [p.sem(f"stm2{i}") for i in range(4)]
    stmb = [p.sem(f"stmb{i}") for i in range(4)]
    peb = p.sem("peb"); evb = p.sem("evb")

    for it in range(16):
        p.dma("sync", ub[:, it * T:(it + 1) * T], u[it * 128:(it + 1) * 128, :], inc=ldu)
    p.dma("sync", dbs[:], dbr, inc=ldb)
    p.dma("sync", bcf[:, 0:256].rearrange("p (t n) -> p t n", n=16),
          bwT.rearrange("(t p) n -> p t n", p=128), inc=ldb)
    p.dma("sync", bcf[:, 256:512].rearrange("p (t n) -> p t n", n=16),
          cwT.rearrange("(t p) n -> p t n", p=128), inc=ldb)
    p.do("gpsimd", lambda e: e.tensor_copy(bcb[:], bcf[:]), wait=[(ldb, 48)], inc=cstw)

    ubv = ub[:].rearrange("p (it t) -> p it t", t=T)
    ldw_after = {}; pem_after = {}; evs_after = {}; evv_after = {}; evu_after = {}
    stm_after = {}; stm2_after = {}
    g = 0
    for ot in range(16):
        s = ot % 2
        w = []
        if ot >= 2:
            w.append((cstw, 1 + (ot - 1)))
        p.dma("sync", dwf[s][:].rearrange("p (t c) -> p t c", c=128),
              dwT[:, ot * 128:(ot + 1) * 128].rearrange("(t p) c -> p t c", p=128),
              wait=w, inc=ldw[s])
        ldw_after[ot] = ldw[s].n
        w = [(ldw[s], ldw_after[ot])]
        if ot >= 2:
            w.append((pem, 8 * (ot - 1)))
        p.do("gpsimd", (lambda s: lambda e: e.tensor_copy(dwb[s][:], dwf[s][:]))(s),
             wait=w, inc=cstw)
        for f in range(8):
            bank = ps_m[g % 4]
            for k in range(16):
                w = []
                if k == 0:
                    w = [(cstw, 1 + (ot + 1))]
                    if g == 0:
                        w.append((ldu, 16 * 16))
                    if g >= 4:
                        w.append((evs, evs_after[g - 4]))
                p.do("tensor", (lambda l, r, bank, s0, s1_: lambda e: e.matmul(
                    bank[:], l, r, start=s0, stop=s1_))(
                    dwb[s][:, k * 128:(k + 1) * 128], ubv[:, k, f * 512:(f + 1) * 512],
                    bank, k == 0, k == 15),
                    wait=w, inc=(pem if k == 15 else None))
            pem_after[g] = pem.n
            w = [(pem, pem_after[g]), (ldb, 48)]
            if g >= 2:
                w.append((evv, evv_after[g - 2]))
            p.do("scalar", (lambda bank, g, ot: lambda e: e.activation(
                scr[g % 2][:], bank[:], AF.Sigmoid, bias=dbs[:, ot:ot + 1], scale=1.0))(bank, g, ot),
                wait=w, inc=evs)
            evs_after[g] = evs.n
            w = [(evs, evs_after[g])]
            if g >= 4:
                w.append((stm[g % 4], stm_after[g - 4]))
            p.do("vector", (lambda g: lambda e: e.tensor_scalar(
                stg[g % 4][:], scr[g % 2][:], 1e-4, None, ALU.max))(g),
                wait=w, inc=evv)
            evv_after[g] = evv.n
            p.dma("gpsimd", dt_d[ot * 128:(ot + 1) * 128, f * 512:(f + 1) * 512],
                  stg[g % 4][:], wait=[(evv, evv_after[g])], inc=stm[g % 4])
            stm_after[g] = stm[g % 4].n
            w = [(evv, evv_after[g])]
            if g >= 4:
                w.append((stm2[g % 4], stm2_after[g - 4]))
            p.do("vector", (lambda g, ot, f: lambda e: e.tensor_tensor(
                stg2[g % 4][:], stg[g % 4][:], ubv[:, ot, f * 512:(f + 1) * 512],
                ALU.mult))(g, ot, f), wait=w, inc=evu)
            evu_after[g] = evu.n
            p.dma("gpsimd", dtu_d[ot * 128:(ot + 1) * 128, f * 512:(f + 1) * 512],
                  stg2[g % 4][:], wait=[(evu, evu_after[g])], inc=stm2[g % 4])
            stm2_after[g] = stm2[g % 4].n
            g += 1
    stmb_after = {}; evb_after = {}; peb_after = {}
    gb = 0
    for bi, (dst, coff) in enumerate([(bm, 0), (cm, 16)]):
        for f in range(8):
            bank = ps_b[gb % 2]
            for k in range(16):
                w = []
                if k == 0:
                    w = [(cstw, 1), (ldu, 16 * 16)]
                    if gb >= 2:
                        w.append((evb, evb_after[gb - 2]))
                p.do("tensor", (lambda l, r, bank, s0, s1_: lambda e: e.matmul(
                    bank[:], l, r, start=s0, stop=s1_))(
                    bcb[:, bi * 256 + k * 16: bi * 256 + (k + 1) * 16],
                    ubv[:, k, f * 512:(f + 1) * 512], bank, k == 0, k == 15),
                    wait=w, inc=(peb if k == 15 else None))
            peb_after[gb] = peb.n
            w = [(peb, peb_after[gb])]
            if gb >= 4:
                w.append((stmb[gb % 4], stmb_after[gb - 4]))
            p.do("vector", (lambda bank, gb: lambda e: e.tensor_copy(
                stgb[gb % 4][:], bank[:]))(bank, gb), wait=w, inc=evb)
            evb_after[gb] = evb.n
            p.dma("gpsimd", dst[:, f * 512:(f + 1) * 512], stgb[gb % 4][:],
                  wait=[(evb, evb_after[gb])], inc=stmb[gb % 4])
            stmb_after[gb] = stmb[gb % 4].n
            gb += 1
    p.do("sync", lambda e: e.nop(),
         wait=[(s_, s_.n) for s_ in stm + stm2 + stmb])
    return p


def build_l0c(p=None):
    if p is None:
        p = Prog("l0c")
    dt = p.inp("dt_d", [2048, T], BF16)
    dtu = p.inp("dtu_d", [2048, T], BF16)
    u = p.inp("u_d", [2048, T], BF16)
    zs = p.inp("zsil_d", [2048, T], BF16)
    bm = p.inp("Bm", [16, T])
    cm = p.inp("Cm", [16, T])
    a_re = p.inp("A_re", [128, 256])
    d_re = p.inp("D_re", [128, 16])
    idn_in = p.inp("idn", [128, 128], BF16)
    y_d = p.out("y_d", [2048, T], BF16)

    H = 1024
    NQ = T // H
    ares = p.sb("ares", [128, 256], F32)
    aneg = p.sb("aneg", [128, 256], F32)
    dres = p.sb("dres", [128, 16], F32)
    idn = p.sb("idn_s", [128, 128], BF16)
    brep = p.sb("brep", [128, 16 * H], BF16)
    crep = p.sb("crep", [128, 16 * H], BF16)
    tin = [p.sb(f"tin{i}", [128, 4 * H], BF16) for i in range(2)]
    dA = [p.sb(f"dA{i}", [128, H], F32) for i in range(2)]
    dBe = [p.sb(f"dBe{i}", [128, H], BF16) for i in range(2)]
    dBe2 = [p.sb(f"dBe2{i}", [128, H], BF16) for i in range(2)]
    stt_ = [p.sb(f"stt{i}", [128, H], BF16) for i in range(2)]
    ytmb = [p.sb(f"ytmp{i}", [128, H], BF16) for i in range(4)]
    yacc = [p.sb(f"yacc{i}", [128, H], F32) for i in range(2)]
    carry = p.sb("carry", [128, 256], F32)
    stg = [p.sb(f"stg{i}", [128, H], BF16) for i in range(2)]
    ps_y = [p.ps(f"ps_y{i}", [128, H], F32) for i in range(2)]

    lda = p.sem("lda")
    ldr = p.sem("ldr")
    ldt = [p.sem(f"ldt{i}") for i in range(2)]
    prep = p.sem("prep")
    sda = p.sem("sda"); gmul = p.sem("gmul"); sadd = p.sem("sadd")
    scar = p.sem("scar")
    vsc = p.sem("vsc"); vsn = p.sem("vsn"); vtl = p.sem("vtl"); vfin = p.sem("vfin")
    pey = p.sem("pey")
    stm = [p.sem(f"stm{i}") for i in range(2)]

    p.dma("sync", ares[:], a_re, inc=lda)
    p.dma("sync", dres[:], d_re, inc=lda)
    p.dma("sync", idn[:], idn_in, inc=lda)
    lda_n = lda.n
    p.do("scalar", lambda e: e.activation(aneg[:], ares[:], AF.Exp), wait=[(lda, lda_n)], inc=sda)
    p.do("vector", lambda e: e.tensor_scalar(aneg[:], aneg[:], -1.0, None, ALU.mult),
         wait=[(sda, 1)], inc=prep)

    ldt_after = {}; ldr_after = {}
    vsc_after = {}; sda_after = {}; gmul_after = {}; sadd_after = {}
    vfin_after = {}; stm_after = {}; carry_w = {}; vsn_after = {}
    scar_after = {}; pey_after = {}; vtl_after = {}
    si = 0
    ti = 0
    pend_carry = None
    q_gmul = {}; q_vsc = {}

    def flush_carry():
        nonlocal pend_carry
        if pend_carry is None:
            return
        psi, pr, pcol = pend_carry
        p.do("scalar", (lambda pr, pcol: lambda e: e.activation(
            carry[:, pcol:pcol + 1], stt_[pr][:, H - 1:H], AF.Copy))(pr, pcol),
            wait=[(vsn, vsn_after[psi])], inc=scar)
        carry_w[pcol] = scar.n
        scar_after[psi] = scar.n
        pend_carry = None

    for q in range(NQ):
        for n in range(16):
            w = []
            if q > 0 and n == 0:
                w = [(gmul, q_gmul[q - 1]), (vsc, q_vsc[q - 1])]
            p.dma("gpsimd", brep[:, n * H:(n + 1) * H],
                  bm[n:n + 1, q * H:(q + 1) * H].partition_broadcast(128).squeeze(1),
                  wait=w, inc=ldr)
            p.dma("gpsimd", crep[:, n * H:(n + 1) * H],
                  cm[n:n + 1, q * H:(q + 1) * H].partition_broadcast(128).squeeze(1),
                  inc=ldr)
        ldr_after[q] = ldr.n
        for ct in range(16):
            s = ti % 2
            w = []
            if ti >= 2:
                w.append((vfin, vfin_after[ti - 2]))
            for gi, srcten in enumerate([dt, dtu, u, zs]):
                p.dma("sync", tin[s][:, gi * H:(gi + 1) * H],
                      srcten[ct * 128:(ct + 1) * 128, q * H:(q + 1) * H],
                      wait=(w if gi == 0 else []), inc=ldt[s])
            ldt_after[ti] = ldt[s].n
            for n in range(16):
                r = si % 2
                col = ct * 16 + n
                w = [(ldt[s], ldt_after[ti]), (prep, 1)]
                if si >= 2:
                    w.append((vsn, vsn_after[si - 2]))
                p.do("scalar", (lambda r, s, col: lambda e: e.activation(
                    dA[r][:], tin[s][:, 0:H], AF.Exp, scale=aneg[:, col:col + 1]))(r, s, col),
                    wait=w, inc=sda)
                sda_after[si] = sda.n
                w = [(ldt[s], ldt_after[ti]), (ldr, ldr_after[q])]
                if si >= 2:
                    w.append((sadd, sadd_after[si - 2]))
                p.do("gpsimd", (lambda r, s, n: lambda e: e.tensor_tensor(
                    dBe[r][:], tin[s][:, H:2 * H], brep[:, n * H:(n + 1) * H], ALU.mult))(r, s, n),
                    wait=w, inc=gmul)
                gmul_after[si] = gmul.n
                w = [(gmul, gmul_after[si])] + p.cwait()
                if si >= 2:
                    w.append((vsn, vsn_after[si - 2]))
                p.do("scalar", (lambda r: lambda e: e.activation(
                    dBe2[r][:], dBe[r][:], AF.Copy, bias=1e-6))(r),
                    wait=w, inc=sadd)
                sadd_after[si] = sadd.n
                flush_carry()
                init = 0.0 if q == 0 else carry[:, col:col + 1]
                w = [(sda, sda_after[si]), (sadd, sadd_after[si])]
                if q > 0:
                    w.append((scar, carry_w[col]))
                if si >= 2 and (si - 2) in scar_after:
                    w.append((scar, scar_after[si - 2]))
                p.do("vector", (lambda r, init: lambda e: e.tensor_tensor_scan(
                    stt_[r][:], dA[r][:], dBe2[r][:], init, ALU.mult, ALU.add))(r, init),
                    wait=w, inc=vsn)
                vsn_after[si] = vsn.n
                if q < NQ - 1:
                    pend_carry = (si, r, col)
                yslot = si % 4
                w = [(ldr, ldr_after[q])]
                if si >= 4:
                    w.append((pey, pey_after[si - 4]))
                p.do("vector", (lambda r, n, yslot: lambda e: e.tensor_tensor(
                    ytmb[yslot][:], stt_[r][:], crep[:, n * H:(n + 1) * H], ALU.mult))(r, n, yslot),
                    wait=w, inc=vsc)
                vsc_after[si] = vsc.n
                for h in range(2):
                    w = []
                    if h == 0:
                        w = [(vsc, vsc_after[si]), (lda, lda_n)]
                        if n == 0 and ti >= 2:
                            w.append((vtl, vtl_after[ti - 2]))
                    p.do("tensor", (lambda yslot, h, ti, n: lambda e: e.matmul(
                        ps_y[ti % 2][:, h * 512:(h + 1) * 512], idn[:],
                        ytmb[yslot][:, h * 512:(h + 1) * 512],
                        start=(n == 0), stop=(n == 15)))(yslot, h, ti, n),
                        wait=w, inc=(pey if h == 1 else None))
                pey_after[si] = pey.n
                si += 1
            flush_carry()
            s2 = ti % 2
            p.do("vector", (lambda s, ct, s2, ti: lambda e: e.scalar_tensor_tensor(
                yacc[s2][:], tin[s][:, 2 * H:3 * H], dres[:, ct:ct + 1],
                ps_y[ti % 2][:], ALU.mult, ALU.add))(s, ct, s2, ti),
                wait=[(pey, pey_after[si - 1]), (lda, lda_n)], inc=vtl)
            vtl_after[ti] = vtl.n
            w = []
            if ti >= 2:
                w.append((stm[ti % 2], stm_after[ti - 2]))
            p.do("vector", (lambda s, s2, ti: lambda e: e.tensor_tensor(
                stg[ti % 2][:], yacc[s2][:], tin[s][:, 3 * H:4 * H], ALU.mult))(s, s2, ti),
                wait=w, inc=vfin)
            vfin_after[ti] = vfin.n
            p.dma("gpsimd", y_d[ct * 128:(ct + 1) * 128, q * H:(q + 1) * H],
                  stg[ti % 2][:], wait=[(vfin, vfin_after[ti])], inc=stm[ti % 2])
            stm_after[ti] = stm[ti % 2].n
            ti += 1
        q_gmul[q] = gmul.n
        q_vsc[q] = vsc.n
    p.do("sync", lambda e: e.nop(), wait=[(s_, s_.n) for s_ in stm])
    return p


def emit_fp6_quant(p, src_ap, dst_ap, scr, waits, inc_sem, eng="vector"):
    W = src_ap.shape[-1]
    NB = W // 32
    qv, t1, t2, i32t, rf = scr["qv"], scr["t1"], scr["t2"], scr["i32"], scr["rf"]
    amax, scale, rscale = scr["amax"], scr["scale"], scr["rscale"]
    v = lambda fn, w=(), inc=None: p.do(eng, fn, wait=w, inc=inc)
    v(lambda e: e.tensor_reduce(amax[:, 0:NB], src_ap.rearrange("p (b k) -> p b k", k=32),
                                AX.X, ALU.max, apply_absolute_value=True), w=list(waits))
    v(lambda e: e.tensor_scalar(scale[:, 0:NB], amax[:, 0:NB], 1.0 / 7.5, 1e-30,
                                ALU.mult, ALU.max))
    v(lambda e: e.reciprocal(rscale[:, 0:NB], scale[:, 0:NB]))
    bc = lambda t: t[:, 0:NB].unsqueeze(2).broadcast_to((128, NB, 32))
    sv = src_ap.rearrange("p (b k) -> p b k", k=32)
    v(lambda e: e.tensor_tensor(qv[:].rearrange("p (b k) -> p b k", k=32), sv, bc(rscale), ALU.mult))
    v(lambda e: e.tensor_tensor(t1[:], qv[:], qv[:], ALU.mult))
    v(lambda e: e.tensor_single_scalar(t2[:], t1[:], 16.0, ALU.is_ge))
    v(lambda e: e.tensor_single_scalar(t1[:], t1[:], 4.0, ALU.is_ge))
    v(lambda e: e.tensor_scalar(t1[:], t1[:], -4.0, 8.0, ALU.mult, ALU.add))
    v(lambda e: e.scalar_tensor_tensor(t1[:], t2[:], -2.0, t1[:], ALU.mult, ALU.add))
    v(lambda e: e.tensor_tensor(qv[:], qv[:], t1[:], ALU.mult))
    v(lambda e: e.tensor_copy(i32t[:], qv[:]))
    v(lambda e: e.tensor_copy(rf[:], i32t[:]))
    v(lambda e: e.tensor_tensor(t2[:], rf[:], qv[:], ALU.subtract))
    v(lambda e: e.tensor_single_scalar(t2[:], t2[:], 0.5, ALU.is_equal))
    v(lambda e: e.tensor_tensor(rf[:], rf[:], t2[:], ALU.subtract))
    v(lambda e: e.reciprocal(t1[:], t1[:]))
    v(lambda e: e.tensor_tensor(rf[:], rf[:], t1[:], ALU.mult))
    v(lambda e: e.tensor_tensor(dst_ap.rearrange("p (b k) -> p b k", k=32),
                                rf[:].rearrange("p (b k) -> p b k", k=32),
                                bc(scale), ALU.mult), inc=inc_sem)


def build_l0d(p=None):
    if p is None:
        p = Prog("l0d")
    y = p.inp("y_d", [2048, T], BF16)
    owT = p.inp("out_wT", [2048, D])
    xn = p.inp("xn_tok", [T, D])
    fw = p.inp("ffn_w", [D, D])
    fbr = p.inp("ffn_b_re", [128, 8])
    x = p.inp("x", [T, D])
    idn_in = p.inp("idn", [128, 128], BF16)
    x1 = p.out("x1", [T, D])
    attn_d = p.nc.dram_tensor("attn_d", [D, T], BF16, kind="Internal").ap()
    ffn_d = p.nc.dram_tensor("ffn_d", [D, T], BF16, kind="Internal").ap()

    idn = p.sb("idn_s", [128, 128], BF16)
    owf = [p.sb(f"owf{i}", [128, D], F32) for i in range(2)]
    owb = p.sb("owb", [128, 16 * D], BF16)
    ycb = [p.sb(f"ycb{i}", [128, 16 * 512], BF16) for i in range(2)]
    fwf = [p.sb(f"fwf{i}", [128, D], F32) for i in range(2)]
    wq = [p.sb(f"wq{i}", [128, D], BF16) for i in range(2)]
    wqT = p.sb("wqT", [128, 64 * 128], BF16)
    xnf = [p.sb(f"xnf{i}", [128, D], F32) for i in range(2)]
    xd = [p.sb(f"xd{i}", [128, D], BF16) for i in range(2)]
    xd_d = p.sb("xd_d", [128, 8 * T], BF16)
    scr = {k: p.sb(f"scr_{k}", [128, D], F32 if k != "i32" else I32)
           for k in ["qv", "t1", "t2", "rf"]}
    scr["i32"] = p.sb("scr_i32", [128, D], I32)
    scr["amax"] = p.sb("scr_amax", [128, 32], F32)
    scr["scale"] = p.sb("scr_scale", [128, 32], F32)
    scr["rscale"] = p.sb("scr_rscale", [128, 32], F32)
    fbs = p.sb("fbs", [128, 8], F32)
    stg = [p.sb(f"stg{i}", [128, 512], BF16) for i in range(4)]

    ps_m = [p.ps(f"ps_m{i}", [128, 512], F32) for i in range(4)]
    ps_t = [p.ps(f"ps_t{i}", [128, 1024], BF16) for i in range(2)]
    ps_e = [p.ps(f"ps_e{i}", [128, 1024], BF16) for i in range(2)]

    li = p.sem("li")
    low = [p.sem(f"low{i}") for i in range(2)]
    ldy = [p.sem(f"ldy{i}") for i in range(2)]
    lfw = [p.sem(f"lfw{i}") for i in range(2)]
    lxn = [p.sem(f"lxn{i}") for i in range(2)]
    ldx = [p.sem(f"ldx{i}") for i in range(2)]
    lat = [p.sem(f"lat{i}") for i in range(2)]
    lft = [p.sem(f"lft{i}") for i in range(2)]
    cow = p.sem("cow")
    qw = p.sem("qw"); qx = p.sem("qx")
    pem = p.sem("pem"); evm = p.sem("evm")
    pet = p.sem("pet"); evt = p.sem("evt")
    pef = p.sem("pef"); evf = p.sem("evf")
    pee = p.sem("pee"); vre = p.sem("vre")
    stm = [p.sem(f"stm{i}") for i in range(4)]
    stf = [p.sem(f"stf{i}") for i in range(4)]
    sto = [p.sem(f"sto{i}") for i in range(2)]

    p.dma("sync", idn[:], idn_in, inc=li)
    p.dma("sync", fbs[:], fbr, inc=li)

    low_after = {}; cow_after = {}
    for k in range(16):
        w = [(cow, k - 1)] if k >= 2 else []
        p.dma("sync", owf[k % 2][:],
              owT[k * 128:(k + 1) * 128, :], wait=w, inc=low[k % 2])
        low_after[k] = low[k % 2].n
        p.do("gpsimd", (lambda k: lambda e: e.tensor_copy(
            owb[:, k * D:(k + 1) * D], owf[k % 2][:]))(k),
            wait=[(low[k % 2], low_after[k])], inc=cow)
        cow_after[k] = cow.n
    ldy_after = {}; pem_after = {}; evm_after = {}; stm_after = {}
    g = 0
    for f in range(8):
        s = f % 2
        w = []
        if f >= 2:
            w.append((pem, 8 * (f - 1)))
        for k in range(16):
            p.dma("sync", ycb[s][:, k * 512:(k + 1) * 512],
                  y[k * 128:(k + 1) * 128, f * 512:(f + 1) * 512],
                  wait=(w if k == 0 else []), inc=ldy[s])
        ldy_after[f] = ldy[s].n
        for ot in range(8):
            bank = ps_m[g % 4]
            for k in range(16):
                w = []
                if k == 0:
                    w = [(ldy[s], ldy_after[f]), (cow, 16)]
                    if g >= 4:
                        w.append((evm, evm_after[g - 4]))
                p.do("tensor", (lambda l, r, bank, s0, s1_: lambda e: e.matmul(
                    bank[:], l, r, start=s0, stop=s1_))(
                    owb[:, (k * 8 + ot) * 128:(k * 8 + ot + 1) * 128],
                    ycb[s][:, k * 512:(k + 1) * 512], bank, k == 0, k == 15),
                    wait=w, inc=(pem if k == 15 else None))
            pem_after[g] = pem.n
            w = [(pem, pem_after[g])]
            if g >= 4:
                w.append((stm[g % 4], stm_after[g - 4]))
            p.do("scalar", (lambda bank, g: lambda e: e.activation(
                stg[g % 4][:], bank[:], AF.Copy))(bank, g), wait=w, inc=evm)
            evm_after[g] = evm.n
            p.dma("gpsimd", attn_d[ot * 128:(ot + 1) * 128, f * 512:(f + 1) * 512],
                  stg[g % 4][:], wait=[(evm, evm_after[g])], inc=stm[g % 4])
            stm_after[g] = stm[g % 4].n
            g += 1

    lfw_after = {}; qw_after = {}; pet_after = {}; evt_after = {}
    for ot in range(8):
        s = ot % 2
        w = [(qw, ot - 1)] if ot >= 2 else []
        p.dma("sync", fwf[s][:], fw[ot * 128:(ot + 1) * 128, :], wait=w, inc=lfw[s])
        lfw_after[ot] = lfw[s].n
    for ot in range(8):
        s = ot % 2
        w = [(lfw[s], lfw_after[ot]), (evm, 64)]
        if ot >= 2:
            w.append((pet, pet_after[ot - 2]))
        emit_fp6_quant(p, fwf[s][:], wq[s][:], scr, w, qw)
        qw_after[ot] = qw.n
        for it in range(8):
            w = [(qw, qw_after[ot]), (li, 16)]
            if ot >= 2 and it == 0:
            	w.append((evt, evt_after[ot - 2]))
            p.do("tensor", (lambda s, ot, it: lambda e: e.transpose(
                ps_t[ot % 2][:, it * 128:(it + 1) * 128], wq[s][:, it * 128:(it + 1) * 128],
                idn[:]))(s, ot, it), wait=w, inc=(pet if it == 7 else None))
        pet_after[ot] = pet.n
        p.do("vector", (lambda ot: lambda e: e.tensor_copy(
            wqT[:].rearrange("p (it o c) -> p it o c", it=8, o=8)[:, :, ot, :],
            ps_t[ot % 2][:].rearrange("p (it c) -> p it c", c=128)))(ot),
            wait=[(pet, pet_after[ot])], inc=evt)
        evt_after[ot] = evt.n

    lxn_after = {}; qx_after = {}
    pet2_after = {}; evt2_after = {}
    for i in range(NTT):
        s = i % 2
        w = [(qx, i - 1)] if i >= 2 else []
        p.dma("sync", xnf[s][:], xn[i * 128:(i + 1) * 128, :], wait=w, inc=lxn[s])
        lxn_after[i] = lxn[s].n
    for i in range(NTT):
        s = i % 2
        w = [(lxn[s], lxn_after[i]), (evt, 8)]
        if i >= 2:
            w.append((pet, pet_after.get(("c", i - 2), 8)))
        emit_fp6_quant(p, xnf[s][:], xd[s][:], scr, w, qx)
        qx_after[i] = qx.n
        for j in range(8):
            w = [(qx, qx_after[i])]
            if j == 0:
                if i >= 2:
                    w.append((evt, evt_after[("c", i - 2)]))
                else:
                    w.append((evt, 8))
            p.do("tensor", (lambda s, i, j: lambda e: e.transpose(
                ps_t[i % 2][:, j * 128:(j + 1) * 128], xd[s][:, j * 128:(j + 1) * 128],
                idn[:]))(s, i, j), wait=w, inc=(pet if j == 7 else None))
        pet_after[("c", i)] = pet.n
        p.do("vector", (lambda i: lambda e: e.tensor_copy(
            xd_d[:].rearrange("p (j t) -> p j t", t=T)[:, :, i * 128:(i + 1) * 128],
            ps_t[i % 2][:].rearrange("p (j c) -> p j c", c=128)))(i),
            wait=[(pet, pet_after[("c", i)])], inc=evt)
        evt_after[("c", i)] = evt.n

    xdv = xd_d[:].rearrange("p (j t) -> p j t", t=T)
    pef_after = {}; evf_after = {}; stf_after = {}
    g2 = 0
    for ot in range(8):
        for f in range(8):
            bank = ps_m[g2 % 4]
            for k in range(8):
                w = []
                if k == 0:
                    w = [(evt, 8 + NTT)]
                    if g2 >= 4:
                        w.append((evf, evf_after[g2 - 4]))
                p.do("tensor", (lambda l, r, bank, s0, s1_: lambda e: e.matmul(
                    bank[:], l, r, start=s0, stop=s1_))(
                    wqT[:].rearrange("p (it o c) -> p it o c", it=8, o=8)[:, k, ot, :],
                    xdv[:, k, f * 512:(f + 1) * 512], bank, k == 0, k == 7),
                    wait=w, inc=(pef if k == 7 else None))
            pef_after[g2] = pef.n
            w = [(pef, pef_after[g2]), (li, 32)]
            if g2 >= 4:
                w.append((stf[g2 % 4], stf_after[g2 - 4]))
            else:
                w.append((stm[g2 % 4], stm[g2 % 4].n))
            p.do("vector", (lambda bank, g2, ot: lambda e: e.tensor_scalar(
                stg[g2 % 4][:], bank[:], fbs[:, ot:ot + 1], None, ALU.add))(bank, g2, ot),
                wait=w, inc=evf)
            evf_after[g2] = evf.n
            p.dma("gpsimd", ffn_d[ot * 128:(ot + 1) * 128, f * 512:(f + 1) * 512],
                  stg[g2 % 4][:], wait=[(evf, evf_after[g2])], inc=stf[g2 % 4])
            stf_after[g2] = stf[g2 % 4].n
            g2 += 1

    xti = [xnf[0], xnf[1]]
    xo = [scr["qv"], scr["t1"]]
    ati = [xd[0], xd[1]]
    fti = [wq[0], wq[1]]
    atv = attn_d.rearrange("(dt p) t -> p dt t", p=128)
    ftv = ffn_d.rearrange("(dt p) t -> p dt t", p=128)
    pee_after = {}; vre_after = {}; sto_after = {}
    for i in range(NTT):
        s = i % 2
        w = [(vre, 2 * (i - 1))] if i >= 2 else [(qx, NTT)]
        p.dma("sync", xti[s][:], x[i * 128:(i + 1) * 128, :], wait=w, inc=ldx[s])
        w2 = ([(s_, s_.n) for s_ in stm] + [(pet, 40)]) if i == 0 else []
        w3 = ([(s_, s_.n) for s_ in stf] + [(pet, 40)]) if i == 0 else []
        if i == 1:
            w2 = [(pet, 40)]
            w3 = [(pet, 40)]
        if i >= 2:
            w2 = [(vre, 2 * (i - 1))]
            w3 = []
        p.dma("sync", ati[s][:].rearrange("p (dt c) -> p dt c", c=128),
              atv[:, :, i * 128:(i + 1) * 128], wait=w2, inc=lat[s])
        p.dma("sync", fti[s][:].rearrange("p (dt c) -> p dt c", c=128),
              ftv[:, :, i * 128:(i + 1) * 128], wait=w3, inc=lft[s])
        for j in range(8):
            w = [(lat[s], lat[s].n)]
            if i >= 2 and j == 0:
                w.append((vre, vre_after[i - 2]))
            p.do("tensor", (lambda s, i, j: lambda e: e.transpose(
                ps_e[i % 2][:, j * 128:(j + 1) * 128], ati[s][:, j * 128:(j + 1) * 128],
                idn[:]))(s, i, j), wait=w, inc=None)
        for j in range(8):
            w = [(lft[s], lft[s].n)]
            if j == 0 and i < 2:
                w.append((evt, 8 + NTT))
            if j == 0 and i >= 2:
                w.append((vre, vre_after[i - 2]))
            p.do("tensor", (lambda s, i, j: lambda e: e.transpose(
                ps_t[i % 2][:, j * 128:(j + 1) * 128], fti[s][:, j * 128:(j + 1) * 128],
                idn[:]))(s, i, j), wait=w, inc=(pee if j == 7 else None))
        pee_after[i] = pee.n
        w = [(pee, pee_after[i]), (ldx[s], ldx[s].n)]
        if i >= 2:
            w.append((sto[s], sto_after[i - 2]))
        p.do("vector", (lambda s, i: lambda e: e.tensor_tensor(
            xo[s][:], xti[s][:], ps_e[i % 2][:].rearrange("p (j c) -> p j c", c=128),
            ALU.add))(s, i), wait=w, inc=vre)
        p.do("vector", (lambda s, i: lambda e: e.tensor_tensor(
            xo[s][:], xo[s][:], ps_t[i % 2][:].rearrange("p (j c) -> p j c", c=128),
            ALU.add))(s, i), inc=vre)
        vre_after[i] = vre.n
        p.dma("gpsimd", x1[i * 128:(i + 1) * 128, :], xo[s][:],
              wait=[(vre, vre_after[i])], inc=sto[s])
        sto_after[i] = sto[s].n
    p.do("sync", lambda e: e.nop(), wait=[(s_, s_.n) for s_ in sto])
    return p


def emit_ln(p, x, xn_d, idn, pfx, xn_store=None, sqrt_eps=1e-5, idn_wait=(),
            bufs=None, ps_t_bufs=2):
    if bufs is None:
        xbuf = [p.sb(f"{pfx}xbuf{i}", [128, D], F32) for i in range(2)]
    else:
        xbuf = bufs[0]
    s1 = p.sb(f"{pfx}s1", [128, NTT], F32)
    s2 = p.sb(f"{pfx}s2", [128, NTT], F32)
    mean = p.sb(f"{pfx}mean", [128, NTT], F32)
    var = p.sb(f"{pfx}var", [128, NTT], F32)
    sd = p.sb(f"{pfx}sd", [128, NTT], F32)
    rstd = p.sb(f"{pfx}rstd", [128, NTT], F32)
    if bufs is None:
        xnt32 = [p.sb(f"{pfx}xnt32{i}", [128, D], F32) for i in range(2)]
        xnt = [p.sb(f"{pfx}xnt{i}", [128, D], BF16) for i in range(2)]
    else:
        xnt32 = bufs[1]
        xnt = bufs[2]
    sqscr = xnt32
    ps_t = [p.ps(f"{pfx}ps_t{i}", [128, 1024], BF16) for i in range(ps_t_bufs)]
    if ps_t_bufs == 1:
        ps_t = [ps_t[0], ps_t[0]]
    ldx = [p.sem(f"{pfx}ldx{i}") for i in range(2)]
    vst = p.sem(f"{pfx}vst"); sst = p.sem(f"{pfx}sst"); vrs = p.sem(f"{pfx}vrs")
    srs = p.sem(f"{pfx}srs"); rrs = p.sem(f"{pfx}rrs")
    xnv = p.sem(f"{pfx}xnv"); pet = p.sem(f"{pfx}pet"); evq = p.sem(f"{pfx}evq")
    xns = [p.sem(f"{pfx}xns{i}") for i in range(2)]

    ldx_after = []
    for i in range(NTT):
        w = [(vst, i - 1), (sst, i - 1)] if i >= 2 else []
        p.dma("sync", _ap(xbuf[i % 2]), x[i * 128:(i + 1) * 128, :], wait=w, inc=ldx[i % 2])
        ldx_after.append(ldx[i % 2].n)
    for i in range(NTT):
        p.do("vector", (lambda i: lambda e: e.tensor_reduce(
            s1[:, i:i + 1], _ap(xbuf[i % 2]), AX.X, ALU.add))(i),
            wait=[(ldx[i % 2], ldx_after[i])], inc=vst)
        p.do("scalar", (lambda i: lambda e: e.activation(
            _ap(sqscr[i % 2]), _ap(xbuf[i % 2]), AF.Square, accum_out=s2[:, i:i + 1]))(i),
            wait=[(ldx[i % 2], ldx_after[i])], inc=sst)
    p.do("vector", lambda e: e.tensor_scalar(mean[:], s1[:], 1.0 / D, None, ALU.mult),
         wait=[(sst, NTT), (vst, NTT)])
    p.do("vector", lambda e: e.tensor_tensor(var[:], mean[:], mean[:], ALU.mult))
    p.do("vector", lambda e: e.scalar_tensor_tensor(
        var[:], s2[:], 1.0 / D, var[:], ALU.mult, ALU.subtract), inc=vrs)
    eps_ap = p.cbias(sqrt_eps)
    p.do("scalar", lambda e: e.activation(sd[:], var[:], AF.Sqrt, bias=eps_ap, scale=1.0),
         wait=[(vrs, vrs.n)] + p.cwait(), inc=srs)
    p.do("vector", lambda e: e.reciprocal(rstd[:], sd[:]), wait=[(srs, srs.n)], inc=rrs)
    rrs_c = rrs.n

    xnv_after = []; xns_after = []; pet_after = []; evq_after = []
    ldx2_after = []
    for i in range(NTT):
        w = [(rrs, rrs_c)] if i < 2 else [(xnv, xnv_after[i - 2] if False else 2 * (i - 1))]
        p.dma("sync", _ap(xbuf[i % 2]), x[i * 128:(i + 1) * 128, :], wait=w, inc=ldx[i % 2])
        ldx2_after.append(ldx[i % 2].n)
    for i in range(NTT):
        w = [(ldx[i % 2], ldx2_after[i]), (rrs, rrs_c)]
        if i >= 2:
            w += [(pet, pet_after[i - 2])]
            if xn_store is not None:
                w += [(xns[i % 2], xns_after[i - 2])]
        p.do("vector", (lambda i: lambda e: e.tensor_scalar(
            xnt32[i % 2][:], xbuf[i % 2][:], mean[:, i:i + 1], rstd[:, i:i + 1],
            ALU.subtract, ALU.mult))(i), wait=w, inc=xnv)
        p.do("vector", (lambda i: lambda e: e.tensor_copy(
            xnt[i % 2][:], xnt32[i % 2][:]))(i), inc=xnv)
        xnv_after.append(xnv.n)
        if xn_store is not None:
            p.dma("gpsimd", xn_store[i * 128:(i + 1) * 128, :], xnt32[i % 2][:],
                  wait=[(xnv, xnv_after[i])], inc=xns[i % 2])
            xns_after.append(xns[i % 2].n)
        else:
            xns_after.append(0)
        for j in range(NDT):
            w = []
            if j == 0:
                w = [(xnv, xnv_after[i])] + list(idn_wait)
                if i >= 2:
                    w.append((evq, evq_after[i - 2]))
                if ps_t_bufs == 1 and i >= 1:
                    w.append((evq, evq_after[i - 1]))
            p.do("tensor", (lambda i, j: lambda e: e.transpose(
                ps_t[i % 2][:, j * 128:(j + 1) * 128], _ap(xnt[i % 2])[:, j * 128:(j + 1) * 128],
                idn[:]))(i, j), wait=w, inc=(pet if j == NDT - 1 else None))
        pet_after.append(pet.n)
        p.do("vector", (lambda i: lambda e: e.tensor_copy(
            xn_d[:].rearrange("p (dt t) -> p dt t", t=T)[:, :, i * 128:(i + 1) * 128],
            ps_t[i % 2][:].rearrange("p (dt c) -> p dt c", c=128)))(i),
            wait=[(pet, pet_after[i])], inc=evq)
        evq_after.append(evq.n)
    p._ln_ps_t = ps_t
    return evq, evq.n, xns


def _ap(t):
    import concourse.ap as _capm
    return t if isinstance(t, _capm.AP) else t[:]


TQ = 2304

def build_l1a(p=None):
    if p is None:
        p = Prog("l1a")
    x = p.inp("x", [T, D])
    wqkv = p.inp("w_qkv", [D, 3 * D])
    bq = p.inp("b_qkv_re", [128, 24])
    bvrow = p.inp("b_v", [D])
    idn_in = p.inp("idn", [128, 128], BF16)
    xn_d_o = p.out("xn_d", [D, T], BF16)
    q_d = p.out("q_d", [D, TQ], BF16)
    k_d = p.out("k_d", [D, TQ], BF16)
    v_tok = p.out("v_tok", [TQ, D], BF16)

    idn = p.sb("idn_s", [128, 128], BF16)
    xn_d = p.sb("xn_d_s", [128, NDT * T], BF16)
    wbuf = [p.sb(f"wbuf{i}", [128, 8 * 128], F32) for i in range(2)]
    wbb = [p.sb(f"wbb{i}", [128, 8 * 128], BF16) for i in range(2)]
    brep = p.sb("brep", [128, D], F32)
    stg = [p.sb(f"stg{i}", [128, 512], BF16) for i in range(4)]
    ps_m = [p.ps(f"ps_m{i}", [128, 512], F32) for i in range(4)]
    li = p.sem("li")
    p.dma("sync", idn[:], idn_in, inc=li)
    p.dma("sync", brep[:], bvrow.unsqueeze(0).partition_broadcast(128).squeeze(1), inc=li)

    evq, evq_n, _ = emit_ln(p, x, xn_d, idn, "ln_", xn_store=None, idn_wait=[(li, li.n)])
    sxn = [p.sem(f"sxn{i}") for i in range(2)]
    xnd = xn_d[:].rearrange("p (dt t) -> p dt t", t=T)
    for k in range(NDT):
        p.dma("gpsimd", xn_d_o[k * 128:(k + 1) * 128, :], xnd[:, k, :],
              wait=[(evq, evq_n)], inc=sxn[k % 2])

    lw = [p.sem(f"lw{i}") for i in range(2)]
    gct = p.sem("gct"); pem = p.sem("pem"); evm = p.sem("evm")
    stm = [p.sem(f"stm{i}") for i in range(4)]
    ldb = p.sem("ldb")
    bqs = p.sb("bqs", [128, 24], F32)
    p.dma("sync", bqs[:], bq, inc=ldb)

    qk_chunks = [(ci * 512, min(512, TQ - ci * 512)) for ci in range(5)]
    lw_after = {}; pem_after = {}; evm_after = {}; stm_after = {}
    g = 0
    for ot in range(16):
        w = [(gct, ot - 1)] if ot >= 2 else []
        p.dma("sync", wbuf[ot % 2][:].rearrange("p (kt o) -> p kt o", o=128),
              wqkv[:, ot * 128:(ot + 1) * 128].rearrange("(kt p) o -> p kt o", p=128),
              wait=w, inc=lw[ot % 2])
        lw_after[ot] = lw[ot % 2].n
        p.do("gpsimd", (lambda ot: lambda e: e.tensor_copy(wbb[ot % 2][:], wbuf[ot % 2][:]))(ot),
             wait=[(lw[ot % 2], lw_after[ot])] + ([(pem, 5 * (ot - 1))] if ot >= 2 else []),
             inc=gct)
        for f, (c0, cw) in enumerate(qk_chunks):
            bank = ps_m[g % 4]
            for k in range(8):
                w = []
                if k == 0:
                    w = [(gct, ot + 1)]
                    if g == 0:
                        w.append((evq, evq_n))
                    if g >= 4:
                        w.append((evm, evm_after[g - 4]))
                p.do("tensor", (lambda l, r, bank, s0, s1_, cw: lambda e: e.matmul(
                    bank[:, 0:cw], l, r, start=s0, stop=s1_))(
                    wbb[ot % 2][:, k * 128:(k + 1) * 128],
                    xnd[:, k, c0:c0 + cw], bank, k == 0, k == 7, cw),
                    wait=w, inc=(pem if k == 7 else None))
            pem_after[g] = pem.n
            w = [(pem, pem_after[g]), (ldb, 16)]
            if g >= 4:
                w.append((stm[g % 4], stm_after[g - 4]))
            p.do("vector", (lambda bank, g, ot, cw: lambda e: e.tensor_scalar(
                stg[g % 4][:, 0:cw], bank[:, 0:cw], bqs[:, ot:ot + 1], None, ALU.add))(bank, g, ot, cw),
                wait=w, inc=evm)
            evm_after[g] = evm.n
            dst = q_d if ot < 8 else k_d
            ot2 = ot % 8
            p.dma("gpsimd", dst[ot2 * 128:(ot2 + 1) * 128, c0:c0 + cw], stg[g % 4][:, 0:cw],
                  wait=[(evm, evm_after[g])], inc=stm[g % 4])
            stm_after[g] = stm[g % 4].n
            g += 1
    wvf = [p.sb(f"wvf{i}", [128, D], F32) for i in range(2)]
    wvb = p.sb("wvb", [128, 8 * D], BF16)
    lwv = [p.sem(f"lwv{i}") for i in range(2)]
    cwv = p.sem("cwv")
    for k in range(8):
        w = [(cwv, k - 1)] if k >= 2 else []
        p.dma("sync", wvf[k % 2][:], wqkv[k * 128:(k + 1) * 128, 2048:3072],
              wait=w, inc=lwv[k % 2])
        p.do("gpsimd", (lambda k: lambda e: e.tensor_copy(
            wvb[:, k * D:(k + 1) * D], wvf[k % 2][:]))(k),
            wait=[(lwv[k % 2], lwv[k % 2].n)], inc=cwv)
    NVT = TQ // 128
    for tt in range(NVT):
        for fc in range(2):
            bank = ps_m[g % 4]
            for k in range(8):
                w = []
                if k == 0:
                    w = [(evq, evq_n), (cwv, 8)]
                    if g >= 4:
                        w.append((evm, evm_after[g - 4]))
                p.do("tensor", (lambda l, r, bank, s0, s1_: lambda e: e.matmul(
                    bank[:], l, r, start=s0, stop=s1_))(
                    xnd[:, k, tt * 128:(tt + 1) * 128],
                    wvb[:, k * 1024 + fc * 512: k * 1024 + fc * 512 + 512],
                    bank, k == 0, k == 7),
                    wait=w, inc=(pem if k == 7 else None))
            pem_after[g] = pem.n
            w = [(pem, pem_after[g]), (li, 32)]
            if g >= 4:
                w.append((stm[g % 4], stm_after[g - 4]))
            p.do("vector", (lambda bank, g, fc: lambda e: e.tensor_tensor(
                stg[g % 4][:], bank[:], brep[:, fc * 512:(fc + 1) * 512], ALU.add))(bank, g, fc),
                wait=w, inc=evm)
            evm_after[g] = evm.n
            p.dma("gpsimd", v_tok[tt * 128:(tt + 1) * 128, fc * 512:(fc + 1) * 512],
                  stg[g % 4][:], wait=[(evm, evm_after[g])], inc=stm[g % 4])
            stm_after[g] = stm[g % 4].n
            g += 1
    p.do("sync", lambda e: e.nop(),
         wait=[(s_, s_.n) for s_ in stm + sxn])
    return p


def build_l1b(p=None):
    if p is None:
        p = Prog("l1b")
    q_d = p.inp("q_d", [D, TQ], BF16)
    k_d = p.inp("k_d", [D, TQ], BF16)
    v_tok = p.inp("v_tok", [TQ, D], BF16)
    ao = p.out("attnout", [T, D], BF16)

    qb = p.sb("qb", [128, 8 * TQ], BF16)
    kb = p.sb("kb", [128, 8 * TQ], BF16)
    vext = p.sb("vext", [128, 18 * 16 * 65], BF16)
    pT = p.sb("pT", [128, 4 * 512], BF16)
    o65 = p.sb("o65", [65, 512], F32)
    stg = [p.sb(f"stg{i}", [128, 4 * 64], BF16) for i in range(2)]
    ps_s = [p.ps(f"ps_s{i}", [128, 512], F32) for i in range(2)]
    ps_v = [p.ps(f"ps_v{i}", [65, 512], F32) for i in range(2)]
    ps_o = [p.ps(f"ps_o{i}", [128, 4 * 65], F32) for i in range(2)]
    idn_in = p.inp("idn", [128, 128], BF16)
    idnf_in = p.inp("idnf", [128, 128])
    idn = p.sb("idn_s", [128, 128], BF16)
    idnf = p.sb("idnf_s", [128, 128], F32)

    li = p.sem("li"); ldq = p.sem("ldq"); msv = p.sem("msv"); ldv = p.sem("ldv")
    pes = p.sem("pes"); sex = p.sem("sex"); pev = p.sem("pev")
    evv = p.sem("evv"); peo = p.sem("peo"); vdv = p.sem("vdv")
    stm = [p.sem(f"stm{i}") for i in range(2)]

    p.dma("sync", idn[:], idn_in, inc=li)
    p.dma("sync", idnf[:], idnf_in, inc=li)
    qv = qb[:].rearrange("p (dt t) -> p dt t", t=TQ)
    kv = kb[:].rearrange("p (dt t) -> p dt t", t=TQ)
    for dt_ in range(8):
        p.dma("sync", qv[:, dt_, :], q_d[dt_ * 128:(dt_ + 1) * 128, :], inc=ldq)
        p.dma("sync", kv[:, dt_, :], k_d[dt_ * 128:(dt_ + 1) * 128, :], inc=ldq)
    p.do("vector", lambda e: e.memset(vext[:], 1.0), inc=msv)
    vxv = vext[:].rearrange("p (tt h c) -> p tt h c", tt=18, c=65)
    for tt in range(18):
        p.dma("sync", vxv[:, tt, :, 0:64],
              v_tok[tt * 128:(tt + 1) * 128, :].rearrange("p (h c) -> p h c", c=64),
              wait=([(msv, 1)] if tt == 0 else []), inc=ldv)
    ldq_n, ldv_n = ldq.n, ldv.n

    p.cbias(-3.0)
    rec = [p.sb(f"rec{i}", [128, 1], F32) for i in range(2)]
    gi = 0
    pes_after = {}; sex_after = {}; pev_after = {}; evv_after = {}
    peo_after = {}; vdv_after = {}; stm_after = {}
    for n in range(8):
        t0 = 256 * n
        for h in range(16):
            dt_ = h // 2
            dro = (h % 2) * 64
            s = gi % 2
            for ks in range(4):
                w = [(ldq, ldq_n), (li, 32)]
                if gi * 4 + ks >= 2:
                    w.append((sex, sex_after[gi * 4 + ks - 2]))
                p.do("tensor", (lambda dt_, dro, t0, ks, ps: lambda e: e.matmul(
                    ps[:],
                    kv[:, dt_, t0 + ks * 128: t0 + (ks + 1) * 128][dro:dro + 64, :],
                    qv[:, dt_, t0:t0 + 512][dro:dro + 64, :],
                    start=True, stop=True))(dt_, dro, t0, ks, ps_s[(gi * 4 + ks) % 2]),
                    wait=w, inc=pes)
                pes_after[gi * 4 + ks] = pes.n
                wexp = [(pes, pes_after[gi * 4 + ks])] + p.cwait()
                if ks == 0 and gi >= 1:
                    wexp.append((pev, pev_after[gi - 1]))
                p.do("scalar", (lambda ks, ps: lambda e: e.activation(
                    pT[:, ks * 512:(ks + 1) * 512], ps[:], AF.Exp,
                    bias=-3.0, scale=0.125))(ks, ps_s[(gi * 4 + ks) % 2]),
                    wait=wexp, inc=sex)
                sex_after[gi * 4 + ks] = sex.n
            for ks in range(4):
                w = []
                if ks == 0:
                    w = [(sex, sex_after[gi * 4 + 3]), (ldv, ldv_n)]
                    if gi >= 2:
                        w.append((evv, evv_after[gi - 2]))
                p.do("tensor", (lambda n, ks, h, s: lambda e: e.matmul(
                    ps_v[s][:], vxv[:, 2 * n + ks, h, :], pT[:, ks * 512:(ks + 1) * 512],
                    start=(ks == 0), stop=(ks == 3)))(n, ks, h, s),
                    wait=w, inc=(pev if ks == 3 else None))
            pev_after[gi] = pev.n
            w = [(pev, pev_after[gi])]
            if gi >= 1:
                w.append((peo, peo_after[gi - 1]))
            p.do("vector", (lambda s: lambda e: e.tensor_copy(o65[:], ps_v[s][:]))(s),
                 wait=w, inc=evv)
            evv_after[gi] = evv.n
            for qs in range(4):
                w = [(evv, evv_after[gi])]
                if qs == 0:
                    w.append((li, 32))
                    if gi >= 2:
                        w.append((vdv, vdv_after[gi - 2]))
                p.do("tensor", (lambda qs, s: lambda e: e.transpose(
                    ps_o[s][:, qs * 65:(qs + 1) * 65], o65[:, qs * 128:(qs + 1) * 128],
                    idnf[0:65, 0:65]))(qs, s),
                    wait=w, inc=(peo if qs == 3 else None))
            peo_after[gi] = peo.n
            for qs in range(4):
                w = [(peo, peo_after[gi])]
                if gi >= 2 and qs == 0:
                    w.append((stm[s], stm_after[gi - 2]))
                p.do("vector", (lambda qs, s: lambda e: e.reciprocal(
                    rec[s][:], ps_o[s][:, qs * 65 + 64: qs * 65 + 65]))(qs, s), wait=w)
                p.do("vector", (lambda qs, s: lambda e: e.tensor_scalar(
                    stg[s][:, qs * 64:(qs + 1) * 64], ps_o[s][:, qs * 65: qs * 65 + 64],
                    rec[s][:], None, ALU.mult))(qs, s), inc=vdv)
            vdv_after[gi] = vdv.n
            for qs in range(4):
                p.dma("gpsimd", ao[512 * n + qs * 128: 512 * n + (qs + 1) * 128,
                                   h * 64:(h + 1) * 64],
                      stg[s][:, qs * 64:(qs + 1) * 64],
                      wait=[(vdv, vdv_after[gi])], inc=stm[s])
            stm_after[gi] = stm[s].n
            gi += 1
    p.do("sync", lambda e: e.nop(), wait=[(s_, s_.n) for s_ in stm])
    return p


def build_l1c(p=None):
    if p is None:
        p = Prog("l1c")
    aoin = p.inp("attnout", [T, D], BF16)
    wo_in = p.inp("w_out", [D, D])
    bo_in = p.inp("b_out", [D])
    x = p.inp("x", [T, D])
    idn_in = p.inp("idn", [128, 128], BF16)
    r1d = p.out("r1", [T, D])

    idn = p.sb("idn_s", [128, 128], BF16)
    wof = [p.sb(f"wof{i}", [128, D], F32) for i in range(2)]
    wob = p.sb("wob", [128, 8 * D], BF16)
    aot = [p.sb(f"aot{i}", [128, D], BF16) for i in range(2)]
    aodt = [p.sb(f"aodt{i}", [128, 8 * 128], BF16) for i in range(2)]
    xt = [p.sb(f"xt{i}", [128, D], F32) for i in range(2)]
    ot = [p.sb(f"ot{i}", [128, D], F32) for i in range(2)]
    brep = p.sb("brep_c", [128, D], F32)
    ps_t = [p.ps(f"ps_t{i}", [128, 1024], BF16) for i in range(2)]
    ps_o = [p.ps(f"ps_o{i}", [128, 1024], F32) for i in range(2)]
    li = p.sem("li")
    p.dma("sync", idn[:], idn_in, inc=li)
    p.dma("sync", brep[:], bo_in.unsqueeze(0).partition_broadcast(128).squeeze(1), inc=li)
    li_n = li.n

    lwo = [p.sem(f"lwo{i}") for i in range(2)]
    cwo = p.sem("cwo")
    for k in range(8):
        w = [(cwo, k - 1)] if k >= 2 else []
        p.dma("sync", wof[k % 2][:], wo_in[k * 128:(k + 1) * 128, :], wait=w, inc=lwo[k % 2])
        p.do("gpsimd", (lambda k: lambda e: e.tensor_copy(
            wob[:, k * D:(k + 1) * D], wof[k % 2][:]))(k),
            wait=[(lwo[k % 2], lwo[k % 2].n)], inc=cwo)
    lao = [p.sem(f"lao{i}") for i in range(2)]
    ldx = [p.sem(f"ldx{i}") for i in range(2)]
    peT = p.sem("peT"); evT = p.sem("evT"); pem = p.sem("pem"); evo = p.sem("evo")
    str_ = [p.sem(f"str{i}") for i in range(2)]
    peT_after = {}; evT_after = {}; pem_after = {}; evo_after = {}; str_after = {}
    wobv = wob[:].rearrange("p (k o) -> p k o", o=D)
    for tt in range(NTT):
        s = tt % 2
        w = [(evT, evT_after[tt - 2])] if tt >= 2 else []
        p.dma("sync", aot[s][:], aoin[tt * 128:(tt + 1) * 128, :], wait=w, inc=lao[s])
        p.dma("sync", xt[s][:], x[tt * 128:(tt + 1) * 128, :],
              wait=([(evo, evo_after[tt - 2])] if tt >= 2 else []), inc=ldx[s])
        for j in range(8):
            w = [(lao[s], lao[s].n), (li, li_n)]
            if tt >= 2 and j == 0:
                w.append((evT, evT_after[tt - 2]))
            p.do("tensor", (lambda s, j: lambda e: e.transpose(
                ps_t[s][:, j * 128:(j + 1) * 128], aot[s][:, j * 128:(j + 1) * 128],
                idn[:]))(s, j), wait=w, inc=(peT if j == 7 else None))
        peT_after[tt] = peT.n
        p.do("vector", (lambda s: lambda e: e.tensor_copy(
            aodt[s][:], ps_t[s][:]))(s), wait=[(peT, peT_after[tt])], inc=evT)
        evT_after[tt] = evT.n
        for oc in range(2):
            bank = ps_o[oc]
            for k in range(8):
                w = []
                if k == 0:
                    w = [(evT, evT_after[tt]), (cwo, 8)]
                    if tt >= 1:
                        w.append((evo, evo_after.get(tt - 1, 0)))
                p.do("tensor", (lambda s, k, oc, bank: lambda e: e.matmul(
                    bank[:, 0:512], aodt[s][:, k * 128:(k + 1) * 128],
                    wobv[:, k, oc * 512:(oc + 1) * 512], start=(k == 0), stop=(k == 7)))(
                    s, k, oc, bank), wait=w, inc=(pem if k == 7 else None))
        pem_after[tt] = pem.n
        w = [(pem, pem_after[tt]), (ldx[s], ldx[s].n)]
        if tt >= 2:
            w.append((str_[s], str_after[tt - 2]))
        p.do("vector", (lambda s: lambda e: e.tensor_tensor(
            ot[s][:, 0:512], ps_o[0][:, 0:512], brep[:, 0:512], ALU.add))(s), wait=w)
        p.do("vector", (lambda s: lambda e: e.tensor_tensor(
            ot[s][:, 512:1024], ps_o[1][:, 0:512], brep[:, 512:1024], ALU.add))(s))
        p.do("vector", (lambda s: lambda e: e.tensor_tensor(
            ot[s][:], ot[s][:], xt[s][:], ALU.add))(s), inc=evo)
        evo_after[tt] = evo.n
        p.dma("gpsimd", r1d[tt * 128:(tt + 1) * 128, :], ot[s][:],
              wait=[(evo, evo_after[tt])], inc=str_[s])
        str_after[tt] = str_[s].n

    p.do("sync", lambda e: e.nop(), wait=[(s_, s_.n) for s_ in str_])
    return p


def build_l1d(p=None):
    if p is None:
        p = Prog("l1d")
    xnd_in = p.inp("xn_d", [D, T], BF16)
    w1T = p.inp("W1T", [D, 4096])
    b1r = p.inp("b1_re", [128, 32])
    w2T = p.inp("W2T", [4096, D])
    b2_in = p.inp("b2", [D])
    r1d = p.inp("r1", [T, D])
    x2 = p.out("x2", [T, D])
    csd = p.nc.dram_tensor("csd", [128 * 32], F32, kind="Internal").ap()

    w1b = p.sb("w1b", [128, 8 * 4096], BF16)
    w2b = p.sb("w2b", [128, 32 * 1024], BF16)
    hb = p.sb("hb", [128, 32 * 512], BF16)
    xnr = [p.sb(f"xnr{i}", [128, 8 * 512], BF16) for i in range(2)]
    ot = [p.sb(f"ot{i}", [128, D], F32) for i in range(2)]
    rt = [p.sb(f"rt{i}", [128, D], F32) for i in range(2)]
    brep = p.sb("brep_c", [128, D], BF16)
    b1s = p.sb("b1s", [128, 32], F32)
    cols = p.sb("cols", [128, 32], F32)
    sct = [p.sb(f"sct{i}", [128, 1], F32) for i in range(2)]
    tot11 = p.sb("tot11", [1, 2], F32)
    i32s = p.sb("i32s", [128, 1024], I32)
    ring = [p.sb(f"tring{i}", [128, 1024], F32) for i in range(2)]
    ps_m = [p.ps(f"ps_m{i}", [128, 512], F32) for i in range(2)]
    ps_o = [p.ps(f"ps_o{i}", [128, 512], F32) for i in range(2)]
    li = p.sem("li")
    p.dma("sync", b1s[:], b1r, inc=li)
    li_n = li.n

    def ternarize(wT, nkt, wide, dst, pfx, sct_t, barrier):
        SW = min(wide, 1024)
        nsub = wide // SW
        nch = nkt * nsub
        lds = [p.sem(f"{pfx}ld{i}") for i in range(2)]
        vred = p.sem(f"{pfx}vred"); tern = p.sem(f"{pfx}tern")
        sttr = p.sem(f"{pfx}sttr"); lsc = p.sem(f"{pfx}lsc")
        def chunk_ap(c):
            k, sub = divmod(c, nsub)
            return wT[k * 128:(k + 1) * 128, sub * SW:(sub + 1) * SW]
        ld_after = {}
        for c in range(nch):
            w = list(barrier) if c < 2 else [(vred, c - 1)]
            p.dma("sync", ring[c % 2][:, 0:SW], chunk_ap(c), wait=w, inc=lds[c % 2])
            ld_after[c] = lds[c % 2].n
        for c in range(nch):
            p.do("vector", (lambda c: lambda e: e.tensor_reduce(
                cols[:, c:c + 1], ring[c % 2][:, 0:SW], AX.X, ALU.add,
                apply_absolute_value=True))(c), wait=[(lds[c % 2], ld_after[c])], inc=vred)
        p.dma("gpsimd", csd[0:128 * nch].rearrange("(p k) -> p k", p=128),
              cols[:, 0:nch], wait=[(vred, vred.n)], inc=sttr)
        tot = 128 * nch
        nld = (tot + 1023) // 1024
        for ci in range(nld):
            c0 = ci * 1024
            cw = min(1024, tot - c0)
            w = [(sttr, sttr.n)] if ci == 0 else [(vred, vred.n)]
            p.dma("sync", ring[0][0:1, 0:cw], csd[c0:c0 + cw].unsqueeze(0),
                  wait=w, inc=lsc)
            p.do("vector", (lambda ci, cw: lambda e: e.tensor_reduce(
                tot11[0:1, 1:2], ring[0][0:1, 0:cw], AX.X, ALU.add))(ci, cw),
                wait=[(lsc, lsc.n)])
            if ci == 0:
                p.do("vector", lambda e: e.tensor_copy(tot11[0:1, 0:1], tot11[0:1, 1:2]),
                     inc=vred)
            else:
                p.do("vector", lambda e: e.tensor_tensor(
                    tot11[0:1, 0:1], tot11[0:1, 0:1], tot11[0:1, 1:2], ALU.add), inc=vred)
        p.dma("gpsimd", csd[0:1].unsqueeze(0), tot11[0:1, 0:1],
              wait=[(vred, vred.n)], inc=sttr)
        p.dma("sync", sct_t[:], csd[0:1].unsqueeze(0).partition_broadcast(128).squeeze(1),
              wait=[(sttr, sttr.n)], inc=lsc)
        p.do("vector", (lambda nkt, wide: lambda e: e.tensor_scalar(
            sct_t[:], sct_t[:], 1.0 / (nkt * 128 * wide), 1e-8, ALU.mult, ALU.max))(nkt, wide),
            wait=[(lsc, lsc.n)])
        p.do("vector", (lambda: lambda e: e.reciprocal(sct_t[:], sct_t[:]))(), inc=tern)
        t0 = tern.n
        ld2 = {}
        for c in range(nch):
            w = [(tern, t0)] if c < 2 else [(tern, t0 + c - 1)]
            p.dma("sync", ring[c % 2][:, 0:SW], chunk_ap(c), wait=w, inc=lds[c % 2])
            ld2[c] = lds[c % 2].n
        for c in range(nch):
            k, sub = divmod(c, nsub)
            dof = k * wide + sub * SW
            p.do("vector", (lambda c: lambda e: e.tensor_scalar(
                ring[c % 2][:, 0:SW], ring[c % 2][:, 0:SW], sct_t[:], None,
                ALU.mult))(c), wait=[(lds[c % 2], ld2[c])])
            p.do("vector", (lambda c: lambda e: e.tensor_copy(
                i32s[:, 0:SW], ring[c % 2][:, 0:SW]))(c))
            p.do("vector", (lambda c: lambda e: e.tensor_copy(
                ring[c % 2][:, 0:SW], i32s[:, 0:SW]))(c))
            p.do("vector", (lambda c, dof: lambda e: e.tensor_scalar(
                dst[:, dof:dof + SW], ring[c % 2][:, 0:SW], 1.0, -1.0,
                ALU.min, ALU.max))(c, dof), inc=tern)
        return tern, tern.n

    barrier = []
    t1s, t1n = ternarize(w1T, 8, 4096, w1b[:], "t1_", sct[0], barrier)
    t2s, t2n = ternarize(w2T, 32, 1024, w2b[:], "t2_", sct[1], [(t1s, t1n)])

    lxn = [p.sem(f"lxn{i}") for i in range(2)]
    lr1 = [p.sem(f"lr{i}") for i in range(2)]
    peh = p.sem("peh"); evh = p.sem("evh"); peo2 = p.sem("peo2"); evf = p.sem("evf")
    stx = [p.sem(f"stx{i}") for i in range(2)]
    b2rep = brep
    lb2 = p.sem("lb2")
    p.dma("gpsimd", b2rep[:], b2_in.unsqueeze(0).partition_broadcast(128).squeeze(1), inc=lb2)
    w1v = w1b[:].rearrange("p (k f) -> p k f", f=4096)
    w2v = w2b[:].rearrange("p (ft o) -> p ft o", o=1024)
    peh_after = {}; evh_after = {}; peo2_after = {}; evf_after = {}; stx_after = {}
    lxn_after = {}; lr1_after = {}
    for tc in range(8):
        s = tc % 2
        w = [(peh, peh_after[(tc - 2, 31)])] if tc >= 2 else []
        for k in range(8):
            p.dma("sync", xnr[s][:, k * 512:(k + 1) * 512],
                  xnd_in[k * 128:(k + 1) * 128, tc * 512:(tc + 1) * 512],
                  wait=(w if k == 0 else []), inc=lxn[s])
        lxn_after[tc] = lxn[s].n
        for ft in range(32):
            bank = ps_m[ft % 2]
            for k in range(8):
                w = []
                if k == 0:
                    w = [(lxn[s], lxn_after[tc]), (t2s, t2n)]
                    if tc * 32 + ft >= 2:
                        w.append((evh, evh_after[tc * 32 + ft - 2]))
                p.do("tensor", (lambda s, k, ft, bank: lambda e: e.matmul(
                    bank[:], w1v[:, k, ft * 128:(ft + 1) * 128],
                    xnr[s][:, k * 512:(k + 1) * 512], start=(k == 0), stop=(k == 7)))(
                    s, k, ft, bank), wait=w, inc=(peh if k == 7 else None))
            peh_after[(tc, ft)] = peh.n
            p.do("scalar", (lambda ft, bank: lambda e: e.activation(
                hb[:, ft * 512:(ft + 1) * 512], bank[:], AF.Gelu,
                bias=b1s[:, ft:ft + 1], scale=1.0))(ft, bank),
                wait=[(peh, peh_after[(tc, ft)]), (li, li_n)], inc=evh)
            evh_after[tc * 32 + ft] = evh.n
        p.dma("sync", rt[0][:], r1d[tc * 512:tc * 512 + 128, :],
              wait=([(evf, evf_after[(tc - 1, 2)])] if tc >= 1 else []), inc=lr1[0])
        for ts in range(4):
            if ts > 0:
                if ts >= 2:
                    wrt = [(evf, evf_after[(tc, ts - 2)])]
                elif tc >= 1:
                    wrt = [(evf, evf_after[(tc - 1, 3)])]
                else:
                    wrt = []
                p.dma("sync", rt[ts % 2][:],
                      r1d[tc * 512 + ts * 128: tc * 512 + (ts + 1) * 128, :],
                      wait=wrt, inc=lr1[ts % 2])
            lr1_after[(tc, ts)] = lr1[ts % 2].n
            for half in range(2):
                bank = ps_o[half]
                for ft in range(32):
                    w = []
                    if ft == 0:
                        w = [(evh, evh_after[tc * 32 + 31])]
                        if (tc, ts) != (0, 0):
                            prev = (tc, ts - 1) if ts > 0 else (tc - 1, 3)
                            w.append((evf, evf_after[prev]))
                    p.do("tensor", (lambda ts, ft, half, bank: lambda e: e.matmul(
                        bank[:], hb[:, ft * 512 + ts * 128: ft * 512 + (ts + 1) * 128],
                        w2v[:, ft, half * 512:(half + 1) * 512],
                        start=(ft == 0), stop=(ft == 31)))(ts, ft, half, bank),
                        wait=w, inc=(peo2 if ft == 31 else None))
            peo2_after[(tc, ts)] = peo2.n
            w = [(peo2, peo2_after[(tc, ts)]), (lr1[ts % 2], lr1_after[(tc, ts)]), (lb2, 16)]
            if (tc * 4 + ts) >= 2:
                pv = (tc, ts - 2) if ts >= 2 else (tc - 1, ts + 2)
                w.append((stx[ts % 2], stx_after[pv]))
            p.do("vector", (lambda ts: lambda e: e.tensor_tensor(
                ot[ts % 2][:, 0:512], ps_o[0][:, 0:512], b2rep[:, 0:512], ALU.add))(ts),
                wait=w)
            p.do("vector", (lambda ts: lambda e: e.tensor_tensor(
                ot[ts % 2][:, 512:1024], ps_o[1][:, 0:512], b2rep[:, 512:1024], ALU.add))(ts))
            p.do("vector", (lambda ts: lambda e: e.tensor_tensor(
                ot[ts % 2][:], ot[ts % 2][:], rt[ts % 2][:], ALU.add))(ts), inc=evf)
            evf_after[(tc, ts)] = evf.n
            p.dma("gpsimd", x2[tc * 512 + ts * 128: tc * 512 + (ts + 1) * 128, :],
                  ot[ts % 2][:], wait=[(evf, evf_after[(tc, ts)])], inc=stx[ts % 2])
            stx_after[(tc, ts)] = stx[ts % 2].n
    p.do("sync", lambda e: e.nop(), wait=[(s_, s_.n) for s_ in stx])
    return p


def build_l2b(p=None):
    if p is None:
        p = Prog("l2b")
    x = p.inp("x", [T, D])
    wk_in = p.inp("Wk", [D, D])
    wq_in = p.inp("Wq", [D, D])
    wo_in = p.inp("Wo", [D, D])
    bo_in = p.inp("bo", [D])
    idn_in = p.inp("idn", [128, 128], BF16)
    idnf_in = p.inp("idnf", [128, 128])
    xn_tok_o = p.out("xn_tok", [T, D], BF16)
    r2 = p.out("r2", [T, D])

    idn = p.sb("idn_s", [128, 128], BF16)
    idnf = p.sb("idnf_s", [128, 128], F32)
    xn_d = p.sb("xn_d_s", [128, NDT * T], BF16)
    ao = p.sb("ao", [128, NTT * D], BF16)
    kp = [p.sb("kp0", [128, T], BF16)]
    qp = [p.sb("qp0", [128, T], BF16)]
    vx = [p.sb("vx0", [128, NTT * 130], BF16)]
    wcb = [p.sb(f"wcb{i}", [128, 8 * 128], BF16) for i in range(2)]
    pT = [p.sb(f"pT{i}", [128, 1024], BF16) for i in range(2)]
    o65 = p.sb("o65", [65, 512], F32)
    rec = [p.sb(f"rec{i}", [128, 1], F32) for i in range(2)]
    stg = [p.sb(f"stg{i}", [128, 4 * 64], BF16) for i in range(2)]
    borep = p.sb("borep", [128, D], BF16)
    aodt = [p.sb(f"aodt{i}", [128, 8 * 128], BF16) for i in range(2)]
    xstg = [aodt[0], aodt[1]]
    xt = [p.sb(f"xt{i}", [128, D], F32) for i in range(2)]
    otl = [p.sb(f"otl{i}", [128, D], F32) for i in range(2)]
    ps_d = [p.ps(f"ps_d{i}", [128, 1024], F32) for i in range(2)]
    ps_v = [p.ps(f"ps_v{i}", [65, 512], F32) for i in range(2)]
    ps_oo = [p.ps("ps_oo0", [128, 4 * 65], F32), None]
    ps_oo[1] = ps_oo[0]
    ps_s = [ps_d[0][:, 0:512], ps_d[0][:, 512:1024]]
    li = p.sem("li")
    lig = p.sem("lig")
    p.dma("sync", idn[:], idn_in, inc=li)
    p.dma("sync", idnf[:], idnf_in, inc=li)
    p.dma("gpsimd", borep[:], bo_in.unsqueeze(0).partition_broadcast(128).squeeze(1), inc=lig)
    li_n = li.n
    p.cbias(-3.0)

    evq, evq_n, _ = emit_ln(p, x, xn_d, idn, "ln_", idn_wait=[(li, li.n)], ps_t_bufs=1)
    ps_t = p._ln_ps_t
    ps_m = None
    xnd = xn_d[:].rearrange("p (dt t) -> p dt t", t=T)
    sxn = [p.sem(f"sxn{i}") for i in range(2)]
    pex = p.sem("pex"); evx = p.sem("evx")
    pex_after = {}; evx_after = {}; sxn_after = {}
    for i in range(NTT):
        s = i % 2
        for j in range(NDT):
            w = []
            if j == 0:
                w = [(evq, evq_n), (li, li_n)]
                if i >= 1:
                    w.append((evx, evx_after[i - 1]))
            p.do("tensor", (lambda i, j: lambda e: e.transpose(
                ps_t[i % 2][:, j * 128:(j + 1) * 128],
                xnd[:, j, i * 128:(i + 1) * 128], idn[:]))(i, j),
                wait=w, inc=(pex if j == NDT - 1 else None))
        pex_after[i] = pex.n
        w = [(pex, pex_after[i])]
        if i >= 2:
            w.append((sxn[s], sxn_after[i - 2]))
        p.do("vector", (lambda s: lambda e: e.tensor_copy(
            xstg[s][:], ps_t[s][:]))(s), wait=w, inc=evx)
        evx_after[i] = evx.n
        p.dma("gpsimd", xn_tok_o[i * 128:(i + 1) * 128, :], xstg[s][:],
              wait=[(evx, evx_after[i])], inc=sxn[s])
        sxn_after[i] = sxn[s].n

    lw = [p.sem(f"lw{i}") for i in range(4)]
    pep = p.sem("pep"); evp = p.sem("evp")
    pes = p.sem("pes"); sex = p.sem("sex"); pev = p.sem("pev")
    evv = p.sem("evv"); peo = p.sem("peo"); vdv = p.sem("vdv")
    sao = p.sem("sao")
    lw_after = {}; evp_after = {}; pes_after = {}; sex_after = {}
    pev_after = {}; evv_after = {}; peo_after = {}; vdv_after = {}
    NKS = 32
    gq = 0
    pes_end = {}; pev_end = {}; sex_end = {}
    sexi = 0
    sex_at = {}
    for hp in range(8):
        s = 0
        w = [(lw[0], lw[0].n), (lw[1], lw[1].n)] if False else []
        if hp >= 1:
            w.append((pes, pes_end[hp - 1]))
        p.dma("gpsimd", wcb[0][:].rearrange("p (k c) -> p k c", c=128),
              wk_in[:, hp * 128:(hp + 1) * 128].rearrange("(k p) c -> p k c", p=128),
              wait=w, inc=lw[0])
        p.dma("gpsimd", wcb[1][:].rearrange("p (k c) -> p k c", c=128),
              wq_in[:, hp * 128:(hp + 1) * 128].rearrange("(k p) c -> p k c", p=128),
              inc=lw[1])
        lwk, lwq = lw[0].n, lw[1].n
        kp_ev = {}; qp_ev = {}; vx_ev = {}
        for f in range(8):
            for k in range(8):
                w = []
                if k == 0:
                    w = [(lw[0], lwk), (evq, evq_n)]
                    if f > 0:
                        w.append((evp, kp_ev[f - 1]))
                    elif hp >= 1:
                        w.append((pes, pes_end[hp - 1]))
                        w.append((sex, sex_end[hp - 1]))
                p.do("tensor", (lambda k, f: lambda e: e.matmul(
                    ps_s[0][:], wcb[0][:, k * 128:(k + 1) * 128],
                    xnd[:, k, f * 512:(f + 1) * 512], start=(k == 0), stop=(k == 7)))(k, f),
                    wait=w, inc=(pep if k == 7 else None))
            p.do("scalar", (lambda f: lambda e: e.activation(
                kp[0][:, f * 512:(f + 1) * 512], ps_s[0][:], AF.Copy))(f),
                wait=[(pep, pep.n)], inc=evp)
            kp_ev[f] = evp.n
            for k in range(8):
                w = []
                if k == 0:
                    w = [(lw[1], lwq)]
                    if f > 0:
                        w.append((evp, qp_ev[f - 1]))
                p.do("tensor", (lambda k, f: lambda e: e.matmul(
                    ps_s[1][:], wcb[1][:, k * 128:(k + 1) * 128],
                    xnd[:, k, f * 512:(f + 1) * 512], start=(k == 0), stop=(k == 7)))(k, f),
                    wait=w, inc=(pep if k == 7 else None))
            p.do("scalar", (lambda f: lambda e: e.activation(
                qp[0][:, f * 512:(f + 1) * 512], ps_s[1][:], AF.Copy))(f),
                wait=[(pep, pep.n)], inc=evp)
            qp_ev[f] = evp.n
        for tt in range(NTT):
            for k in range(8):
                w = []
                if k == 0:
                    w = [(lw[0], lwk), (evq, evq_n)]
                    w.append((evp, kp_ev[7] if tt == 0 else vx_ev[tt - 1]))
                    if hp >= 1 and tt == 0:
                        w.append((pev, pev_end[hp - 1]))
                p.do("tensor", (lambda k, tt: lambda e: e.matmul(
                    ps_s[0][:, 0:128], xnd[:, k, tt * 128:(tt + 1) * 128],
                    wcb[0][:, k * 128:(k + 1) * 128], start=(k == 0), stop=(k == 7)))(k, tt),
                    wait=w, inc=(pep if k == 7 else None))
            p.do("scalar", (lambda tt: lambda e: e.activation(
                vx[0][:].rearrange("p (t h c) -> p t h c", h=2, c=65)[:, tt, :, 0:64],
                ps_s[0][:, 0:128].rearrange("p (h c) -> p h c", c=64), AF.Copy))(tt),
                wait=[(pep, pep.n)], inc=evp)
            vx_ev[tt] = evp.n
        if hp == 0:
            msv = p.sem("msv")
            p.do("vector", lambda e: e.memset(
                vx[0][:].rearrange("p (t h c) -> p t h c", h=2, c=65)[:, :, :, 64:65], 1.0),
                inc=msv)
        evp_c = evp.n
        vxv = vx[0][:].rearrange("p (t c) -> p t c", c=130)
        for hh in range(2):
            h = hp * 2 + hh
            dro = hh * 64
            for qcp in range(4):
                qa, qb = 2 * qcp, 2 * qcp + 1
                for k in range(NKS + 1):
                    if k < NKS:
                        sl = sexi % 2
                        for si, qq in ((0, qa), (1, qb)):
                            w = [(evp, evp_c), (msv, 1)]
                            if sexi >= 2 and si == 0:
                                w.append((sex, sex_at[sexi - 2]))
                            p.do("tensor", (lambda dro, k, qq, sl, si: lambda e: e.matmul(
                                ps_d[sl][:, si * 512:(si + 1) * 512],
                                kp[0][:, k * 128:(k + 1) * 128][dro:dro + 64, :],
                                qp[0][:, qq * 512:(qq + 1) * 512][dro:dro + 64, :],
                                start=True, stop=True))(dro, k, qq, sl, si),
                                wait=w, inc=(pes if si == 1 else None))
                        wexp = [(pes, pes.n)] + p.cwait()
                        if sexi >= 2:
                            wexp.append((pev, pev.n))
                        p.do("scalar", (lambda sl: lambda e: e.activation(
                            pT[sl][:], ps_d[sl][:], AF.Exp, bias=-3.0,
                            scale=1.0 / 32))(sl), wait=wexp, inc=sex)
                        sex_at[sexi] = sex.n
                    if k >= 1:
                        kv_ = k - 1
                        slv = (sexi - 1) % 2
                        w = [(sex, sex_at[sexi - 1])]
                        if kv_ == 0 and gq >= 1:
                            w.append((evv, evv_after.get(gq - 1, 0)))
                        for si in (0, 1):
                            p.do("tensor", (lambda hh, kv_, slv, si: lambda e: e.matmul(
                                ps_v[si][:], vxv[:, kv_, hh * 65:(hh + 1) * 65],
                                pT[slv][:, si * 512:(si + 1) * 512],
                                start=(kv_ == 0), stop=(kv_ == NKS - 1)))(hh, kv_, slv, si),
                                wait=(w if si == 0 else []),
                                inc=(pev if si == 1 else None))
                    if k < NKS:
                        sexi += 1
                pevc = pev.n
                for si, qq in ((0, qa), (1, qb)):
                    w = [(pev, pevc)]
                    if gq >= 1:
                        w.append((peo, peo_after.get(gq - 1, 0)))
                    p.do("vector", (lambda si: lambda e: e.tensor_copy(
                        o65[:], ps_v[si][:]))(si), wait=w, inc=evv)
                    evv_after[gq] = evv.n
                    for qs in range(4):
                        w = [(evv, evv_after[gq]), (li, li_n)]
                        if qs == 0 and gq >= 1:
                            w.append((vdv, vdv_after.get(gq - 1, 0)))
                        p.do("tensor", (lambda qs: lambda e: e.transpose(
                            ps_oo[0][:, qs * 65:(qs + 1) * 65],
                            o65[:, qs * 128:(qs + 1) * 128], idnf[0:65, 0:65]))(qs),
                            wait=w, inc=(peo if qs == 3 else None))
                    peo_after[gq] = peo.n
                    for qs in range(4):
                        w = [(peo, peo_after[gq])]
                        p.do("vector", (lambda qs: lambda e: e.reciprocal(
                            rec[0][:], ps_oo[0][:, qs * 65 + 64:qs * 65 + 65]))(qs),
                            wait=w)
                        p.do("vector", (lambda qs, h, qq: lambda e: e.tensor_scalar(
                            ao[:].rearrange("p (t d) -> p t d", d=D)
                            [:, qq * 4 + qs, h * 64:(h + 1) * 64],
                            ps_oo[0][:, qs * 65:qs * 65 + 64],
                            rec[0][:], None, ALU.mult))(qs, h, qq), inc=vdv)
                    vdv_after[gq] = vdv.n
                    gq += 1
        pes_end[hp] = pes.n
        pev_end[hp] = pev.n
        sex_end[hp] = sex.n
    lwo = p.sem("lwo")
    p.dma("gpsimd", kp[0][:].rearrange("p (k o) -> p k o", o=D),
          wo_in[0:512, :].rearrange("(k p) o -> p k o", p=128),
          wait=[(pes, pes_end[7]), (pev, pev_end[7])], inc=lwo)
    p.dma("gpsimd", qp[0][:].rearrange("p (k o) -> p k o", o=D),
          wo_in[512:1024, :].rearrange("(k p) o -> p k o", p=128), inc=lwo)
    lwo_n = lwo.n

    def wob_ap(k, oc):
        t = kp[0] if k < 4 else qp[0]
        return t[:, (k % 4) * 1024 + oc * 512:(k % 4) * 1024 + oc * 512 + 512]
    ldx = [p.sem(f"ldx2{i}") for i in range(2)]
    peT = p.sem("peT"); evT = p.sem("evT"); pem = p.sem("pem"); evo = p.sem("evo")
    str_ = [p.sem(f"str{i}") for i in range(2)]
    peT_after = {}; evT_after = {}; pem_after = {}; evo_after = {}; str_after = {}
    aov = ao[:].rearrange("p (t d) -> p t d", d=D)
    for tt in range(NTT):
        s = tt % 2
        p.dma("sync", xt[s][:], x[tt * 128:(tt + 1) * 128, :],
              wait=([(evo, evo_after[tt - 2])] if tt >= 2 else []), inc=ldx[s])
        for j in range(8):
            w = []
            if j == 0:
                w = [(vdv, vdv.n), (li, li_n)]
                if tt >= 1:
                    w.append((evT, evT_after[tt - 1]))
            p.do("tensor", (lambda s, j, tt: lambda e: e.transpose(
                ps_t[s][:, j * 128:(j + 1) * 128],
                aov[:, tt, j * 128:(j + 1) * 128], idn[:]))(s, j, tt),
                wait=w, inc=(peT if j == 7 else None))
        peT_after[tt] = peT.n
        w_ev = [(peT, peT_after[tt])]
        if tt < 2:
            w_ev += [(sxn[0], sxn[0].n), (sxn[1], sxn[1].n)]
        p.do("vector", (lambda s: lambda e: e.tensor_copy(
            aodt[s][:], ps_t[s][:]))(s), wait=w_ev, inc=evT)
        evT_after[tt] = evT.n
        for oc in range(2):
            bank = ps_s[oc]
            for k in range(8):
                w = []
                if k == 0:
                    w = [(evT, evT_after[tt]), (li, li_n), (lwo, lwo_n)]
                    if tt >= 1:
                        w.append((evo, evo_after.get(tt - 1, 0)))
                p.do("tensor", (lambda s, k, oc, bank: lambda e: e.matmul(
                    bank[:], aodt[s][:, k * 128:(k + 1) * 128],
                    wob_ap(k, oc), start=(k == 0), stop=(k == 7)))(
                    s, k, oc, bank), wait=w, inc=(pem if k == 7 else None))
        pem_after[tt] = pem.n
        w = [(pem, pem_after[tt]), (ldx[s], ldx[s].n)]
        if tt >= 2:
            w.append((str_[s], str_after[tt - 2]))
        p.do("vector", (lambda s: lambda e: e.tensor_tensor(
            otl[s][:, 0:512], ps_s[0][:], borep[:, 0:512], ALU.add))(s),
            wait=w + [(lig, 16)])
        p.do("vector", (lambda s: lambda e: e.tensor_tensor(
            otl[s][:, 512:1024], ps_s[1][:], borep[:, 512:1024], ALU.add))(s))
        p.do("vector", (lambda s: lambda e: e.tensor_tensor(
            otl[s][:], otl[s][:], xt[s][:], ALU.add))(s), inc=evo)
        evo_after[tt] = evo.n
        p.dma("gpsimd", r2[tt * 128:(tt + 1) * 128, :], otl[s][:],
              wait=[(evo, evo_after[tt])], inc=str_[s])
        str_after[tt] = str_[s].n
    p.do("sync", lambda e: e.nop(),
         wait=[(s_, s_.n) for s_ in str_ + sxn])
    return p


def build_l2c(p=None):
    if p is None:
        p = Prog("l2c")
    xn_in = p.inp("xn_tok", [T, D], BF16)
    nsc = p.inp("norm_scale", [D])
    wrT = p.inp("w_router_T", [D, 8])
    wupT = p.inp("w_upT", [8, D, 5632])
    wdnT = p.inp("w_downT", [8, 2816, D])
    r2in = p.inp("r2", [T, D])
    idn_in = p.inp("idn", [128, 128], BF16)
    x3 = p.out("x3", [T, D])

    idn = p.sb("idn_s", [128, 128], BF16)
    xnr_d = p.sb("xnr_d", [128, NDT * T], BF16)
    yh = p.sb("yh", [128, 16 * D], F32)
    xnt = [p.sb(f"xnt{i}", [128, D], BF16) for i in range(2)]
    rt1 = p.sb("rt1", [128, D], F32)
    xrt = [rt1[:, 0:512].bitcast(BF16), rt1[:, 512:1024].bitcast(BF16)]
    sq = p.sb("sq", [128, D], F32)
    ssc = p.sb("ssc", [128, NTT], F32)
    rstd = p.sb("rstd", [128, NTT], F32)
    ot2 = p.sb("ot2", [128, D], F32)
    screp = ot2[:, 0:512].bitcast(BF16)
    coef = p.sb("coef", [128, NTT * 8], F32)
    scs = p.sb("scs", [128, 8], F32)
    m1 = p.sb("m1", [128, 1], F32)
    m2 = p.sb("m2", [128, 1], F32)
    eqm = p.sb("eqm", [128, 8], F32)
    prb = p.sb("prb", [128, 8], F32)
    wrb = p.sb("wrb", [128, 8 * 8], BF16)
    wval = [p.sb(f"wval{i}", [128, 8 * 512], BF16) for i in range(2)]
    wgat = [p.sb(f"wgat{i}", [128, 8 * 512], BF16) for i in range(2)]
    wdnb = [p.sb(f"wdnb{i}", [128, 4 * D], BF16) for i in range(2)]
    gbufs = [p.sb(f"gb{i}", [128, 4 * 512], BF16) for i in range(2)]
    sgb = p.sb("sgb", [128, 512], BF16)
    rt0 = p.sb("rt0", [128, D], F32)
    rt = [rt0, rt1]

    ps_u = [p.ps(f"ps_u{i}", [128, 512], F32) for i in range(2)]
    pu = None
    ps_y = [p.ps(f"ps_y{i}", [128, 512], F32) for i in range(2)]
    ps_t = [p.ps(f"ps_t{i}", [128, 1024], BF16) for i in range(2)]
    ps_r = p.ps("ps_r", [128, 8], F32)
    pu = [[ps_u[0], ps_u[1]],
          [ps_t[0][:].bitcast(F32), ps_t[1][:].bitcast(F32)]]

    ot = [sq, ot2]
    li = p.sem("li")
    lig = p.sem("lig")
    p.dma("sync", idn[:], idn_in, inc=li)
    p.dma("gpsimd", screp[:], nsc.unsqueeze(0).partition_broadcast(128).squeeze(1), inc=lig)
    p.dma("gpsimd", wrb[:].rearrange("p (k c) -> p k c", c=8),
          wrT.rearrange("(k p) c -> p k c", p=128), inc=lig)
    li_n = li.n
    eps_ap = p.cbias(1e-6)

    ldx = [p.sem(f"ldx{i}") for i in range(2)]
    vrm = p.sem("vrm"); srm = p.sem("srm"); pet = p.sem("pet"); evq = p.sem("evq")
    ldx_after = {}; vrm_after = {}; pet_after = {}; evq_after = {}
    xndv = xnr_d[:].rearrange("p (dt t) -> p dt t", t=T)
    for i in range(NTT):
        s = i % 2
        w = [(vrm, vrm_after[i - 2])] if i >= 2 else []
        p.dma("sync", xnt[s][:], xn_in[i * 128:(i + 1) * 128, :], wait=w, inc=ldx[s])
        ldx_after[i] = ldx[s].n
        p.do("scalar", (lambda s, i: lambda e: e.activation(
            sq[:], xnt[s][:], AF.Square, accum_out=ssc[:, i:i + 1]))(s, i),
            wait=[(ldx[s], ldx_after[i])], inc=srm)
        p.do("scalar", (lambda i: lambda e: e.activation(
            rstd[:, i:i + 1], ssc[:, i:i + 1], AF.Sqrt, bias=eps_ap, scale=1.0 / D))(i),
            wait=p.cwait(), inc=srm)
        p.do("vector", (lambda i: lambda e: e.reciprocal(
            rstd[:, i:i + 1], rstd[:, i:i + 1]))(i), wait=[(srm, srm.n)])
        p.do("vector", (lambda s, i: lambda e: e.scalar_tensor_tensor(
            xrt[s][:], xnt[s][:], rstd[:, i:i + 1], screp[:], ALU.mult, ALU.mult))(s, i),
            wait=[(lig, 32)], inc=vrm)
        vrm_after[i] = vrm.n
        for j in range(NDT):
            w = [(vrm, vrm_after[i]), (li, li_n)]
            if i >= 2 and j == 0:
                w.append((evq, evq_after[i - 2]))
            p.do("tensor", (lambda s, i, j: lambda e: e.transpose(
                ps_t[i % 2][:, j * 128:(j + 1) * 128], xrt[s][:, j * 128:(j + 1) * 128],
                idn[:]))(s, i, j), wait=(w if j == 0 else []), inc=(pet if j == NDT - 1 else None))
        pet_after[i] = pet.n
        p.do("vector", (lambda i: lambda e: e.tensor_copy(
            xndv[:, :, i * 128:(i + 1) * 128],
            ps_t[i % 2][:].rearrange("p (dt c) -> p dt c", c=128)))(i),
            wait=[(pet, pet_after[i])], inc=evq)
        evq_after[i] = evq.n
    evq_n = evq.n

    per = p.sem("per"); vco = p.sem("vco"); sac = p.sem("sac")
    vco_tile = {}
    for i in range(NTT):
        for k in range(8):
            w = [(evq, evq_after[i]), (li, li_n), (lig, 32)] if k == 0 else []
            if k == 0 and i >= 1:
                w.append((vco, vco_tile[i - 1]))
            p.do("tensor", (lambda i, k: lambda e: e.matmul(
                ps_r[:], xndv[:, k, i * 128:(i + 1) * 128],
                wrb[:, k * 8:(k + 1) * 8], start=(k == 0), stop=(k == 7)))(i, k),
                wait=w, inc=(per if k == 7 else None))
        pc = per.n
        p.do("vector", (lambda i: lambda e: e.tensor_copy(scs[:], ps_r[:]))(i),
             wait=[(per, pc)], inc=vco)
        p.do("vector", lambda e: e.tensor_reduce(m1[:], scs[:], AX.X, ALU.max), inc=vco)
        p.do("vector", lambda e: e.tensor_scalar(eqm[:], scs[:], m1[:], None, ALU.is_ge), inc=vco)
        p.do("vector", lambda e: e.scalar_tensor_tensor(
            eqm[:], eqm[:], -1e9, scs[:], ALU.mult, ALU.add), inc=vco)
        p.do("vector", lambda e: e.tensor_reduce(m2[:], eqm[:], AX.X, ALU.max), inc=vco)
        vco_c5 = vco.n
        p.do("scalar", (lambda i: lambda e: e.activation(
            prb[:], scs[:], AF.Exp, bias=0.0, scale=1.0))(i), wait=[(vco, vco_c5)])
        p.do("scalar", (lambda i: lambda e: e.activation(
            sq[:, 0:8], prb[:], AF.Copy, accum_out=m1[:]))(i), inc=sac)
        p.do("vector", lambda e: e.reciprocal(m1[:], m1[:]), wait=[(sac, sac.n)])
        p.do("vector", lambda e: e.tensor_scalar(prb[:], prb[:], m1[:], None, ALU.mult))
        p.do("vector", lambda e: e.tensor_scalar(eqm[:], scs[:], m2[:], None, ALU.is_ge))
        p.do("vector", (lambda i: lambda e: e.tensor_tensor(
            coef[:, i * 8:(i + 1) * 8], prb[:], eqm[:], ALU.mult))(i), inc=vco)
        vco_tile[i] = vco.n
    vco_n = vco.n

    CH = [(0, 512), (512, 512), (1024, 512), (1536, 512), (2048, 512), (2560, 256)]
    lwv = [p.sem(f"lwv{i}") for i in range(2)]
    peu = p.sem("peu"); evg = p.sem("evg"); ssl = p.sem("ssl")
    pey = p.sem("pey"); evy = p.sem("evy")
    gi = 0
    ci = 0
    evy_after = {}; peu_after = {}; evg_after = {}; pey_after = {}
    lwv_after = {}
    lr = [p.sem(f"lr{i}") for i in range(2)]
    lx2 = [p.sem(f"lx2{i}") for i in range(2)]
    stx = [p.sem(f"stx{i}") for i in range(2)]
    vfa = p.sem("vfa")
    vfa_after = {}; stx_after = {}
    evy_half = {}

    def emit_fadd(half, t):
        s = t % 2
        tt = half * 16 + t
        g = tt
        w = [(vfa, vfa_after[g - 2])] if g >= 2 else [(evq, evq_n)]
        p.dma("sync", rt[s][:], r2in[tt * 128:(tt + 1) * 128, :], wait=w, inc=lr[s])
        wx = [(vrm, vrm.n)] if g < 2 else []
        p.dma("sync", xnt[s][:], xn_in[tt * 128:(tt + 1) * 128, :], wait=wx, inc=lx2[s])
        w = [(lr[s], lr[s].n), (lx2[s], lx2[s].n), (evy, evy_half[half])]
        if g >= 2:
            w.append((stx[s], stx_after[g - 2]))
        p.do("gpsimd", (lambda s, t: lambda e: e.tensor_tensor(
            ot[s][:], rt[s][:], yh[:].rearrange("p (t d) -> p t d", d=D)[:, t, :],
            ALU.add))(s, t), wait=w)
        p.do("gpsimd", (lambda s: lambda e: e.tensor_tensor(
            ot[s][:], ot[s][:], xnt[s][:], ALU.add))(s), inc=vfa)
        vfa_after[g] = vfa.n
        p.dma("sync", x3[tt * 128:(tt + 1) * 128, :], ot[s][:],
              wait=[(vfa, vfa_after[g])], inc=stx[s])
        stx_after[g] = stx[s].n

    dcnt = 0
    pey_tg = {}
    gtg = 0

    def emit_downs(pend):
        nonlocal dcnt
        ci_, s_, tg_, gtg_, evg_l, nfs_, first_, half_, _e2 = pend
        if half_ == 1 and first_:
            for ts4 in range(4):
                emit_fadd(0, tg_ * 4 + ts4)
        tb_ = half_ * 16
        for chh in range(2):
            for ts in range(4):
                bank = ps_y[(chh * 4 + ts) % 2]
                for fs in range(nfs_):
                    w = []
                    if fs == 0:
                        w = [(evg, evg_l)]
                        if dcnt >= 2:
                            w.append((evy, evy.n - 1))
                    p.do("tensor", (lambda s_, fs, chh, ts, bank, nfs_, gtg_:
                                    lambda e: e.matmul(
                        bank[:], gbufs[gtg_ % 2][:, fs * 512 + ts * 128: fs * 512 + (ts + 1) * 128],
                        wdnb[s_][:].rearrange("p (f c) -> p f c", c=D)[:, fs, chh * 512:(chh + 1) * 512],
                        start=(fs == 0), stop=(fs == nfs_ - 1)))(s_, fs, chh, ts, bank, nfs_, gtg_),
                        wait=w, inc=(pey if fs == nfs_ - 1 else None))
                dcnt += 1
                pyc = pey.n
                ti_g = tb_ + tg_ * 4 + ts
                wv = [(pey, pyc), (vco, vco_tile[ti_g])]
                if first_ and half_ == 1:
                    wv.append((vfa, vfa_after[tg_ * 4 + ts]))
                if first_:
                    p.do("vector", (lambda bank, ti_g, e2, tg_, ts, chh:
                                    lambda e: e.tensor_scalar(
                        yh[:].rearrange("p (t d) -> p t d", d=D)[:, tg_ * 4 + ts, chh * 512:(chh + 1) * 512],
                        bank[:], coef[:, ti_g * 8 + e2: ti_g * 8 + e2 + 1],
                        None, ALU.mult))(bank, ti_g, pend[8], tg_, ts, chh),
                        wait=wv, inc=evy)
                else:
                    p.do("vector", (lambda bank, ti_g, e2, tg_, ts, chh:
                                    lambda e: e.scalar_tensor_tensor(
                        yh[:].rearrange("p (t d) -> p t d", d=D)[:, tg_ * 4 + ts, chh * 512:(chh + 1) * 512],
                        bank[:], coef[:, ti_g * 8 + e2: ti_g * 8 + e2 + 1],
                        yh[:].rearrange("p (t d) -> p t d", d=D)[:, tg_ * 4 + ts, chh * 512:(chh + 1) * 512],
                        ALU.mult, ALU.add))(bank, ti_g, pend[8], tg_, ts, chh),
                        wait=wv, inc=evy)
        pey_tg[gtg_] = pey.n

    for half in range(2):
        tb = half * 16
        pend = None
        for e_ in range(8):
            for (c0, cw) in CH:
                s = ci % 2
                nfs = cw // 128
                w = []
                if ci >= 2:
                    w.append((peu, peu_after[ci - 2]))
                    w.append((pey, pey_tg[gtg - 5]))
                p.dma("gpsimd", wval[s][:, 0:8 * cw].rearrange("p (k c) -> p k c", c=cw),
                      wupT[e_][:, c0:c0 + cw].rearrange("(k p) c -> p k c", p=128),
                      wait=w, inc=lwv[s])
                p.dma("gpsimd", wgat[s][:, 0:8 * cw].rearrange("p (k c) -> p k c", c=cw),
                      wupT[e_][:, 2816 + c0:2816 + c0 + cw].rearrange("(k p) c -> p k c", p=128),
                      inc=lwv[s])
                p.dma("gpsimd", wdnb[s][:, 0:nfs * D].rearrange("p (f c) -> p f c", c=D),
                      wdnT[e_][c0:c0 + cw, :].rearrange("(f p) c -> p f c", p=128),
                      inc=lwv[s])
                lwv_after[ci] = lwv[s].n
                for tg in range(4):
                    tok0 = half * 2048 + tg * 512
                    for fs in range(nfs):
                        sel = gi % 2
                        for which, wt in [(0, wval), (1, wgat)]:
                            bank = pu[sel][which]
                            for k in range(8):
                                w = []
                                if k == 0:
                                    w = [(lwv[s], lwv_after[ci]),
                                         (evq, evq_after[half * 16 + tg * 4 + 3])]
                                    if gi >= 2:
                                        w.append((evg, evg_after[gi - 2]))
                                p.do("tensor", (lambda s, k, fs, cw, tok0, wt, bank:
                                                lambda e: e.matmul(
                                    bank[:], wt[s][:].rearrange("p (k c) -> p k c", c=cw)[:, k, fs * 128:(fs + 1) * 128],
                                    xndv[:, k, tok0:tok0 + 512],
                                    start=(k == 0), stop=(k == 7)))(s, k, fs, cw, tok0, wt, bank),
                                    wait=w, inc=(peu if k == 7 else None))
                        peu_after[ci] = peu.n
                        wg = [(peu, peu.n)]
                        p.do("scalar", (lambda sel: lambda e: e.activation(
                            sgb[:], pu[sel][1][:], AF.Silu))(sel),
                            wait=wg, inc=ssl)
                        wm = [(ssl, ssl.n)]
                        if fs == 0 and gtg >= 2:
                            wm.append((pey, pey_tg[gtg - 2]))
                        p.do("vector", (lambda fs, sel, gtg: lambda e: e.tensor_tensor(
                            gbufs[gtg % 2][:, fs * 512:(fs + 1) * 512], pu[sel][0][:], sgb[:],
                            ALU.mult))(fs, sel, gtg), wait=wm, inc=evg)
                        evg_after[gi] = evg.n
                        gi += 1
                    if pend is not None:
                        emit_downs(pend)
                    pend = (ci, s, tg, gtg, evg_after[gi - 1], nfs,
                            (e_ == 0 and c0 == 0), half, e_)
                    gtg += 1
                pey_after[ci] = pey.n
                ci += 1
        emit_downs(pend)
        pend = None
        evy_half[half] = evy.n
    for t in range(16):
        emit_fadd(1, t)
    p.do("sync", lambda e: e.nop(), wait=[(s_, s_.n) for s_ in stx])
    return p
L3A_PAIRS = 8
L3A_PROJ = True
L3A_STAGE = "full"
L3A_DEBUG = False


def build_l3a(p=None):
    if p is None:
        p = Prog("l3a")
    x = p.inp("x", [T, D])
    wqkvT = p.inp("w_qkvT", [D, 3 * D])
    wprT = p.inp("w_projT", [D, D])
    qnw = p.inp("q_norm_w", [64])
    knw = p.inp("k_norm_w", [64])
    costab = p.inp("costab", [T, 32])
    sintab = p.inp("sintab", [T, 32])
    cmask = p.inp("cmask", [128, 896])
    idn_in = p.inp("idn", [128, 128], BF16)
    idnf_in = p.inp("idnf", [128, 128])
    xnd_o = p.out("xn_d", [D, T], BF16)
    r3 = p.out("r3", [T, D])
    if L3A_DEBUG:
        dbg_qd = p.out("dbg_qd", [128, T], BF16)
        dbg_kd = p.out("dbg_kd", [128, T], BF16)
        dbg_ao = p.out("dbg_ao", [T, D], BF16)

    idn = p.sb("idn_s", [128, 128], BF16)
    idnf = p.sb("idnf_s", [128, 128], F32)
    xn_d = p.sb("xn_d_s", [128, NDT * T], BF16)
    ao = p.sb("ao", [128, NTT * D], BF16)
    cosb = p.sb("cosb", [128, NTT * 32], BF16)
    sinb = p.sb("sinb", [128, NTT * 32], BF16)
    cmb = p.sb("cmb", [128, 896], BF16)
    wnr = p.sb("wnr", [128, 256], F32)
    wcb = [p.sb(f"wcb{i}", [128, 8 * 128], BF16) for i in range(3)]
    qtok = p.sb("qtok", [128, NTT * 128], BF16)
    ktok = p.sb("ktok", [128, NTT * 128], BF16)
    qro = p.sb("qro", [128, NTT * 128], BF16)
    kro = p.sb("kro", [128, NTT * 128], BF16)
    q_d = p.sb("q_d", [128, T], BF16)
    k_d = p.sb("k_d", [128, T], BF16)
    vx = p.sb("vx", [128, NTT * 130], BF16)
    sscr = p.sb("sscr", [128, 128], F32)
    rs2 = p.sb("rs2", [128, 2], F32)
    pT = [p.sb(f"pT{i}", [128, 1024], BF16) for i in range(2)]
    o65 = p.sb("o65", [65, 512], F32)
    rec = [p.sb(f"rec{i}", [128, 1], F32) for i in range(2)]
    aodt = [None, None]
    xt = [None, None]
    otl = [None, None]
    ps_d = [p.ps(f"ps_d{i}", [128, 1024], F32) for i in range(2)]
    ps_s = [ps_d[0][:, 0:512], ps_d[0][:, 512:1024]]
    ps_v = [p.ps(f"ps_v{i}", [65, 512], F32) for i in range(2)]
    ps_oo = [p.ps("ps_oo0", [128, 4 * 65], F32), None]
    ps_oo[1] = ps_oo[0]

    li = p.sem("li"); lig = p.sem("lig")
    p.dma("sync", idn[:], idn_in, inc=li)
    p.dma("sync", idnf[:], idnf_in, inc=li)
    p.dma("gpsimd", cmb[:], cmask, inc=lig)
    p.dma("gpsimd", cosb[:].rearrange("p (t c) -> p t c", c=32),
          costab.rearrange("(t p) c -> p t c", p=128), inc=lig)
    p.dma("gpsimd", sinb[:].rearrange("p (t c) -> p t c", c=32),
          sintab.rearrange("(t p) c -> p t c", p=128), inc=lig)
    for rep in range(2):
        p.dma("gpsimd", wnr[:, rep * 64:(rep + 1) * 64],
              qnw.unsqueeze(0).partition_broadcast(128).squeeze(1), inc=lig)
        p.dma("gpsimd", wnr[:, 128 + rep * 64: 128 + (rep + 1) * 64],
              knw.unsqueeze(0).partition_broadcast(128).squeeze(1), inc=lig)
    li_n, lig_n = li.n, lig.n
    p.cbias(-10.0)
    eps6_ap = p.cbias(1e-6)

    ln_bufs = ([qro[:, 0:2048].bitcast(F32), qro[:, 2048:4096].bitcast(F32)],
               [kro[:, 0:2048].bitcast(F32), kro[:, 2048:4096].bitcast(F32)],
               [qtok[:, 0:1024], qtok[:, 1024:2048]])
    evq, evq_n, _ = emit_ln(p, x, xn_d, idn, "ln_", idn_wait=[(li, li_n)],
                            bufs=ln_bufs, ps_t_bufs=1)
    ps_t = p._ln_ps_t
    xnd = xn_d[:].rearrange("p (dt t) -> p dt t", t=T)
    sxn = [p.sem(f"sxn{i}") for i in range(2)]
    for k in range(NDT):
        p.dma("gpsimd", xnd_o[k * 128:(k + 1) * 128, :], xnd[:, k, :],
              wait=[(evq, evq_n)], inc=sxn[k % 2])

    lw = [p.sem(f"lw{i}") for i in range(3)]
    pep = p.sem("pep"); evp = p.sem("evp"); vnr = p.sem("vnr"); snr = p.sem("snr")
    peq = p.sem("peq"); evdq = p.sem("evdq")
    pes = p.sem("pes"); sex = p.sem("sex"); pev = p.sem("pev")
    evv = p.sem("evv"); peo = p.sem("peo"); vdv = p.sem("vdv")
    vmk = p.sem("vmk")
    msv = p.sem("msv")
    p.do("vector", lambda e: e.memset(
        vx[:].rearrange("p (t h c) -> p t h c", h=2, c=65)[:, :, :, 64:65], 1.0), inc=msv)

    cosv = cosb[:].rearrange("p (t c) -> p t c", c=32)
    sinv = sinb[:].rearrange("p (t c) -> p t c", c=32)
    pes_end = {}; pev_end = {}
    sexi = 0
    sex_at = {}
    gq = 0
    evv_after = {}; peo_after = {}; vdv_after = {}
    evdq_slot = {}; evdq_prev_pair = 0
    for hp in range(L3A_PAIRS):
        wq_l = []
        if hp >= 1:
            wq_l = [(pes, pes_end[hp - 1]), (pev, pev_end[hp - 1])]
        for wi, coff in [(0, hp * 128), (1, 1024 + hp * 128), (2, 2048 + hp * 128)]:
            p.dma("gpsimd", wcb[wi][:].rearrange("p (k c) -> p k c", c=128),
                  wqkvT[:, coff:coff + 128].rearrange("(k p) c -> p k c", p=128),
                  wait=(wq_l if wi == 0 else []), inc=lw[wi])
        lw_n = [lw[i].n for i in range(3)]
        pj = 0
        evp_at = {}
        for tt in range(NTT):
            for wi in range(3):
                sl_ = pj % 2
                for k in range(8):
                    w = []
                    if k == 0:
                        w = [(lw[wi], lw_n[wi]), (evq, evq_n)]
                        if hp >= 1 and tt == 0 and wi == 0:
                            w.append((pev, pev_end[hp - 1]))
                        if pj >= 2:
                            w.append((evp, evp_at[pj - 2]))
                    p.do("tensor", (lambda k, tt, wi, sl_: lambda e: e.matmul(
                        ps_d[sl_][:, 0:128], xnd[:, k, tt * 128:(tt + 1) * 128],
                        wcb[wi][:, k * 128:(k + 1) * 128], start=(k == 0), stop=(k == 7)))(k, tt, wi, sl_),
                        wait=w, inc=(pep if k == 7 else None))
                dst = [qtok, ktok, None][wi]
                if wi < 2:
                    p.do("scalar", (lambda dst, tt, sl_: lambda e: e.activation(
                        dst[:, tt * 128:(tt + 1) * 128], ps_d[sl_][:, 0:128], AF.Copy))(dst, tt, sl_),
                        wait=[(pep, pep.n)], inc=evp)
                else:
                    p.do("scalar", (lambda tt, sl_: lambda e: e.activation(
                        vx[:].rearrange("p (t h c) -> p t h c", h=2, c=65)[:, tt, :, 0:64],
                        ps_d[sl_][:, 0:128].rearrange("p (h c) -> p h c", c=64), AF.Copy))(tt, sl_),
                        wait=[(pep, pep.n), (msv, 1)], inc=evp)
                evp_at[pj] = evp.n
                pj += 1
        evp_c = evp.n
        if L3A_STAGE == "proj":
            pes_end[hp] = pes.n; pev_end[hp] = pev.n
            continue
        for tt in range(NTT):
            for qi, (tok, ro, woff) in enumerate([(qtok, qro, 0), (ktok, kro, 128)]):
                tv = tok[:, tt * 128:(tt + 1) * 128]
                rv = ro[:, tt * 128:(tt + 1) * 128]
                w = [(evp, evp_c), (lig, lig_n)]
                p.do("vector", (lambda tv: lambda e: e.tensor_tensor(
                    sscr[:], tv, tv, ALU.mult))(tv), wait=w)
                p.do("vector", (lambda: lambda e: e.tensor_reduce(
                    rs2[:], sscr[:].rearrange("p (h c) -> p h c", c=64), AX.X, ALU.add))(),
                    inc=vnr)
                p.do("scalar", (lambda: lambda e: e.activation(
                    rs2[:], rs2[:], AF.Sqrt, bias=eps6_ap, scale=1.0 / 64))(),
                    wait=[(vnr, vnr.n)] + p.cwait(), inc=snr)
                p.do("vector", (lambda: lambda e: e.reciprocal(rs2[:], rs2[:]))(),
                     wait=[(snr, snr.n)])
                for hh in range(2):
                    p.do("vector", (lambda tv, hh: lambda e: e.tensor_scalar(
                        sscr[:, hh * 64:(hh + 1) * 64], tv[:, hh * 64:(hh + 1) * 64],
                        rs2[:, hh:hh + 1], None, ALU.mult))(tv, hh))
                p.do("vector", (lambda woff: lambda e: e.tensor_tensor(
                    sscr[:], sscr[:], wnr[:, woff:woff + 128], ALU.mult))(woff))
                sv = sscr[:].rearrange("p (h l c) -> p h l c", h=2, c=32)
                rvv = rv.rearrange("p (h l c) -> p h l c", h=2, c=32)
                cb = cosv[:, tt, :].unsqueeze(1).broadcast_to((128, 2, 32))
                sb2 = sinv[:, tt, :].unsqueeze(1).broadcast_to((128, 2, 32))
                p.do("vector", (lambda rvv, sv, cb: lambda e: e.tensor_tensor(
                    rvv[:, :, 0, :], sv[:, :, 0, :], cb, ALU.mult))(rvv, sv, cb))
                p.do("vector", (lambda rvv, sv, sb2: lambda e: e.tensor_tensor(
                    rvv[:, :, 1, :], sv[:, :, 0, :], sb2, ALU.mult))(rvv, sv, sb2))
                p.do("vector", (lambda sv, cb: lambda e: e.tensor_tensor(
                    sv[:, :, 0, :], sv[:, :, 1, :], cb, ALU.mult))(sv, cb))
                p.do("vector", (lambda rvv, sv: lambda e: e.tensor_tensor(
                    rvv[:, :, 1, :], sv[:, :, 0, :], rvv[:, :, 1, :], ALU.add))(rvv, sv))
                p.do("vector", (lambda sv, sb2: lambda e: e.tensor_tensor(
                    sv[:, :, 0, :], sv[:, :, 1, :], sb2, ALU.mult))(sv, sb2))
                p.do("vector", (lambda rvv, sv: lambda e: e.tensor_tensor(
                    rvv[:, :, 0, :], rvv[:, :, 0, :], sv[:, :, 0, :], ALU.subtract))(rvv, sv),
                    inc=vnr)
        vnr_c = vnr.n
        if L3A_STAGE == "rope":
            pes_end[hp] = pes.n; pev_end[hp] = pev.n
            continue
        for qi, (ro, dmat) in enumerate([(qro, q_d), (kro, k_d)]):
            for tg4 in range(NTT // 4):
                for j4 in range(4):
                    tt = tg4 * 4 + j4
                    w = []
                    if j4 == 0:
                        w = [(vnr, vnr_c), (li, li_n)]
                        if hp >= 1 and tg4 == 0 and qi == 0:
                            w.append((pes, pes_end[hp - 1]))
                        if tg4 >= 1:
                            w.append((evdq, evdq_slot[(qi, tg4 - 1)]))
                        elif qi == 1:
                            w.append((evdq, evdq_slot[(0, NTT // 4 - 1)]))
                        elif hp >= 1:
                            w.append((evdq, evdq_prev_pair))
                    p.do("tensor", (lambda ro, tt, qi, j4: lambda e: e.transpose(
                        ps_t[qi][:, j4 * 128:(j4 + 1) * 128],
                        ro[:, tt * 128:(tt + 1) * 128], idn[:]))(ro, tt, qi, j4),
                        wait=w, inc=(peq if j4 == 3 else None))
                pq = peq.n
                p.do("vector", (lambda dmat, tg4, qi: lambda e: e.tensor_copy(
                    dmat[:, tg4 * 512:(tg4 + 1) * 512],
                    ps_t[qi][:, 0:512]))(dmat, tg4, qi),
                    wait=[(peq, pq)], inc=evdq)
                evdq_slot[(qi, tg4)] = evdq.n
        evdq_c = evdq.n
        evdq_prev_pair = evdq.n
        if L3A_STAGE == "trans":
            pes_end[hp] = pes.n; pev_end[hp] = pev.n
            continue
        vxv = vx[:].rearrange("p (t c) -> p t c", c=130)
        for hh in range(2):
            h = hp * 2 + hh
            dro = hh * 64
            for qcp in range(4):
                qa, qb = 2 * qcp, 2 * qcp + 1
                ksa, ksb = 4 * qa + 4, 4 * qb + 4
                for k in range(ksb + 1):
                    if k < ksb:
                        sl = sexi % 2
                        both = k < ksa
                        w = [(evdq, evdq_c)]
                        if sexi >= 2:
                            w.append((sex, sex_at[sexi - 2]))
                        if both:
                            p.do("tensor", (lambda dro, k, qa, sl: lambda e: e.matmul(
                                ps_d[sl][:, 0:512],
                                k_d[:, k * 128:(k + 1) * 128][dro:dro + 64, :],
                                q_d[:, qa * 512:(qa + 1) * 512][dro:dro + 64, :],
                                start=True, stop=True))(dro, k, qa, sl),
                                wait=w)
                            w = []
                        p.do("tensor", (lambda dro, k, qb, sl: lambda e: e.matmul(
                            ps_d[sl][:, 512:1024],
                            k_d[:, k * 128:(k + 1) * 128][dro:dro + 64, :],
                            q_d[:, qb * 512:(qb + 1) * 512][dro:dro + 64, :],
                            start=True, stop=True))(dro, k, qb, sl),
                            wait=w, inc=pes)
                        pes_c = pes.n
                        wexp = [(pes, pes_c)]
                        mq = qa if (both and k >= 4 * qa) else (qb if k >= 4 * qb else None)
                        if mq is not None:
                            off = 0 if mq == qa else 512
                            o_ = k * 128 - mq * 512
                            p.do("vector", (lambda sl, o_, off: lambda e: e.tensor_tensor(
                                ps_d[sl][:, off:off + 512], ps_d[sl][:, off:off + 512],
                                cmb[:, 384 - o_:384 - o_ + 512], ALU.add))(sl, o_, off),
                                wait=[(pes, pes_c), (lig, lig_n)], inc=vmk)
                            wexp = [(vmk, vmk.n)]
                        wexp = wexp + p.cwait()
                        if sexi >= 2:
                            wexp.append((pev, pev.n))
                        if both:
                            p.do("scalar", (lambda sl: lambda e: e.activation(
                                pT[sl][:], ps_d[sl][:], AF.Exp, bias=-10.0,
                                scale=0.125))(sl), wait=wexp, inc=sex)
                        else:
                            p.do("scalar", (lambda sl: lambda e: e.activation(
                                pT[sl][:, 512:1024], ps_d[sl][:, 512:1024], AF.Exp,
                                bias=-10.0, scale=0.125))(sl), wait=wexp, inc=sex)
                        sex_at[sexi] = sex.n
                    if k >= 1:
                        kv = k - 1
                        slv = (sexi - 1) % 2
                        bothv = kv < ksa
                        w = [(sex, sex_at[sexi - 1]), (msv, 1)]
                        if kv == 0 and gq >= 2:
                            w.append((evv, gq))
                        if bothv:
                            p.do("tensor", (lambda hh, kv, slv, st_, sp_: lambda e: e.matmul(
                                ps_v[0][:], vxv[:, kv, hh * 65:(hh + 1) * 65],
                                pT[slv][:, 0:512], start=st_, stop=sp_))(
                                hh, kv, slv, kv == 0, kv == ksa - 1),
                                wait=w)
                            w = []
                        p.do("tensor", (lambda hh, kv, slv, st_, sp_: lambda e: e.matmul(
                            ps_v[1][:], vxv[:, kv, hh * 65:(hh + 1) * 65],
                            pT[slv][:, 512:1024], start=st_, stop=sp_))(
                            hh, kv, slv, kv == 0, kv == ksb - 1),
                            wait=w, inc=(pev if kv == ksb - 1 else None))
                    if k < ksb:
                        sexi += 1
                pevc = pev.n
                for si, qq in ((0, qa), (1, qb)):
                    w = [(pev, pevc)]
                    if gq >= 1:
                        w.append((peo, peo_after[gq - 1]))
                    p.do("vector", (lambda si: lambda e: e.tensor_copy(
                        o65[:], ps_v[si][:]))(si), wait=w, inc=evv)
                    evv_after[gq] = evv.n
                    for qs in range(4):
                        w = [(evv, evv_after[gq]), (li, li_n)]
                        if gq >= 1 and qs == 0:
                            w.append((vdv, vdv_after[gq - 1]))
                        p.do("tensor", (lambda qs: lambda e: e.transpose(
                            ps_oo[0][:, qs * 65:(qs + 1) * 65],
                            o65[:, qs * 128:(qs + 1) * 128], idnf[0:65, 0:65]))(qs),
                            wait=w, inc=(peo if qs == 3 else None))
                    peo_after[gq] = peo.n
                    for qs in range(4):
                        w = [(peo, peo_after[gq])]
                        p.do("vector", (lambda qs: lambda e: e.reciprocal(
                            rec[0][:], ps_oo[0][:, qs * 65 + 64:qs * 65 + 65]))(qs),
                            wait=w)
                        p.do("vector", (lambda qs, h, qq: lambda e: e.tensor_scalar(
                            ao[:].rearrange("p (t d) -> p t d", d=D)
                            [:, qq * 4 + qs, h * 64:(h + 1) * 64],
                            ps_oo[0][:, qs * 65:qs * 65 + 64],
                            rec[0][:], None, ALU.mult))(qs, h, qq), inc=vdv)
                    vdv_after[gq] = vdv.n
                    gq += 1
        pes_end[hp] = pes.n
        pev_end[hp] = pev.n

    if not L3A_PROJ:
        fin = p.sem("fin")
        p.do("vector", lambda e: e.tensor_copy(ao[:, 0:4], ao[:, 0:4]),
             wait=[(vdv, vdv.n)], inc=fin)
        p.dma("gpsimd", r3[0:128, 0:64], ao[:, 0:128].bitcast(F32), wait=[(fin, 1)],
              inc=p.sem("strx"))
        p.do("sync", lambda e: e.nop(), wait=[(s_, s_.n) for s_ in sxn])
        return p
    lwp = p.sem("lwp")
    p.dma("gpsimd", qtok[:].rearrange("p (k o) -> p k o", o=D),
          wprT[0:512, :].rearrange("(k p) o -> p k o", p=128),
          wait=[(pes, pes.n), (pev, pev.n), (vnr, vnr.n)], inc=lwp)
    p.dma("gpsimd", ktok[:].rearrange("p (k o) -> p k o", o=D),
          wprT[512:1024, :].rearrange("(k p) o -> p k o", p=128), inc=lwp)
    lwp_n = lwp.n

    def wpv_ap(k, oc):
        t = qtok if k < 4 else ktok
        return t[:, (k % 4) * 1024 + oc * 512:(k % 4) * 1024 + oc * 512 + 512]
    aodt[0] = k_d[:, 0:1024]
    aodt[1] = k_d[:, 1024:2048]
    xt[0] = qro[:, 0:2048].bitcast(F32)
    xt[1] = qro[:, 2048:4096].bitcast(F32)
    otl[0] = kro[:, 0:2048].bitcast(F32)
    otl[1] = kro[:, 2048:4096].bitcast(F32)
    evdq_fin = evdq.n
    ldx = [p.sem(f"ldx2{i}") for i in range(2)]
    peT = p.sem("peT"); evT = p.sem("evT"); pem = p.sem("pem"); evo = p.sem("evo")
    str_ = [p.sem(f"str{i}") for i in range(2)]
    peT_after = {}; evT_after = {}; pem_after = {}; evo_after = {}; str_after = {}
    aov = ao[:].rearrange("p (t d) -> p t d", d=D)
    for tt in range(NTT):
        s = tt % 2
        p.dma("sync", xt[s], x[tt * 128:(tt + 1) * 128, :],
              wait=([(evo, evo_after[tt - 2])] if tt >= 2 else [(evdq, evdq_fin)]),
              inc=ldx[s])
        for j in range(8):
            w = []
            if j == 0:
                w = [(vdv, vdv.n), (li, li_n), (lig, lig_n)]
                if tt >= 1:
                    w.append((evT, evT_after[tt - 1]))
            p.do("tensor", (lambda s, j, tt: lambda e: e.transpose(
                ps_t[s][:, j * 128:(j + 1) * 128],
                aov[:, tt, j * 128:(j + 1) * 128], idn[:]))(s, j, tt),
                wait=w, inc=(peT if j == 7 else None))
        peT_after[tt] = peT.n
        p.do("vector", (lambda s: lambda e: e.tensor_copy(
            aodt[s], ps_t[s][:]))(s), wait=[(peT, peT_after[tt])], inc=evT)
        evT_after[tt] = evT.n
        for oc in range(2):
            bank = ps_s[oc]
            for k in range(8):
                w = []
                if k == 0:
                    w = [(evT, evT_after[tt]), (sex, sexi), (lwp, lwp_n)]
                    if tt >= 1:
                        w.append((evo, evo_after.get(tt - 1, 0)))
                p.do("tensor", (lambda s, k, oc, bank: lambda e: e.matmul(
                    bank[:], aodt[s][:, k * 128:(k + 1) * 128],
                    wpv_ap(k, oc), start=(k == 0), stop=(k == 7)))(
                    s, k, oc, bank), wait=w, inc=(pem if k == 7 else None))
        pem_after[tt] = pem.n
        w = [(pem, pem_after[tt]), (ldx[s], ldx[s].n)]
        if tt >= 2:
            w.append((str_[s], str_after[tt - 2]))
        p.do("vector", (lambda s: lambda e: e.tensor_tensor(
            otl[s][:, 0:512], ps_s[0][:], xt[s][:, 0:512], ALU.add))(s), wait=w)
        p.do("vector", (lambda s: lambda e: e.tensor_tensor(
            otl[s][:, 512:1024], ps_s[1][:], xt[s][:, 512:1024], ALU.add))(s), inc=evo)
        evo_after[tt] = evo.n
        p.dma("gpsimd", r3[tt * 128:(tt + 1) * 128, :], otl[s],
              wait=[(evo, evo_after[tt])], inc=str_[s])
        str_after[tt] = str_[s].n
    p.do("sync", lambda e: e.nop(),
         wait=[(s_, s_.n) for s_ in str_ + sxn])
    return p


def build_l3b(p=None):
    if p is None:
        p = Prog("l3b")
    xnd_in = p.inp("xn_d", [D, T], BF16)
    wr_in = p.inp("Wr", [D, 8])
    w1_in = p.inp("W1", [8, D, 4096])
    w2_in = p.inp("W2", [8, 4096, D])
    r3in = p.inp("r3", [T, D])
    x4 = p.out("x4", [T, D])

    xn_d = p.sb("xn_d_s", [128, NDT * T], BF16)
    yh = p.sb("yh", [128, 16 * D], F32)
    coef = p.sb("coef", [128, NTT * 8], F32)
    scs = p.sb("scs", [128, 8], F32)
    m1 = p.sb("m1", [128, 1], F32)
    m2 = p.sb("m2", [128, 1], F32)
    eqm = p.sb("eqm", [128, 8], F32)
    prb = p.sb("prb", [128, 8], F32)
    sqs = p.sb("sqs", [128, 8], F32)
    wrb = p.sb("wrb", [128, 8 * 8], BF16)
    w1b = [p.sb(f"w1b{i}", [128, 8 * 512], BF16) for i in range(2)]
    w2b = [p.sb(f"w2b{i}", [128, 4 * D], BF16) for i in range(2)]
    hbufs = [p.sb(f"hb{i}", [128, 4 * 512], BF16) for i in range(2)]
    rt = [p.sb(f"rt{i}", [128, D], F32) for i in range(2)]
    ot = [p.sb(f"ot{i}", [128, D], F32) for i in range(2)]
    ps_u = [p.ps(f"ps_u{i}", [128, 512], F32) for i in range(2)]
    ps_y = [p.ps(f"ps_y{i}", [128, 512], F32) for i in range(2)]
    ps_r = p.ps("ps_r", [128, 8], F32)

    lig = p.sem("lig"); lxd = p.sem("lxd")
    p.dma("gpsimd", wrb[:].rearrange("p (k c) -> p k c", c=8),
          wr_in.rearrange("(k p) c -> p k c", p=128), inc=lig)
    for k in range(NDT):
        p.dma("sync", xn_d[:].rearrange("p (dt t) -> p dt t", t=T)[:, k, :],
              xnd_in[k * 128:(k + 1) * 128, :], inc=lxd)
    lig_n, lxd_n = lig.n, lxd.n
    xndv = xn_d[:].rearrange("p (dt t) -> p dt t", t=T)

    per = p.sem("per"); vco = p.sem("vco"); sac = p.sem("sac")
    vco_tile = {}
    for i in range(NTT):
        for k in range(8):
            w = [(lxd, lxd_n), (lig, lig_n)] if k == 0 else []
            if k == 0 and i >= 1:
                w.append((vco, vco_tile[i - 1]))
            p.do("tensor", (lambda i, k: lambda e: e.matmul(
                ps_r[:], xndv[:, k, i * 128:(i + 1) * 128],
                wrb[:, k * 8:(k + 1) * 8], start=(k == 0), stop=(k == 7)))(i, k),
                wait=w, inc=(per if k == 7 else None))
        pc = per.n
        p.do("vector", (lambda i: lambda e: e.tensor_copy(scs[:], ps_r[:]))(i),
             wait=[(per, pc)], inc=vco)
        p.do("vector", lambda e: e.tensor_reduce(m1[:], scs[:], AX.X, ALU.max), inc=vco)
        p.do("vector", lambda e: e.tensor_scalar(eqm[:], scs[:], m1[:], None, ALU.is_ge), inc=vco)
        p.do("vector", lambda e: e.scalar_tensor_tensor(
            eqm[:], eqm[:], -1e9, scs[:], ALU.mult, ALU.add), inc=vco)
        p.do("vector", lambda e: e.tensor_reduce(m2[:], eqm[:], AX.X, ALU.max), inc=vco)
        vco_c5 = vco.n
        p.do("scalar", (lambda i: lambda e: e.activation(
            prb[:], scs[:], AF.Exp, bias=0.0, scale=1.0))(i), wait=[(vco, vco_c5)])
        p.do("scalar", (lambda i: lambda e: e.activation(
            sqs[:], prb[:], AF.Copy, accum_out=m1[:]))(i), inc=sac)
        p.do("vector", lambda e: e.reciprocal(m1[:], m1[:]), wait=[(sac, sac.n)])
        p.do("vector", lambda e: e.tensor_scalar(prb[:], prb[:], m1[:], None, ALU.mult))
        p.do("vector", lambda e: e.tensor_scalar(eqm[:], scs[:], m2[:], None, ALU.is_ge))
        p.do("vector", lambda e: e.tensor_tensor(prb[:], prb[:], eqm[:], ALU.mult))
        p.do("vector", (lambda: lambda e: e.tensor_reduce(
            m2[:], prb[:], AX.X, ALU.add))())
        p.do("vector", lambda e: e.reciprocal(m2[:], m2[:]))
        p.do("vector", (lambda i: lambda e: e.tensor_scalar(
            coef[:, i * 8:(i + 1) * 8], prb[:], m2[:], None, ALU.mult))(i), inc=vco)
        vco_tile[i] = vco.n
    vco_n = vco.n

    lwv = [p.sem(f"lwv{i}") for i in range(2)]
    peu = p.sem("peu"); srl = p.sem("srl")
    pey = p.sem("pey"); evy = p.sem("evy")
    lr = [p.sem(f"lr{i}") for i in range(2)]
    stx = [p.sem(f"stx{i}") for i in range(2)]
    vfa = p.sem("vfa")
    gi = 0; ci = 0
    peu_after = {}; srl_after = {}; pey_after = {}; lwv_after = {}
    vfa_after = {}; stx_after = {}
    evy_half = {}
    dcnt = 0; pey_tg = {}; gtg = 0

    def emit_fadd(half_, t):
        s = t % 2
        tt = half_ * 16 + t
        g = tt
        w = [(vfa, vfa_after[g - 2])] if g >= 2 else []
        p.dma("sync", rt[s][:], r3in[tt * 128:(tt + 1) * 128, :], wait=w, inc=lr[s])
        w = [(lr[s], lr[s].n), (evy, evy_half[half_])]
        if g >= 2:
            w.append((stx[s], stx_after[g - 2]))
        p.do("gpsimd", (lambda s, t: lambda e: e.tensor_tensor(
            ot[s][:], rt[s][:], yh[:].rearrange("p (t d) -> p t d", d=D)[:, t, :],
            ALU.add))(s, t), wait=w, inc=vfa)
        vfa_after[g] = vfa.n
        p.dma("sync", x4[tt * 128:(tt + 1) * 128, :], ot[s][:],
              wait=[(vfa, vfa_after[g])], inc=stx[s])
        stx_after[g] = stx[s].n

    def emit_downs(pend):
        nonlocal dcnt
        ci_, s_, tg_, gtg_, srl_l, first_, half_, e2 = pend
        if half_ == 1 and first_:
            for ts4 in range(4):
                emit_fadd(0, tg_ * 4 + ts4)
        tb_ = half_ * 16
        for chh in range(2):
            for ts in range(4):
                bank = ps_y[ts % 2]
                for fs in range(4):
                    w = []
                    if fs == 0:
                        w = [(srl, srl_l)]
                        if dcnt >= 2:
                            w.append((evy, evy.n - 1))
                    p.do("tensor", (lambda s_, fs, chh, ts, bank, gtg_: lambda e: e.matmul(
                        bank[:], hbufs[gtg_ % 2][:, fs * 512 + ts * 128: fs * 512 + (ts + 1) * 128],
                        w2b[s_][:].rearrange("p (f c) -> p f c", c=D)[:, fs, chh * 512:(chh + 1) * 512],
                        start=(fs == 0), stop=(fs == 3)))(s_, fs, chh, ts, bank, gtg_),
                        wait=w, inc=(pey if fs == 3 else None))
                dcnt += 1
                pyc = pey.n
                ti_g = tb_ + tg_ * 4 + ts
                wv = [(pey, pyc), (vco, vco_tile[ti_g])]
                if first_ and half_ == 1:
                    wv.append((vfa, vfa_after[tg_ * 4 + ts]))
                if first_:
                    p.do("vector", (lambda bank, ti_g, e2, tg_, ts, chh:
                                    lambda e: e.tensor_scalar(
                        yh[:].rearrange("p (t d) -> p t d", d=D)[:, tg_ * 4 + ts, chh * 512:(chh + 1) * 512],
                        bank[:], coef[:, ti_g * 8 + e2: ti_g * 8 + e2 + 1],
                        None, ALU.mult))(bank, ti_g, e2, tg_, ts, chh),
                        wait=wv, inc=evy)
                else:
                    p.do("vector", (lambda bank, ti_g, e2, tg_, ts, chh:
                                    lambda e: e.scalar_tensor_tensor(
                        yh[:].rearrange("p (t d) -> p t d", d=D)[:, tg_ * 4 + ts, chh * 512:(chh + 1) * 512],
                        bank[:], coef[:, ti_g * 8 + e2: ti_g * 8 + e2 + 1],
                        yh[:].rearrange("p (t d) -> p t d", d=D)[:, tg_ * 4 + ts, chh * 512:(chh + 1) * 512],
                        ALU.mult, ALU.add))(bank, ti_g, e2, tg_, ts, chh),
                        wait=wv, inc=evy)
        pey_tg[gtg_] = pey.n

    for half in range(2):
        tb = half * 16
        pend = None
        for e_ in range(8):
            for fc in range(8):
                s = ci % 2
                w = []
                if ci >= 2:
                    w.append((peu, peu_after[ci - 2]))
                    w.append((pey, pey_tg[gtg - 5]))
                p.dma("gpsimd", w1b[s][:].rearrange("p (k c) -> p k c", c=512),
                      w1_in[e_][:, fc * 512:(fc + 1) * 512].rearrange("(k p) c -> p k c", p=128),
                      wait=w, inc=lwv[s])
                p.dma("gpsimd", w2b[s][:].rearrange("p (f c) -> p f c", c=D),
                      w2_in[e_][fc * 512:(fc + 1) * 512, :].rearrange("(f p) c -> p f c", p=128),
                      inc=lwv[s])
                lwv_after[ci] = lwv[s].n
                for tg in range(4):
                    tok0 = half * 2048 + tg * 512
                    for fs in range(4):
                        bank = ps_u[fs % 2]
                        for k in range(8):
                            w = []
                            if k == 0:
                                w = [(lwv[s], lwv_after[ci]), (lxd, lxd_n)]
                                if gi >= 2:
                                    w.append((srl, srl_after[gi - 2]))
                            p.do("tensor", (lambda s, k, fs, tok0, bank: lambda e: e.matmul(
                                bank[:], w1b[s][:].rearrange("p (k c) -> p k c", c=512)[:, k, fs * 128:(fs + 1) * 128],
                                xndv[:, k, tok0:tok0 + 512],
                                start=(k == 0), stop=(k == 7)))(s, k, fs, tok0, bank),
                                wait=w, inc=(peu if k == 7 else None))
                        peu_after[ci] = peu.n
                        wr_ = [(peu, peu.n)]
                        if fs == 0 and gtg >= 2:
                            wr_.append((pey, pey_tg[gtg - 2]))
                        p.do("scalar", (lambda fs, bank, gtg: lambda e: e.activation(
                            hbufs[gtg % 2][:, fs * 512:(fs + 1) * 512], bank[:], AF.Relu))(fs, bank, gtg),
                            wait=wr_, inc=srl)
                        srl_after[gi] = srl.n
                        gi += 1
                    if pend is not None:
                        emit_downs(pend)
                    pend = (ci, s, tg, gtg, srl_after[gi - 1],
                            (e_ == 0 and fc == 0), half, e_)
                    gtg += 1
                pey_after[ci] = pey.n
                ci += 1
        emit_downs(pend)
        pend = None
        evy_half[half] = evy.n
    for t in range(16):
        emit_fadd(1, t)
    p.do("sync", lambda e: e.nop(), wait=[(s_, s_.n) for s_ in stx])
    return p


def build_fused():
    p = Prog("mk")
    nc = p.nc
    I = lambda name, shape, dt_=BF16: nc.dram_tensor(name, list(shape), dt_, kind="Internal").ap()
    h = {
        "xn0": I("h_xn0", [T, D], F32), "xc_d": I("h_xc", [2048, T]),
        "zsil_d": I("h_zs", [2048, T]), "u_d": I("h_u", [2048, T]),
        "dt_d": I("h_dt", [2048, T]), "dtu_d": I("h_dtu", [2048, T]),
        "Bm": I("h_bm", [16, T], F32), "Cm": I("h_cm", [16, T], F32),
        "y_d": I("h_y", [2048, T]), "x1": I("h_x1", [T, D], F32),
        "xn1d": I("h_xn1d", [D, T]), "q1d": I("h_q1d", [D, TQ]),
        "k1d": I("h_k1d", [D, TQ]), "v1": I("h_v1", [TQ, D]),
        "ao1": I("h_ao1", [T, D]), "r1": I("h_r1", [T, D], F32),
        "x2": I("h_x2", [T, D], F32), "xn2": I("h_xn2", [T, D]),
        "r2": I("h_r2", [T, D], F32), "x3": I("h_x3", [T, D], F32),
        "xn3d": I("h_xn3d", [D, T]), "r3": I("h_r3", [T, D], F32),
    }
    phases = [
        (build_l0a, {"xn_tok": "xn0", "xc_d": "xc_d", "zsil_d": "zsil_d"}),
        (build_l0b1, {"xc_d": "xc_d", "u_d": "u_d"}),
        (build_l0b2, {"u_d": "u_d", "dt_d": "dt_d", "dtu_d": "dtu_d",
                      "Bm": "Bm", "Cm": "Cm"}),
        (build_l0c, {"dt_d": "dt_d", "dtu_d": "dtu_d", "u_d": "u_d",
                     "zsil_d": "zsil_d", "Bm": "Bm", "Cm": "Cm", "y_d": "y_d"}),
        (build_l0d, {"y_d": "y_d", "xn_tok": "xn0", "x1": "x1"}),
        (build_l1a, {"x": "x1", "xn_d": "xn1d", "q_d": "q1d", "k_d": "k1d",
                     "v_tok": "v1"}),
        (build_l1b, {"q_d": "q1d", "k_d": "k1d", "v_tok": "v1",
                     "attnout": "ao1"}),
        (build_l1c, {"attnout": "ao1", "x": "x1", "r1": "r1"}),
        (build_l1d, {"xn_d": "xn1d", "r1": "r1", "x2": "x2"}),
        (build_l2b, {"x": "x2", "xn_tok": "xn2", "r2": "r2"}),
        (build_l2c, {"xn_tok": "xn2", "r2": "r2", "x3": "x3"}),
        (build_l3a, {"x": "x3", "xn_d": "xn3d", "r3": "r3"}),
        (build_l3b, {"xn_d": "xn3d", "r3": "r3", "x4": None}),
    ]
    x4 = nc.dram_tensor("x4", [T, D], F32, kind="ExternalOutput").ap()
    p.outs["x4"] = x4
    for cv in (1e-5, 1e-6, -3.0, -10.0):
        p.cbias(cv)
    for bi, (builder, iomap) in enumerate(phases):
        p.begin_phase()
        p.io_map = {k: (x4 if v is None else h[v]) for k, v in iomap.items()}
        builder(p)
        p.io_map = {}
        p.end_phase()
    return p


import ml_dtypes
from concourse.bass_utils import run_bass_kernel_spmd

_FUSED = [None]


def kernel(**inp):
    NB = 8
    C = np.ascontiguousarray
    x = np.asarray(inp["L0_attn_x"], dtype=np.float32)
    idn = np.eye(128).astype(ml_dtypes.bfloat16)
    idnf = np.eye(128, dtype=np.float32)
    freqs = 32.0 ** (-np.arange(0, 64, 2, dtype=np.float32) / 64)
    ang = np.arange(T, dtype=np.float32)[:, None] * freqs[None, :]
    COS = C(np.cos(ang).astype(np.float32))
    SIN = C(np.sin(ang).astype(np.float32))
    U = np.arange(896)[None, :] - 384
    CM = C(np.where(U < np.arange(128)[:, None], -1e30, 0.0).astype(np.float32))

    w = dict(
        in_proj_wT=C(inp["L0_attn_in_proj_w"].T),
        conv_wT=C(np.asarray(inp["L0_attn_conv_w"]).transpose(2, 1, 0)),
        conv_b_re=C(np.asarray(inp["L0_attn_conv_b"]).reshape(16, 128).T),
        dt_wT=C(inp["L0_attn_dt_w"].T),
        dt_b_re=C(np.asarray(inp["L0_attn_dt_b"]).reshape(16, 128).T),
        B_wT=C(inp["L0_attn_B_w"].T),
        C_wT=C(inp["L0_attn_C_w"].T),
        A_re=C(np.asarray(inp["L0_attn_A_log"]).reshape(16, 128, 16).transpose(1, 0, 2).reshape(128, 256)),
        D_re=C(np.asarray(inp["L0_attn_D"]).reshape(16, 128).T),
        out_wT=C(inp["L0_attn_out_w"].T),
        ffn_w=C(inp["L0_ffn_weight"]),
        ffn_b_re=C(np.asarray(inp["L0_ffn_bias"]).reshape(8, 128).T),
        w_qkv=C(inp["L1_attn_w_qkv"]),
        b_qkv_re=C(np.asarray(inp["L1_attn_b_qkv"]).reshape(24, 128).T),
        b_v=C(np.asarray(inp["L1_attn_b_qkv"])[2048:]),
        w_out=C(inp["L1_attn_w_out"]),
        b_out=C(inp["L1_attn_b_out"]),
        W1T=C(inp["L1_ffn_W1"].T),
        b1_re=C(np.asarray(inp["L1_ffn_b1"]).reshape(32, 128).T),
        W2T=C(inp["L1_ffn_W2"].T),
        b2=C(inp["L1_ffn_b2"]),
        Wk=C(inp["L2_attn_Wk"]),
        Wq=C(inp["L2_attn_Wq"]),
        Wo=C(inp["L2_attn_Wo"]),
        bo=C(inp["L2_attn_bo"]),
        norm_scale=C(inp["L2_ffn_norm_scale"]),
        w_router_T=C(inp["L2_ffn_w_router"].T),
        w_upT=C(np.asarray(inp["L2_ffn_w_up"]).transpose(0, 2, 1)),
        w_downT=C(np.asarray(inp["L2_ffn_w_down"]).transpose(0, 2, 1)),
        w_qkvT=C(inp["L3_attn_w_qkv"].T),
        w_projT=C(inp["L3_attn_w_proj"].T),
        q_norm_w=C(inp["L3_attn_q_norm_w"]),
        k_norm_w=C(inp["L3_attn_k_norm_w"]),
        Wr=C(inp["L3_ffn_Wr"]),
        W1=C(inp["L3_ffn_W1"]),
        W2=C(inp["L3_ffn_W2"]),
        idn=idn, idnf=idnf, costab=COS, sintab=SIN, cmask=CM,
    )

    if _FUSED[0] is None:
        p = build_fused()
        _FUSED[0] = p.finalize()
    nc = _FUSED[0]
    maps = [dict(w, x=C(x[b])) for b in range(NB)]
    res = run_bass_kernel_spmd(nc, maps, core_ids=list(range(NB)))
    return np.stack([np.asarray(res.results[b]["x4"]) for b in range(NB)], axis=0)


# version 06 (final)
# speedup vs baseline: 1.2146x; 1.2146x over previous
import numpy as np
from contextlib import ExitStack
import concourse.bass as bass
import concourse.mybir as mybir

F32 = mybir.dt.float32
BF16 = mybir.dt.bfloat16
I32 = mybir.dt.int32
U8 = mybir.dt.uint8
AF = mybir.ActivationFunctionType
ALU = mybir.AluOpType
AX = mybir.AxisListType

T = 4096
D = 1024
NTT = T // 128
NDT = D // 128
ENGS = ("sync", "vector", "scalar", "tensor", "gpsimd")


class Sem:
    _uid = [0]

    def __init__(self, prog, name):
        Sem._uid[0] += 1
        name = f"{name}_{Sem._uid[0]}"
        self.h = prog.es.enter_context(prog.nc.semaphore(name))
        self.n = 0
        self.base = 0
        self.name = name


DETECT_RACES = True

class Prog:
    def __init__(self, name):
        self.nc = bass.Bass("TRN2", target_bir_lowering=False, name=name,
                            detect_race_conditions=DETECT_RACES)
        self.es = ExitStack()
        self.ins = {}
        self.outs = {}
        self.q = {e: [] for e in ENGS}
        self.consts = {}
        self.cst = None
        self._ord = {}
        self._last = {}
        self.io_map = {}
        self._scopes = []
        self._sem_free = []
        self._sem_free_swdge = []
        self._phase_sems = []
        self._pending = {}
        self._seen = {e: {} for e in ENGS}
        self._phb = None
        self._global_sems = False

    def inp(self, name, shape, dtype=F32):
        if name in self.io_map:
            return self.io_map[name]
        if name in self.ins:
            return self.ins[name]
        ap = self.nc.dram_tensor(name, list(shape), dtype, kind="ExternalInput").ap()
        self.ins[name] = ap
        return ap

    def out(self, name, shape, dtype=F32):
        if name in self.io_map:
            return self.io_map[name]
        ap = self.nc.dram_tensor(name, list(shape), dtype, kind="ExternalOutput").ap()
        self.outs[name] = ap
        return ap

    def sb(self, name, shape, dtype):
        if self._scopes:
            name = f"p{self._phase_no}_{name}"
            return self._scopes[-1].enter_context(
                self.nc.sbuf_tensor(name, list(shape), dtype))
        return self.es.enter_context(self.nc.sbuf_tensor(name, list(shape), dtype))

    def ps(self, name, shape, dtype=F32):
        if self._scopes:
            name = f"p{self._phase_no}_{name}"
            return self._scopes[-1].enter_context(
                self.nc.psum_tensor(name, list(shape), dtype))
        return self.es.enter_context(self.nc.psum_tensor(name, list(shape), dtype))

    def sem(self, name, _global=False):
        if self._scopes and not _global:
            if self._sem_free:
                h, cum = self._sem_free.pop()
                s = Sem.__new__(Sem)
                s.h, s.n, s.base, s.name = h, 0, cum, name
            else:
                s = Sem(self, name + "_pool")
            self._phase_sems.append(s)
            return s
        return Sem(self, name)

    def begin_phase(self):
        self._phase_no = getattr(self, "_phase_no", -1) + 1
        self._scopes.append(ExitStack())

    def end_phase(self):
        if self._phb is None:
            self._phb = Sem(self, "phb")
        waits = [(s, v) for (s, v) in self._last.values()]
        self.do("sync", lambda e: e.nop(), wait=waits, inc=self._phb, chain=False)
        tgt = self._phb.n
        self._last = {}
        self._ord = {}
        for eng in ENGS:
            self._pending.setdefault(eng, []).append((self._phb, tgt))
        for s in self._phase_sems:
            if getattr(s, "swdge", False):
                self._sem_free_swdge.append((s.h, s.base + s.n))
            else:
                self._sem_free.append((s.h, s.base + s.n))
        self._phase_sems = []
        self._scopes.pop().close()

    def cbias(self, value, dtype=F32):
        key = (dtype, float(value))
        if key not in self.consts:
            t = self.es.enter_context(self.nc.sbuf_tensor(
                f"c_{len(self.consts)}", [128, 1], dtype))
            self.consts[key] = t
            self.nc.const_aps.register(dtype, float(value), t[:])
        if self.cst is None:
            self.cst = self.sem("cst", _global=True)
            self.cst.n = 1
        return self.consts[key][:]

    def cwait(self):
        return [(self.cst, 1)] if self.cst is not None else []

    def do(self, eng, fn, wait=(), inc=None, inc_k=1, chain=True):
        if self._pending.get(eng):
            wait = list(wait) + self._pending.pop(eng)
        seen = self._seen[eng]
        w = []
        for (s, v) in wait:
            tv = s.base + v
            hid = id(s.h)
            if tv <= 0 or seen.get(hid, -1) >= tv:
                continue
            seen[hid] = tv
            w.append((s.h, tv))
        if eng in ("vector", "scalar", "gpsimd") and chain:
            if eng in self._last:
                ls, lv = self._last[eng]
                tv = ls.base + lv
                hid = id(ls.h)
                if seen.get(hid, -1) < tv:
                    seen[hid] = tv
                    w.append((ls.h, tv))
            if inc is None:
                if eng not in self._ord:
                    self._ord[eng] = self.sem("ord_" + eng)
                inc = self._ord[eng]
        if inc is not None:
            inc.n += inc_k
            tgt = (inc.h, inc_k)
            if eng in ("vector", "scalar", "gpsimd") and chain:
                self._last[eng] = (inc, inc.n)
        else:
            tgt = None
        self.q[eng].append((w, fn, tgt))

    def dma(self, eng, dst, src, wait=(), inc=None, **kw):
        if eng == "gpsimd" and inc is not None and self._scopes:
            self._make_swdge(inc)
        self.do(eng, lambda e: e.dma_start(out=dst, in_=src, **kw),
                wait=wait, inc=inc, inc_k=16, chain=False)

    def _make_swdge(self, s):
        if getattr(s, "swdge", False):
            return
        assert s.n == 0, f"sem {s.name} got non-swdge incs before gpsimd dma"
        if s in self._phase_sems:
            self._sem_free.append((s.h, s.base))
        if self._sem_free_swdge:
            h, cum = self._sem_free_swdge.pop()
            s.h, s.base = h, cum
        else:
            tmp = Sem(self, s.name + "_sw")
            s.h, s.base = tmp.h, 0
        s.swdge = True

    def finalize(self):
        if self.consts:
            items = list(self.consts.items())
            pre = []
            for i, ((dt_, v), t) in enumerate(items):
                inc = (self.cst.h, 1) if i == len(items) - 1 else None
                pre.append(([], (lambda t, v: lambda e: e.memset(t[:], v))(t, v), inc))
            self.q["gpsimd"] = pre + self.q["gpsimd"]
        blk = self.es.enter_context(self.nc.Block())
        for e in ENGS:
            items = self.q[e]
            if not items:
                continue

            def mk(items):
                def body(eng):
                    for w, fn, tgt in items:
                        for (h, v) in w:
                            eng.wait_ge(h, v)
                        inst = fn(eng)
                        if tgt is not None:
                            assert inst is not None, "inc on op that returned None"
                            inst.then_inc(*tgt)
                return body
            getattr(blk, e)(mk(items))
        self.es.close()
        return self.nc


def ceil_div(a, b):
    return -(-a) // b


def mm_sweep(p, name, nk, nots, nfs, lhsT_ap, rhs_ap, psum_banks, evac,
             pe_sem, wait_first=(), evac_done_sem=None, pwidth=128, fwidth=512):
    nb = len(psum_banks)
    gi = 0
    for ot in range(nots):
        for f in range(nfs):
            bank = psum_banks[gi % nb]
            for k in range(nk):
                w = list(wait_first(ot, f, k) if callable(wait_first) else wait_first) if k == 0 else []
                if k == 0 and gi >= nb and evac_done_sem is not None:
                    w.append((evac_done_sem, gi - nb + 1))
                inc = pe_sem if k == nk - 1 else None
                p.do("tensor",
                     (lambda l, r, b, s0, s1: lambda e: e.matmul(
                         b, l, r, start=s0, stop=s1))(
                         lhsT_ap(k, ot), rhs_ap(k, f), bank[:, 0:fwidth],
                         k == 0, k == nk - 1),
                     wait=w, inc=inc)
            evac(ot, f, bank, pe_sem.n)
            gi += 1


def build_l0a(p=None):
    if p is None:
        p = Prog("l0a")
    x = p.inp("x", [T, D])
    wT = p.inp("in_proj_wT", [D, 4096])
    idn_in = p.inp("idn", [128, 128], BF16)
    xn_tok = p.out("xn_tok", [T, D], F32)
    xc_d = p.out("xc_d", [2048, T], BF16)
    zsil_d = p.out("zsil_d", [2048, T], BF16)

    xbuf = [p.sb(f"xbuf{i}", [128, D], F32) for i in range(2)]
    sqscr = [p.sb(f"sqscr{i}", [128, D], F32) for i in range(2)]
    s1 = p.sb("s1", [128, NTT], F32)
    s2 = p.sb("s2", [128, NTT], F32)
    mean = p.sb("mean", [128, NTT], F32)
    var = p.sb("var", [128, NTT], F32)
    sd = p.sb("sd", [128, NTT], F32)
    rstd = p.sb("rstd", [128, NTT], F32)
    xnt32 = [p.sb(f"xnt32{i}", [128, D], F32) for i in range(2)]
    xnt = [p.sb(f"xnt{i}", [128, D], BF16) for i in range(2)]
    xn_d = p.sb("xn_d", [128, NDT * T], BF16)
    idn = p.sb("idn_s", [128, 128], BF16)
    wbuf = [p.sb(f"wbuf{i}", [128, 8 * 128], F32) for i in range(2)]
    wbb = [p.sb(f"wbb{i}", [128, 8 * 128], BF16) for i in range(2)]
    stg = [p.sb(f"stg{i}", [128, 512], BF16) for i in range(4)]
    ps_t = [p.ps(f"ps_t{i}", [128, 1024], BF16) for i in range(2)]
    ps_m = [p.ps(f"ps_m{i}", [128, 512], F32) for i in range(4)]

    ldx = [p.sem(f"ldx{i}") for i in range(2)]
    li = p.sem("li")
    lw = [p.sem(f"lw{i}") for i in range(2)]
    vst = p.sem("vst"); sst = p.sem("sst"); vrs = p.sem("vrs")
    srs = p.sem("srs"); rrs = p.sem("rrs")
    xnv = p.sem("xnv"); pet = p.sem("pet"); evq = p.sem("evq")
    xns = [p.sem(f"xns{i}") for i in range(2)]
    gct = p.sem("gct"); pem = p.sem("pem"); evm = p.sem("evm")
    stm = [p.sem(f"stm{i}") for i in range(4)]

    p.dma("sync", idn[:], idn_in, inc=li)

    ldx_after = []
    for i in range(NTT):
        w = [(vst, i - 1), (sst, i - 1)] if i >= 2 else []
        p.dma("sync", _ap(xbuf[i % 2]), x[i * 128:(i + 1) * 128, :], wait=w, inc=ldx[i % 2])
        ldx_after.append(ldx[i % 2].n)
    for i in range(NTT):
        p.do("vector", (lambda i: lambda e: e.tensor_reduce(
            s1[:, i:i + 1], _ap(xbuf[i % 2]), AX.X, ALU.add))(i),
            wait=[(ldx[i % 2], ldx_after[i])], inc=vst)
        p.do("scalar", (lambda i: lambda e: e.activation(
            _ap(sqscr[i % 2]), _ap(xbuf[i % 2]), AF.Square, accum_out=s2[:, i:i + 1]))(i),
            wait=[(ldx[i % 2], ldx_after[i])], inc=sst)
    stc = p.sem("stc")
    p.do("vector", lambda e: e.tensor_scalar(mean[:], s1[:], 1.0 / D, None, ALU.mult),
         wait=[(sst, NTT), (vst, NTT)], inc=stc)
    p.do("vector", lambda e: e.tensor_tensor(var[:], mean[:], mean[:], ALU.mult),
         wait=[(stc, 1)], inc=stc)
    p.do("vector", lambda e: e.scalar_tensor_tensor(
        var[:], s2[:], 1.0 / D, var[:], ALU.mult, ALU.subtract),
        wait=[(stc, 2)], inc=vrs)
    eps_ap = p.cbias(1e-5)
    p.do("scalar", lambda e: e.activation(sd[:], var[:], AF.Sqrt, bias=eps_ap, scale=1.0),
         wait=[(vrs, 1)] + p.cwait(), inc=srs)
    p.do("vector", lambda e: e.reciprocal(rstd[:], sd[:]), wait=[(srs, 1)], inc=rrs)

    xnv_after = []; xns_after = []; pet_after = []; evq_after = []
    ldx2_after = []
    for i in range(NTT):
        w = [(rrs, 1)] if i < 2 else [(xnv, i - 1)]
        p.dma("sync", _ap(xbuf[i % 2]), x[i * 128:(i + 1) * 128, :], wait=w, inc=ldx[i % 2])
        ldx2_after.append(ldx[i % 2].n)
    for i in range(NTT):
        w = [(ldx[i % 2], ldx2_after[i]), (rrs, 1), (stc, 1)]
        if i >= 2:
            w += [(xns[i % 2], xns_after[i - 2]), (pet, pet_after[i - 2])]
        p.do("vector", (lambda i: lambda e: e.tensor_scalar(
            xnt32[i % 2][:], xbuf[i % 2][:], mean[:, i:i + 1], rstd[:, i:i + 1],
            ALU.subtract, ALU.mult))(i), wait=w)
        p.do("vector", (lambda i: lambda e: e.tensor_copy(
            xnt[i % 2][:], xnt32[i % 2][:]))(i), inc=xnv)
        xnv_after.append(xnv.n)
        p.dma("gpsimd", xn_tok[i * 128:(i + 1) * 128, :], xnt32[i % 2][:],
              wait=[(xnv, xnv_after[i])], inc=xns[i % 2])
        xns_after.append(xns[i % 2].n)
        for j in range(NDT):
            w = []
            if j == 0:
                w = [(xnv, xnv_after[i]), (li, 16)]
                if i >= 2:
                    w.append((evq, evq_after[i - 2]))
            p.do("tensor", (lambda i, j: lambda e: e.transpose(
                ps_t[i % 2][:, j * 128:(j + 1) * 128], xnt[i % 2][:, j * 128:(j + 1) * 128],
                idn[:]))(i, j), wait=w, inc=(pet if j == NDT - 1 else None))
        pet_after.append(pet.n)
        p.do("vector", (lambda i: lambda e: e.tensor_copy(
            xn_d[:].rearrange("p (dt t) -> p dt t", t=T)[:, :, i * 128:(i + 1) * 128],
            ps_t[i % 2][:].rearrange("p (dt c) -> p dt c", c=128)))(i),
            wait=[(pet, pet_after[i])], inc=evq)
        evq_after.append(evq.n)

    lw_after = {}; gct_after = {}; pem_after = {}; evm_after = {}; stm_after = {}
    xnd = xn_d[:].rearrange("p (dt t) -> p dt t", t=T)
    g = 0
    for ot in range(32):
        w = [(gct, ot - 1)] if ot >= 2 else []
        p.dma("sync", wbuf[ot % 2][:].rearrange("p (kt o) -> p kt o", o=128),
              wT[:, ot * 128:(ot + 1) * 128].rearrange("(kt p) o -> p kt o", p=128),
              wait=w, inc=lw[ot % 2])
        lw_after[ot] = lw[ot % 2].n
    for ot in range(32):
        w = [(lw[ot % 2], lw_after[ot])]
        if ot >= 2:
            w.append((pem, 8 * (ot - 1)))
        p.do("gpsimd", (lambda ot: lambda e: e.tensor_copy(wbb[ot % 2][:], wbuf[ot % 2][:]))(ot),
             wait=w, inc=gct)
        gct_after[ot] = gct.n
        for f in range(8):
            bank = ps_m[g % 4]
            for k in range(8):
                w = []
                if k == 0:
                    w = [(gct, gct_after[ot])]
                    if g == 0:
                        w.append((evq, NTT))
                    if g >= 4:
                        w.append((evm, evm_after[g - 4]))
                p.do("tensor", (lambda ot, f, k, bank, s0, s1_: lambda e: e.matmul(
                    bank[:], wbb[ot % 2][:, k * 128:(k + 1) * 128],
                    xnd[:, k, f * 512:(f + 1) * 512], start=s0, stop=s1_))(
                    ot, f, k, bank, k == 0, k == 7),
                    wait=w, inc=(pem if k == 7 else None))
            pem_after[(ot, f)] = pem.n
            w = [(pem, pem_after[(ot, f)])]
            if g >= 4:
                w.append((stm[g % 4], stm_after[g - 4]))
            func = AF.Copy if ot < 16 else AF.Silu
            p.do("scalar", (lambda bank, g, func: lambda e: e.activation(
                stg[g % 4][:], bank[:], func))(bank, g, func), wait=w, inc=evm)
            evm_after[g] = evm.n
            if ot < 16:
                dst = xc_d[ot * 128:(ot + 1) * 128, f * 512:(f + 1) * 512]
            else:
                dst = zsil_d[(ot - 16) * 128:(ot - 16 + 1) * 128, f * 512:(f + 1) * 512]
            p.dma("gpsimd", dst, stg[g % 4][:], wait=[(evm, evm_after[g])], inc=stm[g % 4])
            stm_after[g] = stm[g % 4].n
            g += 1

    p.do("sync", lambda e: e.nop(),
         wait=[(s, s.n) for s in stm] + [(s, s.n) for s in xns])
    return p


def build_l0b1(p=None):
    if p is None:
        p = Prog("l0b1")
    xc = p.inp("xc_d", [2048, T], BF16)
    cw = p.inp("conv_wT", [4, 2048, 2048])
    cb = p.inp("conv_b_re", [128, 16])
    u_d = p.out("u_d", [2048, T], BF16)

    XW = 4104
    xcb = p.sb("xcb", [128, 16 * XW], BF16)
    slab = [p.sb(f"slab{i}", [128, 4 * 128], F32) for i in range(4)]
    wcb = [p.sb(f"wcb{i}", [128, 64 * 128], BF16) for i in range(2)]
    cbs = p.sb("cbs", [128, 16], F32)
    stg = [p.sb(f"stg{i}", [128, 512], BF16) for i in range(4)]
    ps_m = [p.ps(f"ps_m{i}", [128, 512], F32) for i in range(4)]

    ldc = p.sem("ldc"); ldb = p.sem("ldb")
    ldw = [p.sem(f"ldw{i}") for i in range(4)]
    cst = p.sem("cstw"); pem = p.sem("pem"); evm = p.sem("evm")
    stm = [p.sem(f"stm{i}") for i in range(4)]
    msz = p.sem("msz")

    p.do("vector", lambda e: e.memset(
        xcb[:].rearrange("p (t w) -> p t w", w=XW)[:, :, 0:8], 0.0), inc=msz)
    for it in range(16):
        p.dma("sync", xcb[:, it * XW + 8: it * XW + 8 + T],
              xc[it * 128:(it + 1) * 128, :], inc=ldc)
    p.dma("sync", cbs[:], cb, inc=ldb)

    ldw_after = {}; cst_after = {}; pem_after = {}; evm_after = {}; stm_after = {}
    g = 0
    si = 0
    for oct in range(16):
        for it in range(16):
            s = si % 4
            w = []
            if si >= 4:
                w.append((cst, cst_after[si - 4]))
            p.dma("sync", slab[s][:].rearrange("p (k c) -> p k c", c=128),
                  cw[:, it * 128:(it + 1) * 128, oct * 128:(oct + 1) * 128]
                  .rearrange("k p c -> p k c"), wait=w, inc=ldw[s])
            ldw_after[si] = ldw[s].n
            w = [(ldw[s], ldw_after[si])]
            if oct >= 2 and it == 0:
                w.append((pem, 8 * (oct - 1)))
            p.do("gpsimd", (lambda oct, it, s: lambda e: e.tensor_copy(
                wcb[oct % 2][:, (it * 4) * 128:(it * 4 + 4) * 128], slab[s][:]))(oct, it, s),
                wait=w, inc=cst)
            cst_after[si] = cst.n
            si += 1
        for f in range(8):
            bank = ps_m[g % 4]
            ki = 0
            for it in range(16):
                for k in range(4):
                    w = []
                    if ki == 0:
                        w = [(cst, cst_after[oct * 16 + 15]), (msz, 1)]
                        if g == 0:
                            w.append((ldc, 16 * 16))
                        if g >= 4:
                            w.append((evm, evm_after[g - 4]))
                    src = xcb[:, it * XW + 8 + (k - 3) + f * 512: it * XW + 8 + (k - 3) + f * 512 + 512]
                    p.do("tensor", (lambda lt, rt, bank, s0, s1_: lambda e: e.matmul(
                        bank[:], lt, rt, start=s0, stop=s1_))(
                        wcb[oct % 2][:, (it * 4 + k) * 128:(it * 4 + k + 1) * 128],
                        src, bank, ki == 0, ki == 63),
                        wait=w, inc=(pem if ki == 63 else None))
                    ki += 1
            pem_after[g] = pem.n
            w = [(pem, pem_after[g]), (ldb, 16)]
            if g >= 4:
                w.append((stm[g % 4], stm_after[g - 4]))
            p.do("vector", (lambda bank, g, oct: lambda e: e.tensor_scalar(
                stg[g % 4][:], bank[:], cbs[:, oct:oct + 1], None, ALU.add))(bank, g, oct),
                wait=w, inc=evm)
            evm_after[g] = evm.n
            p.dma("gpsimd", u_d[oct * 128:(oct + 1) * 128, f * 512:(f + 1) * 512],
                  stg[g % 4][:], wait=[(evm, evm_after[g])], inc=stm[g % 4])
            stm_after[g] = stm[g % 4].n
            g += 1
    p.do("sync", lambda e: e.nop(), wait=[(s, s.n) for s in stm])
    return p


def build_l0b2(p=None):
    if p is None:
        p = Prog("l0b2")
    u = p.inp("u_d", [2048, T], BF16)
    dwT = p.inp("dt_wT", [2048, 2048])
    dbr = p.inp("dt_b_re", [128, 16])
    bwT = p.inp("B_wT", [2048, 16])
    cwT = p.inp("C_wT", [2048, 16])
    dt_d = p.out("dt_d", [2048, T], BF16)
    dtu_d = p.out("dtu_d", [2048, T], BF16)
    bm = p.out("Bm", [16, T])
    cm = p.out("Cm", [16, T])

    ub = p.sb("ub", [128, 16 * T], BF16)
    dwf = [p.sb(f"dwf{i}", [128, 16 * 128], F32) for i in range(2)]
    dwb = [p.sb(f"dwb{i}", [128, 16 * 128], BF16) for i in range(2)]
    dbs = p.sb("dbs", [128, 16], F32)
    bcf = p.sb("bcf", [128, 512], F32)
    bcb = p.sb("bcb", [128, 512], BF16)
    scr = [p.sb(f"scr{i}", [128, 512], F32) for i in range(2)]
    stg = [p.sb(f"stg{i}", [128, 512], BF16) for i in range(4)]
    stg2 = [p.sb(f"stg2{i}", [128, 512], BF16) for i in range(4)]
    stgb = [p.sb(f"stgb{i}", [16, 512], F32) for i in range(4)]
    ps_m = [p.ps(f"ps_m{i}", [128, 512], F32) for i in range(4)]
    ps_b = [p.ps(f"ps_b{i}", [16, 512], F32) for i in range(2)]

    ldu = p.sem("ldu"); ldb = p.sem("ldb")
    ldw = [p.sem(f"ldw{i}") for i in range(2)]
    cstw = p.sem("cstw2"); pem = p.sem("pem"); evs = p.sem("evs"); evv = p.sem("evv")
    evu = p.sem("evu")
    stm = [p.sem(f"stm{i}") for i in range(4)]
    stm2 = [p.sem(f"stm2{i}") for i in range(4)]
    stmb = [p.sem(f"stmb{i}") for i in range(4)]
    peb = p.sem("peb"); evb = p.sem("evb")

    for it in range(16):
        p.dma("sync", ub[:, it * T:(it + 1) * T], u[it * 128:(it + 1) * 128, :], inc=ldu)
    p.dma("sync", dbs[:], dbr, inc=ldb)
    p.dma("sync", bcf[:, 0:256].rearrange("p (t n) -> p t n", n=16),
          bwT.rearrange("(t p) n -> p t n", p=128), inc=ldb)
    p.dma("sync", bcf[:, 256:512].rearrange("p (t n) -> p t n", n=16),
          cwT.rearrange("(t p) n -> p t n", p=128), inc=ldb)
    p.do("gpsimd", lambda e: e.tensor_copy(bcb[:], bcf[:]), wait=[(ldb, 48)], inc=cstw)

    ubv = ub[:].rearrange("p (it t) -> p it t", t=T)
    ldw_after = {}; pem_after = {}; evs_after = {}; evv_after = {}; evu_after = {}
    stm_after = {}; stm2_after = {}
    g = 0
    for ot in range(16):
        s = ot % 2
        w = []
        if ot >= 2:
            w.append((cstw, 1 + (ot - 1)))
        p.dma("sync", dwf[s][:].rearrange("p (t c) -> p t c", c=128),
              dwT[:, ot * 128:(ot + 1) * 128].rearrange("(t p) c -> p t c", p=128),
              wait=w, inc=ldw[s])
        ldw_after[ot] = ldw[s].n
        w = [(ldw[s], ldw_after[ot])]
        if ot >= 2:
            w.append((pem, 8 * (ot - 1)))
        p.do("gpsimd", (lambda s: lambda e: e.tensor_copy(dwb[s][:], dwf[s][:]))(s),
             wait=w, inc=cstw)
        for f in range(8):
            bank = ps_m[g % 4]
            for k in range(16):
                w = []
                if k == 0:
                    w = [(cstw, 1 + (ot + 1))]
                    if g == 0:
                        w.append((ldu, 16 * 16))
                    if g >= 4:
                        w.append((evs, evs_after[g - 4]))
                p.do("tensor", (lambda l, r, bank, s0, s1_: lambda e: e.matmul(
                    bank[:], l, r, start=s0, stop=s1_))(
                    dwb[s][:, k * 128:(k + 1) * 128], ubv[:, k, f * 512:(f + 1) * 512],
                    bank, k == 0, k == 15),
                    wait=w, inc=(pem if k == 15 else None))
            pem_after[g] = pem.n
            w = [(pem, pem_after[g]), (ldb, 48)]
            if g >= 2:
                w.append((evv, evv_after[g - 2]))
            p.do("scalar", (lambda bank, g, ot: lambda e: e.activation(
                scr[g % 2][:], bank[:], AF.Sigmoid, bias=dbs[:, ot:ot + 1], scale=1.0))(bank, g, ot),
                wait=w, inc=evs)
            evs_after[g] = evs.n
            w = [(evs, evs_after[g])]
            if g >= 4:
                w.append((stm[g % 4], stm_after[g - 4]))
            p.do("vector", (lambda g: lambda e: e.tensor_scalar(
                stg[g % 4][:], scr[g % 2][:], 1e-4, None, ALU.max))(g),
                wait=w, inc=evv)
            evv_after[g] = evv.n
            p.dma("gpsimd", dt_d[ot * 128:(ot + 1) * 128, f * 512:(f + 1) * 512],
                  stg[g % 4][:], wait=[(evv, evv_after[g])], inc=stm[g % 4])
            stm_after[g] = stm[g % 4].n
            w = [(evv, evv_after[g])]
            if g >= 4:
                w.append((stm2[g % 4], stm2_after[g - 4]))
            p.do("vector", (lambda g, ot, f: lambda e: e.tensor_tensor(
                stg2[g % 4][:], stg[g % 4][:], ubv[:, ot, f * 512:(f + 1) * 512],
                ALU.mult))(g, ot, f), wait=w, inc=evu)
            evu_after[g] = evu.n
            p.dma("gpsimd", dtu_d[ot * 128:(ot + 1) * 128, f * 512:(f + 1) * 512],
                  stg2[g % 4][:], wait=[(evu, evu_after[g])], inc=stm2[g % 4])
            stm2_after[g] = stm2[g % 4].n
            g += 1
    stmb_after = {}; evb_after = {}; peb_after = {}
    gb = 0
    for bi, (dst, coff) in enumerate([(bm, 0), (cm, 16)]):
        for f in range(8):
            bank = ps_b[gb % 2]
            for k in range(16):
                w = []
                if k == 0:
                    w = [(cstw, 1), (ldu, 16 * 16)]
                    if gb >= 2:
                        w.append((evb, evb_after[gb - 2]))
                p.do("tensor", (lambda l, r, bank, s0, s1_: lambda e: e.matmul(
                    bank[:], l, r, start=s0, stop=s1_))(
                    bcb[:, bi * 256 + k * 16: bi * 256 + (k + 1) * 16],
                    ubv[:, k, f * 512:(f + 1) * 512], bank, k == 0, k == 15),
                    wait=w, inc=(peb if k == 15 else None))
            peb_after[gb] = peb.n
            w = [(peb, peb_after[gb])]
            if gb >= 4:
                w.append((stmb[gb % 4], stmb_after[gb - 4]))
            p.do("vector", (lambda bank, gb: lambda e: e.tensor_copy(
                stgb[gb % 4][:], bank[:]))(bank, gb), wait=w, inc=evb)
            evb_after[gb] = evb.n
            p.dma("gpsimd", dst[:, f * 512:(f + 1) * 512], stgb[gb % 4][:],
                  wait=[(evb, evb_after[gb])], inc=stmb[gb % 4])
            stmb_after[gb] = stmb[gb % 4].n
            gb += 1
    p.do("sync", lambda e: e.nop(),
         wait=[(s_, s_.n) for s_ in stm + stm2 + stmb])
    return p


def build_l0c(p=None):
    if p is None:
        p = Prog("l0c")
    dt = p.inp("dt_d", [2048, T], BF16)
    dtu = p.inp("dtu_d", [2048, T], BF16)
    u = p.inp("u_d", [2048, T], BF16)
    zs = p.inp("zsil_d", [2048, T], BF16)
    bm = p.inp("Bm", [16, T])
    cm = p.inp("Cm", [16, T])
    a_re = p.inp("A_re", [128, 256])
    d_re = p.inp("D_re", [128, 16])
    idn_in = p.inp("idn", [128, 128], BF16)
    y_d = p.out("y_d", [2048, T], BF16)

    H = 1024
    NQ = T // H
    ares = p.sb("ares", [128, 256], F32)
    aneg = p.sb("aneg", [128, 256], F32)
    dres = p.sb("dres", [128, 16], F32)
    idn = p.sb("idn_s", [128, 128], BF16)
    brep = p.sb("brep", [128, 16 * H], BF16)
    crep = p.sb("crep", [128, 16 * H], BF16)
    tin = [p.sb(f"tin{i}", [128, 4 * H], BF16) for i in range(2)]
    dA = [p.sb(f"dA{i}", [128, H], F32) for i in range(2)]
    dBe = [p.sb(f"dBe{i}", [128, H], BF16) for i in range(2)]
    dBe2 = [p.sb(f"dBe2{i}", [128, H], BF16) for i in range(2)]
    stt_ = [p.sb(f"stt{i}", [128, H], BF16) for i in range(2)]
    ytmb = [p.sb(f"ytmp{i}", [128, H], BF16) for i in range(4)]
    yacc = [p.sb(f"yacc{i}", [128, H], F32) for i in range(2)]
    carry = p.sb("carry", [128, 256], F32)
    stg = [p.sb(f"stg{i}", [128, H], BF16) for i in range(2)]
    ps_y = [p.ps(f"ps_y{i}", [128, H], F32) for i in range(2)]

    lda = p.sem("lda")
    ldr = p.sem("ldr")
    ldt = [p.sem(f"ldt{i}") for i in range(2)]
    prep = p.sem("prep")
    sda = p.sem("sda"); gmul = p.sem("gmul"); sadd = p.sem("sadd")
    scar = p.sem("scar")
    vsc = p.sem("vsc"); vsn = p.sem("vsn"); vtl = p.sem("vtl"); vfin = p.sem("vfin")
    pey = p.sem("pey")
    stm = [p.sem(f"stm{i}") for i in range(2)]

    p.dma("sync", ares[:], a_re, inc=lda)
    p.dma("sync", dres[:], d_re, inc=lda)
    p.dma("sync", idn[:], idn_in, inc=lda)
    lda_n = lda.n
    p.do("scalar", lambda e: e.activation(aneg[:], ares[:], AF.Exp), wait=[(lda, lda_n)], inc=sda)
    p.do("vector", lambda e: e.tensor_scalar(aneg[:], aneg[:], -1.0, None, ALU.mult),
         wait=[(sda, 1)], inc=prep)

    ldt_after = {}; ldr_after = {}
    vsc_after = {}; sda_after = {}; gmul_after = {}; sadd_after = {}
    vfin_after = {}; stm_after = {}; carry_w = {}; vsn_after = {}
    scar_after = {}; pey_after = {}; vtl_after = {}
    si = 0
    ti = 0
    pend_carry = None
    q_gmul = {}; q_vsc = {}

    def flush_carry():
        nonlocal pend_carry
        if pend_carry is None:
            return
        psi, pr, pcol = pend_carry
        p.do("scalar", (lambda pr, pcol: lambda e: e.activation(
            carry[:, pcol:pcol + 1], stt_[pr][:, H - 1:H], AF.Copy))(pr, pcol),
            wait=[(vsn, vsn_after[psi])], inc=scar)
        carry_w[pcol] = scar.n
        scar_after[psi] = scar.n
        pend_carry = None

    for q in range(NQ):
        for n in range(16):
            w = []
            if q > 0 and n == 0:
                w = [(gmul, q_gmul[q - 1]), (vsc, q_vsc[q - 1])]
            p.dma("gpsimd", brep[:, n * H:(n + 1) * H],
                  bm[n:n + 1, q * H:(q + 1) * H].partition_broadcast(128).squeeze(1),
                  wait=w, inc=ldr)
            p.dma("gpsimd", crep[:, n * H:(n + 1) * H],
                  cm[n:n + 1, q * H:(q + 1) * H].partition_broadcast(128).squeeze(1),
                  inc=ldr)
        ldr_after[q] = ldr.n
        for ct in range(16):
            s = ti % 2
            w = []
            if ti >= 2:
                w.append((vfin, vfin_after[ti - 2]))
            for gi, srcten in enumerate([dt, dtu, u, zs]):
                p.dma("sync", tin[s][:, gi * H:(gi + 1) * H],
                      srcten[ct * 128:(ct + 1) * 128, q * H:(q + 1) * H],
                      wait=(w if gi == 0 else []), inc=ldt[s])
            ldt_after[ti] = ldt[s].n
            for n in range(16):
                r = si % 2
                col = ct * 16 + n
                w = [(ldt[s], ldt_after[ti]), (prep, 1)]
                if si >= 2:
                    w.append((vsn, vsn_after[si - 2]))
                p.do("scalar", (lambda r, s, col: lambda e: e.activation(
                    dA[r][:], tin[s][:, 0:H], AF.Exp, scale=aneg[:, col:col + 1]))(r, s, col),
                    wait=w, inc=sda)
                sda_after[si] = sda.n
                w = [(ldt[s], ldt_after[ti]), (ldr, ldr_after[q])]
                if si >= 2:
                    w.append((sadd, sadd_after[si - 2]))
                p.do("gpsimd", (lambda r, s, n: lambda e: e.tensor_tensor(
                    dBe[r][:], tin[s][:, H:2 * H], brep[:, n * H:(n + 1) * H], ALU.mult))(r, s, n),
                    wait=w, inc=gmul)
                gmul_after[si] = gmul.n
                w = [(gmul, gmul_after[si])] + p.cwait()
                if si >= 2:
                    w.append((vsn, vsn_after[si - 2]))
                p.do("scalar", (lambda r: lambda e: e.activation(
                    dBe2[r][:], dBe[r][:], AF.Copy, bias=1e-6))(r),
                    wait=w, inc=sadd)
                sadd_after[si] = sadd.n
                flush_carry()
                init = 0.0 if q == 0 else carry[:, col:col + 1]
                w = [(sda, sda_after[si]), (sadd, sadd_after[si])]
                if q > 0:
                    w.append((scar, carry_w[col]))
                if si >= 2 and (si - 2) in scar_after:
                    w.append((scar, scar_after[si - 2]))
                p.do("vector", (lambda r, init: lambda e: e.tensor_tensor_scan(
                    stt_[r][:], dA[r][:], dBe2[r][:], init, ALU.mult, ALU.add))(r, init),
                    wait=w, inc=vsn)
                vsn_after[si] = vsn.n
                if q < NQ - 1:
                    pend_carry = (si, r, col)
                yslot = si % 4
                w = [(ldr, ldr_after[q])]
                if si >= 4:
                    w.append((pey, pey_after[si - 4]))
                p.do("vector", (lambda r, n, yslot: lambda e: e.tensor_tensor(
                    ytmb[yslot][:], stt_[r][:], crep[:, n * H:(n + 1) * H], ALU.mult))(r, n, yslot),
                    wait=w, inc=vsc)
                vsc_after[si] = vsc.n
                for h in range(2):
                    w = []
                    if h == 0:
                        w = [(vsc, vsc_after[si]), (lda, lda_n)]
                        if n == 0 and ti >= 2:
                            w.append((vtl, vtl_after[ti - 2]))
                    p.do("tensor", (lambda yslot, h, ti, n: lambda e: e.matmul(
                        ps_y[ti % 2][:, h * 512:(h + 1) * 512], idn[:],
                        ytmb[yslot][:, h * 512:(h + 1) * 512],
                        start=(n == 0), stop=(n == 15)))(yslot, h, ti, n),
                        wait=w, inc=(pey if h == 1 else None))
                pey_after[si] = pey.n
                si += 1
            flush_carry()
            s2 = ti % 2
            p.do("vector", (lambda s, ct, s2, ti: lambda e: e.scalar_tensor_tensor(
                yacc[s2][:], tin[s][:, 2 * H:3 * H], dres[:, ct:ct + 1],
                ps_y[ti % 2][:], ALU.mult, ALU.add))(s, ct, s2, ti),
                wait=[(pey, pey_after[si - 1]), (lda, lda_n)], inc=vtl)
            vtl_after[ti] = vtl.n
            w = []
            if ti >= 2:
                w.append((stm[ti % 2], stm_after[ti - 2]))
            p.do("vector", (lambda s, s2, ti: lambda e: e.tensor_tensor(
                stg[ti % 2][:], yacc[s2][:], tin[s][:, 3 * H:4 * H], ALU.mult))(s, s2, ti),
                wait=w, inc=vfin)
            vfin_after[ti] = vfin.n
            p.dma("gpsimd", y_d[ct * 128:(ct + 1) * 128, q * H:(q + 1) * H],
                  stg[ti % 2][:], wait=[(vfin, vfin_after[ti])], inc=stm[ti % 2])
            stm_after[ti] = stm[ti % 2].n
            ti += 1
        q_gmul[q] = gmul.n
        q_vsc[q] = vsc.n
    p.do("sync", lambda e: e.nop(), wait=[(s_, s_.n) for s_ in stm])
    return p


def emit_fp6_quant(p, src_ap, dst_ap, scr, waits, inc_sem, eng="vector"):
    W = src_ap.shape[-1]
    NB = W // 32
    qv, t1, t2, i32t, rf = scr["qv"], scr["t1"], scr["t2"], scr["i32"], scr["rf"]
    amax, scale, rscale = scr["amax"], scr["scale"], scr["rscale"]
    v = lambda fn, w=(), inc=None: p.do(eng, fn, wait=w, inc=inc)
    v(lambda e: e.tensor_reduce(amax[:, 0:NB], src_ap.rearrange("p (b k) -> p b k", k=32),
                                AX.X, ALU.max, apply_absolute_value=True), w=list(waits))
    v(lambda e: e.tensor_scalar(scale[:, 0:NB], amax[:, 0:NB], 1.0 / 7.5, 1e-30,
                                ALU.mult, ALU.max))
    v(lambda e: e.reciprocal(rscale[:, 0:NB], scale[:, 0:NB]))
    bc = lambda t: t[:, 0:NB].unsqueeze(2).broadcast_to((128, NB, 32))
    sv = src_ap.rearrange("p (b k) -> p b k", k=32)
    v(lambda e: e.tensor_tensor(qv[:].rearrange("p (b k) -> p b k", k=32), sv, bc(rscale), ALU.mult))
    v(lambda e: e.tensor_tensor(t1[:], qv[:], qv[:], ALU.mult))
    v(lambda e: e.tensor_single_scalar(t2[:], t1[:], 16.0, ALU.is_ge))
    v(lambda e: e.tensor_single_scalar(t1[:], t1[:], 4.0, ALU.is_ge))
    v(lambda e: e.tensor_scalar(t1[:], t1[:], -4.0, 8.0, ALU.mult, ALU.add))
    v(lambda e: e.scalar_tensor_tensor(t1[:], t2[:], -2.0, t1[:], ALU.mult, ALU.add))
    v(lambda e: e.tensor_tensor(qv[:], qv[:], t1[:], ALU.mult))
    v(lambda e: e.tensor_copy(i32t[:], qv[:]))
    v(lambda e: e.tensor_copy(rf[:], i32t[:]))
    v(lambda e: e.tensor_tensor(t2[:], rf[:], qv[:], ALU.subtract))
    v(lambda e: e.tensor_single_scalar(t2[:], t2[:], 0.5, ALU.is_equal))
    v(lambda e: e.tensor_tensor(rf[:], rf[:], t2[:], ALU.subtract))
    v(lambda e: e.reciprocal(t1[:], t1[:]))
    v(lambda e: e.tensor_tensor(rf[:], rf[:], t1[:], ALU.mult))
    v(lambda e: e.tensor_tensor(dst_ap.rearrange("p (b k) -> p b k", k=32),
                                rf[:].rearrange("p (b k) -> p b k", k=32),
                                bc(scale), ALU.mult), inc=inc_sem)


def build_l0d(p=None):
    if p is None:
        p = Prog("l0d")
    y = p.inp("y_d", [2048, T], BF16)
    owT = p.inp("out_wT", [2048, D])
    xn = p.inp("xn_tok", [T, D])
    fw = p.inp("ffn_w", [D, D])
    fbr = p.inp("ffn_b_re", [128, 8])
    x = p.inp("x", [T, D])
    idn_in = p.inp("idn", [128, 128], BF16)
    x1 = p.out("x1", [T, D])
    attn_d = p.nc.dram_tensor("attn_d", [D, T], BF16, kind="Internal").ap()
    ffn_d = p.nc.dram_tensor("ffn_d", [D, T], BF16, kind="Internal").ap()

    idn = p.sb("idn_s", [128, 128], BF16)
    owf = [p.sb(f"owf{i}", [128, D], F32) for i in range(2)]
    owb = p.sb("owb", [128, 16 * D], BF16)
    ycb = [p.sb(f"ycb{i}", [128, 16 * 512], BF16) for i in range(2)]
    fwf = [p.sb(f"fwf{i}", [128, D], F32) for i in range(2)]
    wq = [p.sb(f"wq{i}", [128, D], BF16) for i in range(2)]
    wqT = p.sb("wqT", [128, 64 * 128], BF16)
    xnf = [p.sb(f"xnf{i}", [128, D], F32) for i in range(2)]
    xd = [p.sb(f"xd{i}", [128, D], BF16) for i in range(2)]
    xd_d = p.sb("xd_d", [128, 8 * T], BF16)
    scr = {k: p.sb(f"scr_{k}", [128, D], F32 if k != "i32" else I32)
           for k in ["qv", "t1", "t2", "rf"]}
    scr["i32"] = p.sb("scr_i32", [128, D], I32)
    scr["amax"] = p.sb("scr_amax", [128, 32], F32)
    scr["scale"] = p.sb("scr_scale", [128, 32], F32)
    scr["rscale"] = p.sb("scr_rscale", [128, 32], F32)
    fbs = p.sb("fbs", [128, 8], F32)
    stg = [p.sb(f"stg{i}", [128, 512], BF16) for i in range(4)]

    ps_m = [p.ps(f"ps_m{i}", [128, 512], F32) for i in range(4)]
    ps_t = [p.ps(f"ps_t{i}", [128, 1024], BF16) for i in range(2)]
    ps_e = [p.ps(f"ps_e{i}", [128, 1024], BF16) for i in range(2)]

    li = p.sem("li")
    low = [p.sem(f"low{i}") for i in range(2)]
    ldy = [p.sem(f"ldy{i}") for i in range(2)]
    lfw = [p.sem(f"lfw{i}") for i in range(2)]
    lxn = [p.sem(f"lxn{i}") for i in range(2)]
    ldx = [p.sem(f"ldx{i}") for i in range(2)]
    lat = [p.sem(f"lat{i}") for i in range(2)]
    lft = [p.sem(f"lft{i}") for i in range(2)]
    cow = p.sem("cow")
    qw = p.sem("qw"); qx = p.sem("qx")
    pem = p.sem("pem"); evm = p.sem("evm")
    pet = p.sem("pet"); evt = p.sem("evt")
    pef = p.sem("pef"); evf = p.sem("evf")
    pee = p.sem("pee"); vre = p.sem("vre")
    stm = [p.sem(f"stm{i}") for i in range(4)]
    stf = [p.sem(f"stf{i}") for i in range(4)]
    sto = [p.sem(f"sto{i}") for i in range(2)]

    p.dma("sync", idn[:], idn_in, inc=li)
    p.dma("sync", fbs[:], fbr, inc=li)

    low_after = {}; cow_after = {}
    for k in range(16):
        w = [(cow, k - 1)] if k >= 2 else []
        p.dma("sync", owf[k % 2][:],
              owT[k * 128:(k + 1) * 128, :], wait=w, inc=low[k % 2])
        low_after[k] = low[k % 2].n
        p.do("gpsimd", (lambda k: lambda e: e.tensor_copy(
            owb[:, k * D:(k + 1) * D], owf[k % 2][:]))(k),
            wait=[(low[k % 2], low_after[k])], inc=cow)
        cow_after[k] = cow.n
    ldy_after = {}; pem_after = {}; evm_after = {}; stm_after = {}
    g = 0
    for f in range(8):
        s = f % 2
        w = []
        if f >= 2:
            w.append((pem, 8 * (f - 1)))
        for k in range(16):
            p.dma("sync", ycb[s][:, k * 512:(k + 1) * 512],
                  y[k * 128:(k + 1) * 128, f * 512:(f + 1) * 512],
                  wait=(w if k == 0 else []), inc=ldy[s])
        ldy_after[f] = ldy[s].n
        for ot in range(8):
            bank = ps_m[g % 4]
            for k in range(16):
                w = []
                if k == 0:
                    w = [(ldy[s], ldy_after[f]), (cow, 16)]
                    if g >= 4:
                        w.append((evm, evm_after[g - 4]))
                p.do("tensor", (lambda l, r, bank, s0, s1_: lambda e: e.matmul(
                    bank[:], l, r, start=s0, stop=s1_))(
                    owb[:, (k * 8 + ot) * 128:(k * 8 + ot + 1) * 128],
                    ycb[s][:, k * 512:(k + 1) * 512], bank, k == 0, k == 15),
                    wait=w, inc=(pem if k == 15 else None))
            pem_after[g] = pem.n
            w = [(pem, pem_after[g])]
            if g >= 4:
                w.append((stm[g % 4], stm_after[g - 4]))
            p.do("scalar", (lambda bank, g: lambda e: e.activation(
                stg[g % 4][:], bank[:], AF.Copy))(bank, g), wait=w, inc=evm)
            evm_after[g] = evm.n
            p.dma("gpsimd", attn_d[ot * 128:(ot + 1) * 128, f * 512:(f + 1) * 512],
                  stg[g % 4][:], wait=[(evm, evm_after[g])], inc=stm[g % 4])
            stm_after[g] = stm[g % 4].n
            g += 1

    lfw_after = {}; qw_after = {}; pet_after = {}; evt_after = {}
    for ot in range(8):
        s = ot % 2
        w = [(qw, ot - 1)] if ot >= 2 else []
        p.dma("sync", fwf[s][:], fw[ot * 128:(ot + 1) * 128, :], wait=w, inc=lfw[s])
        lfw_after[ot] = lfw[s].n
    for ot in range(8):
        s = ot % 2
        w = [(lfw[s], lfw_after[ot]), (evm, 64)]
        if ot >= 2:
            w.append((pet, pet_after[ot - 2]))
        emit_fp6_quant(p, fwf[s][:], wq[s][:], scr, w, qw)
        qw_after[ot] = qw.n
        for it in range(8):
            w = [(qw, qw_after[ot]), (li, 16)]
            if ot >= 2 and it == 0:
            	w.append((evt, evt_after[ot - 2]))
            p.do("tensor", (lambda s, ot, it: lambda e: e.transpose(
                ps_t[ot % 2][:, it * 128:(it + 1) * 128], wq[s][:, it * 128:(it + 1) * 128],
                idn[:]))(s, ot, it), wait=w, inc=(pet if it == 7 else None))
        pet_after[ot] = pet.n
        p.do("vector", (lambda ot: lambda e: e.tensor_copy(
            wqT[:].rearrange("p (it o c) -> p it o c", it=8, o=8)[:, :, ot, :],
            ps_t[ot % 2][:].rearrange("p (it c) -> p it c", c=128)))(ot),
            wait=[(pet, pet_after[ot])], inc=evt)
        evt_after[ot] = evt.n

    lxn_after = {}; qx_after = {}
    pet2_after = {}; evt2_after = {}
    for i in range(NTT):
        s = i % 2
        w = [(qx, i - 1)] if i >= 2 else []
        p.dma("sync", xnf[s][:], xn[i * 128:(i + 1) * 128, :], wait=w, inc=lxn[s])
        lxn_after[i] = lxn[s].n
    for i in range(NTT):
        s = i % 2
        w = [(lxn[s], lxn_after[i]), (evt, 8)]
        if i >= 2:
            w.append((pet, pet_after.get(("c", i - 2), 8)))
        emit_fp6_quant(p, xnf[s][:], xd[s][:], scr, w, qx)
        qx_after[i] = qx.n
        for j in range(8):
            w = [(qx, qx_after[i])]
            if j == 0:
                if i >= 2:
                    w.append((evt, evt_after[("c", i - 2)]))
                else:
                    w.append((evt, 8))
            p.do("tensor", (lambda s, i, j: lambda e: e.transpose(
                ps_t[i % 2][:, j * 128:(j + 1) * 128], xd[s][:, j * 128:(j + 1) * 128],
                idn[:]))(s, i, j), wait=w, inc=(pet if j == 7 else None))
        pet_after[("c", i)] = pet.n
        p.do("vector", (lambda i: lambda e: e.tensor_copy(
            xd_d[:].rearrange("p (j t) -> p j t", t=T)[:, :, i * 128:(i + 1) * 128],
            ps_t[i % 2][:].rearrange("p (j c) -> p j c", c=128)))(i),
            wait=[(pet, pet_after[("c", i)])], inc=evt)
        evt_after[("c", i)] = evt.n

    xdv = xd_d[:].rearrange("p (j t) -> p j t", t=T)
    pef_after = {}; evf_after = {}; stf_after = {}
    g2 = 0
    for ot in range(8):
        for f in range(8):
            bank = ps_m[g2 % 4]
            for k in range(8):
                w = []
                if k == 0:
                    w = [(evt, 8 + NTT)]
                    if g2 >= 4:
                        w.append((evf, evf_after[g2 - 4]))
                p.do("tensor", (lambda l, r, bank, s0, s1_: lambda e: e.matmul(
                    bank[:], l, r, start=s0, stop=s1_))(
                    wqT[:].rearrange("p (it o c) -> p it o c", it=8, o=8)[:, k, ot, :],
                    xdv[:, k, f * 512:(f + 1) * 512], bank, k == 0, k == 7),
                    wait=w, inc=(pef if k == 7 else None))
            pef_after[g2] = pef.n
            w = [(pef, pef_after[g2]), (li, 32)]
            if g2 >= 4:
                w.append((stf[g2 % 4], stf_after[g2 - 4]))
            else:
                w.append((stm[g2 % 4], stm[g2 % 4].n))
            p.do("vector", (lambda bank, g2, ot: lambda e: e.tensor_scalar(
                stg[g2 % 4][:], bank[:], fbs[:, ot:ot + 1], None, ALU.add))(bank, g2, ot),
                wait=w, inc=evf)
            evf_after[g2] = evf.n
            p.dma("gpsimd", ffn_d[ot * 128:(ot + 1) * 128, f * 512:(f + 1) * 512],
                  stg[g2 % 4][:], wait=[(evf, evf_after[g2])], inc=stf[g2 % 4])
            stf_after[g2] = stf[g2 % 4].n
            g2 += 1

    xti = [xnf[0], xnf[1]]
    xo = [scr["qv"], scr["t1"]]
    ati = [xd[0], xd[1]]
    fti = [wq[0], wq[1]]
    atv = attn_d.rearrange("(dt p) t -> p dt t", p=128)
    ftv = ffn_d.rearrange("(dt p) t -> p dt t", p=128)
    pee_after = {}; vre_after = {}; sto_after = {}
    for i in range(NTT):
        s = i % 2
        w = [(vre, 2 * (i - 1))] if i >= 2 else [(qx, NTT)]
        p.dma("sync", xti[s][:], x[i * 128:(i + 1) * 128, :], wait=w, inc=ldx[s])
        w2 = ([(s_, s_.n) for s_ in stm] + [(pet, 40)]) if i == 0 else []
        w3 = ([(s_, s_.n) for s_ in stf] + [(pet, 40)]) if i == 0 else []
        if i == 1:
            w2 = [(pet, 40)]
            w3 = [(pet, 40)]
        if i >= 2:
            w2 = [(vre, 2 * (i - 1))]
            w3 = []
        p.dma("sync", ati[s][:].rearrange("p (dt c) -> p dt c", c=128),
              atv[:, :, i * 128:(i + 1) * 128], wait=w2, inc=lat[s])
        p.dma("sync", fti[s][:].rearrange("p (dt c) -> p dt c", c=128),
              ftv[:, :, i * 128:(i + 1) * 128], wait=w3, inc=lft[s])
        for j in range(8):
            w = [(lat[s], lat[s].n)]
            if i >= 2 and j == 0:
                w.append((vre, vre_after[i - 2]))
            p.do("tensor", (lambda s, i, j: lambda e: e.transpose(
                ps_e[i % 2][:, j * 128:(j + 1) * 128], ati[s][:, j * 128:(j + 1) * 128],
                idn[:]))(s, i, j), wait=w, inc=None)
        for j in range(8):
            w = [(lft[s], lft[s].n)]
            if j == 0 and i < 2:
                w.append((evt, 8 + NTT))
            if j == 0 and i >= 2:
                w.append((vre, vre_after[i - 2]))
            p.do("tensor", (lambda s, i, j: lambda e: e.transpose(
                ps_t[i % 2][:, j * 128:(j + 1) * 128], fti[s][:, j * 128:(j + 1) * 128],
                idn[:]))(s, i, j), wait=w, inc=(pee if j == 7 else None))
        pee_after[i] = pee.n
        w = [(pee, pee_after[i]), (ldx[s], ldx[s].n)]
        if i >= 2:
            w.append((sto[s], sto_after[i - 2]))
        p.do("vector", (lambda s, i: lambda e: e.tensor_tensor(
            xo[s][:], xti[s][:], ps_e[i % 2][:].rearrange("p (j c) -> p j c", c=128),
            ALU.add))(s, i), wait=w, inc=vre)
        p.do("vector", (lambda s, i: lambda e: e.tensor_tensor(
            xo[s][:], xo[s][:], ps_t[i % 2][:].rearrange("p (j c) -> p j c", c=128),
            ALU.add))(s, i), inc=vre)
        vre_after[i] = vre.n
        p.dma("gpsimd", x1[i * 128:(i + 1) * 128, :], xo[s][:],
              wait=[(vre, vre_after[i])], inc=sto[s])
        sto_after[i] = sto[s].n
    p.do("sync", lambda e: e.nop(), wait=[(s_, s_.n) for s_ in sto])
    return p


def emit_ln(p, x, xn_d, idn, pfx, xn_store=None, sqrt_eps=1e-5, idn_wait=(),
            bufs=None, ps_t_bufs=2):
    if bufs is None:
        xbuf = [p.sb(f"{pfx}xbuf{i}", [128, D], F32) for i in range(2)]
    else:
        xbuf = bufs[0]
    s1 = p.sb(f"{pfx}s1", [128, NTT], F32)
    s2 = p.sb(f"{pfx}s2", [128, NTT], F32)
    mean = p.sb(f"{pfx}mean", [128, NTT], F32)
    var = p.sb(f"{pfx}var", [128, NTT], F32)
    sd = p.sb(f"{pfx}sd", [128, NTT], F32)
    rstd = p.sb(f"{pfx}rstd", [128, NTT], F32)
    if bufs is None:
        xnt32 = [p.sb(f"{pfx}xnt32{i}", [128, D], F32) for i in range(2)]
        xnt = [p.sb(f"{pfx}xnt{i}", [128, D], BF16) for i in range(2)]
    else:
        xnt32 = bufs[1]
        xnt = bufs[2]
    sqscr = xnt32
    ps_t = [p.ps(f"{pfx}ps_t{i}", [128, 1024], BF16) for i in range(ps_t_bufs)]
    if ps_t_bufs == 1:
        ps_t = [ps_t[0], ps_t[0]]
    ldx = [p.sem(f"{pfx}ldx{i}") for i in range(2)]
    vst = p.sem(f"{pfx}vst"); sst = p.sem(f"{pfx}sst"); vrs = p.sem(f"{pfx}vrs")
    srs = p.sem(f"{pfx}srs"); rrs = p.sem(f"{pfx}rrs")
    xnv = p.sem(f"{pfx}xnv"); pet = p.sem(f"{pfx}pet"); evq = p.sem(f"{pfx}evq")
    xns = [p.sem(f"{pfx}xns{i}") for i in range(2)]

    ldx_after = []
    for i in range(NTT):
        w = [(vst, i - 1), (sst, i - 1)] if i >= 2 else []
        p.dma("sync", _ap(xbuf[i % 2]), x[i * 128:(i + 1) * 128, :], wait=w, inc=ldx[i % 2])
        ldx_after.append(ldx[i % 2].n)
    for i in range(NTT):
        p.do("vector", (lambda i: lambda e: e.tensor_reduce(
            s1[:, i:i + 1], _ap(xbuf[i % 2]), AX.X, ALU.add))(i),
            wait=[(ldx[i % 2], ldx_after[i])], inc=vst)
        p.do("scalar", (lambda i: lambda e: e.activation(
            _ap(sqscr[i % 2]), _ap(xbuf[i % 2]), AF.Square, accum_out=s2[:, i:i + 1]))(i),
            wait=[(ldx[i % 2], ldx_after[i])], inc=sst)
    p.do("vector", lambda e: e.tensor_scalar(mean[:], s1[:], 1.0 / D, None, ALU.mult),
         wait=[(sst, NTT), (vst, NTT)])
    p.do("vector", lambda e: e.tensor_tensor(var[:], mean[:], mean[:], ALU.mult))
    p.do("vector", lambda e: e.scalar_tensor_tensor(
        var[:], s2[:], 1.0 / D, var[:], ALU.mult, ALU.subtract), inc=vrs)
    eps_ap = p.cbias(sqrt_eps)
    p.do("scalar", lambda e: e.activation(sd[:], var[:], AF.Sqrt, bias=eps_ap, scale=1.0),
         wait=[(vrs, vrs.n)] + p.cwait(), inc=srs)
    p.do("vector", lambda e: e.reciprocal(rstd[:], sd[:]), wait=[(srs, srs.n)], inc=rrs)
    rrs_c = rrs.n

    xnv_after = []; xns_after = []; pet_after = []; evq_after = []
    ldx2_after = []
    for i in range(NTT):
        w = [(rrs, rrs_c)] if i < 2 else [(xnv, xnv_after[i - 2] if False else 2 * (i - 1))]
        p.dma("sync", _ap(xbuf[i % 2]), x[i * 128:(i + 1) * 128, :], wait=w, inc=ldx[i % 2])
        ldx2_after.append(ldx[i % 2].n)
    for i in range(NTT):
        w = [(ldx[i % 2], ldx2_after[i]), (rrs, rrs_c)]
        if i >= 2:
            w += [(pet, pet_after[i - 2])]
            if xn_store is not None:
                w += [(xns[i % 2], xns_after[i - 2])]
        p.do("vector", (lambda i: lambda e: e.tensor_scalar(
            xnt32[i % 2][:], xbuf[i % 2][:], mean[:, i:i + 1], rstd[:, i:i + 1],
            ALU.subtract, ALU.mult))(i), wait=w, inc=xnv)
        p.do("vector", (lambda i: lambda e: e.tensor_copy(
            xnt[i % 2][:], xnt32[i % 2][:]))(i), inc=xnv)
        xnv_after.append(xnv.n)
        if xn_store is not None:
            p.dma("gpsimd", xn_store[i * 128:(i + 1) * 128, :], xnt32[i % 2][:],
                  wait=[(xnv, xnv_after[i])], inc=xns[i % 2])
            xns_after.append(xns[i % 2].n)
        else:
            xns_after.append(0)
        for j in range(NDT):
            w = []
            if j == 0:
                w = [(xnv, xnv_after[i])] + list(idn_wait)
                if i >= 2:
                    w.append((evq, evq_after[i - 2]))
                if ps_t_bufs == 1 and i >= 1:
                    w.append((evq, evq_after[i - 1]))
            p.do("tensor", (lambda i, j: lambda e: e.transpose(
                ps_t[i % 2][:, j * 128:(j + 1) * 128], _ap(xnt[i % 2])[:, j * 128:(j + 1) * 128],
                idn[:]))(i, j), wait=w, inc=(pet if j == NDT - 1 else None))
        pet_after.append(pet.n)
        p.do("vector", (lambda i: lambda e: e.tensor_copy(
            xn_d[:].rearrange("p (dt t) -> p dt t", t=T)[:, :, i * 128:(i + 1) * 128],
            ps_t[i % 2][:].rearrange("p (dt c) -> p dt c", c=128)))(i),
            wait=[(pet, pet_after[i])], inc=evq)
        evq_after.append(evq.n)
    p._ln_ps_t = ps_t
    return evq, evq.n, xns


def _ap(t):
    import concourse.ap as _capm
    return t if isinstance(t, _capm.AP) else t[:]


TQ = 2304

def build_l1a(p=None):
    if p is None:
        p = Prog("l1a")
    x = p.inp("x", [T, D])
    wqkv = p.inp("w_qkv", [D, 3 * D])
    bq = p.inp("b_qkv_re", [128, 24])
    bvrow = p.inp("b_v", [D])
    idn_in = p.inp("idn", [128, 128], BF16)
    xn_d_o = p.out("xn_d", [D, T], BF16)
    q_d = p.out("q_d", [D, TQ], BF16)
    k_d = p.out("k_d", [D, TQ], BF16)
    v_tok = p.out("v_tok", [TQ, D], BF16)

    idn = p.sb("idn_s", [128, 128], BF16)
    xn_d = p.sb("xn_d_s", [128, NDT * T], BF16)
    wbuf = [p.sb(f"wbuf{i}", [128, 8 * 128], F32) for i in range(2)]
    wbb = [p.sb(f"wbb{i}", [128, 8 * 128], BF16) for i in range(2)]
    brep = p.sb("brep", [128, D], F32)
    stg = [p.sb(f"stg{i}", [128, 512], BF16) for i in range(4)]
    ps_m = [p.ps(f"ps_m{i}", [128, 512], F32) for i in range(4)]
    li = p.sem("li")
    p.dma("sync", idn[:], idn_in, inc=li)
    p.dma("sync", brep[:], bvrow.unsqueeze(0).partition_broadcast(128).squeeze(1), inc=li)

    evq, evq_n, _ = emit_ln(p, x, xn_d, idn, "ln_", xn_store=None, idn_wait=[(li, li.n)])
    sxn = [p.sem(f"sxn{i}") for i in range(2)]
    xnd = xn_d[:].rearrange("p (dt t) -> p dt t", t=T)
    for k in range(NDT):
        p.dma("gpsimd", xn_d_o[k * 128:(k + 1) * 128, :], xnd[:, k, :],
              wait=[(evq, evq_n)], inc=sxn[k % 2])

    lw = [p.sem(f"lw{i}") for i in range(2)]
    gct = p.sem("gct"); pem = p.sem("pem"); evm = p.sem("evm")
    stm = [p.sem(f"stm{i}") for i in range(4)]
    ldb = p.sem("ldb")
    bqs = p.sb("bqs", [128, 24], F32)
    p.dma("sync", bqs[:], bq, inc=ldb)

    qk_chunks = [(ci * 512, min(512, TQ - ci * 512)) for ci in range(5)]
    lw_after = {}; pem_after = {}; evm_after = {}; stm_after = {}
    g = 0
    for ot in range(16):
        w = [(gct, ot - 1)] if ot >= 2 else []
        p.dma("sync", wbuf[ot % 2][:].rearrange("p (kt o) -> p kt o", o=128),
              wqkv[:, ot * 128:(ot + 1) * 128].rearrange("(kt p) o -> p kt o", p=128),
              wait=w, inc=lw[ot % 2])
        lw_after[ot] = lw[ot % 2].n
        p.do("gpsimd", (lambda ot: lambda e: e.tensor_copy(wbb[ot % 2][:], wbuf[ot % 2][:]))(ot),
             wait=[(lw[ot % 2], lw_after[ot])] + ([(pem, 5 * (ot - 1))] if ot >= 2 else []),
             inc=gct)
        for f, (c0, cw) in enumerate(qk_chunks):
            bank = ps_m[g % 4]
            for k in range(8):
                w = []
                if k == 0:
                    w = [(gct, ot + 1)]
                    if g == 0:
                        w.append((evq, evq_n))
                    if g >= 4:
                        w.append((evm, evm_after[g - 4]))
                p.do("tensor", (lambda l, r, bank, s0, s1_, cw: lambda e: e.matmul(
                    bank[:, 0:cw], l, r, start=s0, stop=s1_))(
                    wbb[ot % 2][:, k * 128:(k + 1) * 128],
                    xnd[:, k, c0:c0 + cw], bank, k == 0, k == 7, cw),
                    wait=w, inc=(pem if k == 7 else None))
            pem_after[g] = pem.n
            w = [(pem, pem_after[g]), (ldb, 16)]
            if g >= 4:
                w.append((stm[g % 4], stm_after[g - 4]))
            p.do("vector", (lambda bank, g, ot, cw: lambda e: e.tensor_scalar(
                stg[g % 4][:, 0:cw], bank[:, 0:cw], bqs[:, ot:ot + 1], None, ALU.add))(bank, g, ot, cw),
                wait=w, inc=evm)
            evm_after[g] = evm.n
            dst = q_d if ot < 8 else k_d
            ot2 = ot % 8
            p.dma("gpsimd", dst[ot2 * 128:(ot2 + 1) * 128, c0:c0 + cw], stg[g % 4][:, 0:cw],
                  wait=[(evm, evm_after[g])], inc=stm[g % 4])
            stm_after[g] = stm[g % 4].n
            g += 1
    wvf = [p.sb(f"wvf{i}", [128, D], F32) for i in range(2)]
    wvb = p.sb("wvb", [128, 8 * D], BF16)
    lwv = [p.sem(f"lwv{i}") for i in range(2)]
    cwv = p.sem("cwv")
    for k in range(8):
        w = [(cwv, k - 1)] if k >= 2 else []
        p.dma("sync", wvf[k % 2][:], wqkv[k * 128:(k + 1) * 128, 2048:3072],
              wait=w, inc=lwv[k % 2])
        p.do("gpsimd", (lambda k: lambda e: e.tensor_copy(
            wvb[:, k * D:(k + 1) * D], wvf[k % 2][:]))(k),
            wait=[(lwv[k % 2], lwv[k % 2].n)], inc=cwv)
    NVT = TQ // 128
    for tt in range(NVT):
        for fc in range(2):
            bank = ps_m[g % 4]
            for k in range(8):
                w = []
                if k == 0:
                    w = [(evq, evq_n), (cwv, 8)]
                    if g >= 4:
                        w.append((evm, evm_after[g - 4]))
                p.do("tensor", (lambda l, r, bank, s0, s1_: lambda e: e.matmul(
                    bank[:], l, r, start=s0, stop=s1_))(
                    xnd[:, k, tt * 128:(tt + 1) * 128],
                    wvb[:, k * 1024 + fc * 512: k * 1024 + fc * 512 + 512],
                    bank, k == 0, k == 7),
                    wait=w, inc=(pem if k == 7 else None))
            pem_after[g] = pem.n
            w = [(pem, pem_after[g]), (li, 32)]
            if g >= 4:
                w.append((stm[g % 4], stm_after[g - 4]))
            p.do("vector", (lambda bank, g, fc: lambda e: e.tensor_tensor(
                stg[g % 4][:], bank[:], brep[:, fc * 512:(fc + 1) * 512], ALU.add))(bank, g, fc),
                wait=w, inc=evm)
            evm_after[g] = evm.n
            p.dma("gpsimd", v_tok[tt * 128:(tt + 1) * 128, fc * 512:(fc + 1) * 512],
                  stg[g % 4][:], wait=[(evm, evm_after[g])], inc=stm[g % 4])
            stm_after[g] = stm[g % 4].n
            g += 1
    p.do("sync", lambda e: e.nop(),
         wait=[(s_, s_.n) for s_ in stm + sxn])
    return p


def build_l1b(p=None):
    if p is None:
        p = Prog("l1b")
    q_d = p.inp("q_d", [D, TQ], BF16)
    k_d = p.inp("k_d", [D, TQ], BF16)
    v_tok = p.inp("v_tok", [TQ, D], BF16)
    ao = p.out("attnout", [T, D], BF16)

    qb = p.sb("qb", [128, 8 * TQ], BF16)
    kb = p.sb("kb", [128, 8 * TQ], BF16)
    vext = p.sb("vext", [128, 18 * 16 * 65], BF16)
    pT = p.sb("pT", [128, 4 * 512], BF16)
    o65 = p.sb("o65", [65, 512], F32)
    stg = [p.sb(f"stg{i}", [128, 4 * 64], BF16) for i in range(2)]
    ps_s = [p.ps(f"ps_s{i}", [128, 512], F32) for i in range(2)]
    ps_v = [p.ps(f"ps_v{i}", [65, 512], F32) for i in range(2)]
    ps_o = [p.ps(f"ps_o{i}", [128, 4 * 65], F32) for i in range(2)]
    idn_in = p.inp("idn", [128, 128], BF16)
    idnf_in = p.inp("idnf", [128, 128])
    idn = p.sb("idn_s", [128, 128], BF16)
    idnf = p.sb("idnf_s", [128, 128], F32)

    li = p.sem("li"); ldq = p.sem("ldq"); msv = p.sem("msv"); ldv = p.sem("ldv")
    pes = p.sem("pes"); sex = p.sem("sex"); pev = p.sem("pev")
    evv = p.sem("evv"); peo = p.sem("peo"); vdv = p.sem("vdv")
    stm = [p.sem(f"stm{i}") for i in range(2)]

    p.dma("sync", idn[:], idn_in, inc=li)
    p.dma("sync", idnf[:], idnf_in, inc=li)
    qv = qb[:].rearrange("p (dt t) -> p dt t", t=TQ)
    kv = kb[:].rearrange("p (dt t) -> p dt t", t=TQ)
    for dt_ in range(8):
        p.dma("sync", qv[:, dt_, :], q_d[dt_ * 128:(dt_ + 1) * 128, :], inc=ldq)
        p.dma("sync", kv[:, dt_, :], k_d[dt_ * 128:(dt_ + 1) * 128, :], inc=ldq)
    p.do("vector", lambda e: e.memset(vext[:], 1.0), inc=msv)
    vxv = vext[:].rearrange("p (tt h c) -> p tt h c", tt=18, c=65)
    for tt in range(18):
        p.dma("sync", vxv[:, tt, :, 0:64],
              v_tok[tt * 128:(tt + 1) * 128, :].rearrange("p (h c) -> p h c", c=64),
              wait=([(msv, 1)] if tt == 0 else []), inc=ldv)
    ldq_n, ldv_n = ldq.n, ldv.n

    p.cbias(-3.0)
    rec = [p.sb(f"rec{i}", [128, 1], F32) for i in range(2)]
    gi = 0
    pes_after = {}; sex_after = {}; pev_after = {}; evv_after = {}
    peo_after = {}; vdv_after = {}; stm_after = {}
    for n in range(8):
        t0 = 256 * n
        for h in range(16):
            dt_ = h // 2
            dro = (h % 2) * 64
            s = gi % 2
            for ks in range(4):
                w = [(ldq, ldq_n), (li, 32)]
                if gi * 4 + ks >= 2:
                    w.append((sex, sex_after[gi * 4 + ks - 2]))
                p.do("tensor", (lambda dt_, dro, t0, ks, ps: lambda e: e.matmul(
                    ps[:],
                    kv[:, dt_, t0 + ks * 128: t0 + (ks + 1) * 128][dro:dro + 64, :],
                    qv[:, dt_, t0:t0 + 512][dro:dro + 64, :],
                    start=True, stop=True))(dt_, dro, t0, ks, ps_s[(gi * 4 + ks) % 2]),
                    wait=w, inc=pes)
                pes_after[gi * 4 + ks] = pes.n
                wexp = [(pes, pes_after[gi * 4 + ks])] + p.cwait()
                if ks == 0 and gi >= 1:
                    wexp.append((pev, pev_after[gi - 1]))
                p.do("scalar", (lambda ks, ps: lambda e: e.activation(
                    pT[:, ks * 512:(ks + 1) * 512], ps[:], AF.Exp,
                    bias=-3.0, scale=0.125))(ks, ps_s[(gi * 4 + ks) % 2]),
                    wait=wexp, inc=sex)
                sex_after[gi * 4 + ks] = sex.n
            for ks in range(4):
                w = []
                if ks == 0:
                    w = [(sex, sex_after[gi * 4 + 3]), (ldv, ldv_n)]
                    if gi >= 2:
                        w.append((evv, evv_after[gi - 2]))
                p.do("tensor", (lambda n, ks, h, s: lambda e: e.matmul(
                    ps_v[s][:], vxv[:, 2 * n + ks, h, :], pT[:, ks * 512:(ks + 1) * 512],
                    start=(ks == 0), stop=(ks == 3)))(n, ks, h, s),
                    wait=w, inc=(pev if ks == 3 else None))
            pev_after[gi] = pev.n
            w = [(pev, pev_after[gi])]
            if gi >= 1:
                w.append((peo, peo_after[gi - 1]))
            p.do("vector", (lambda s: lambda e: e.tensor_copy(o65[:], ps_v[s][:]))(s),
                 wait=w, inc=evv)
            evv_after[gi] = evv.n
            for qs in range(4):
                w = [(evv, evv_after[gi])]
                if qs == 0:
                    w.append((li, 32))
                    if gi >= 2:
                        w.append((vdv, vdv_after[gi - 2]))
                p.do("tensor", (lambda qs, s: lambda e: e.transpose(
                    ps_o[s][:, qs * 65:(qs + 1) * 65], o65[:, qs * 128:(qs + 1) * 128],
                    idnf[0:65, 0:65]))(qs, s),
                    wait=w, inc=(peo if qs == 3 else None))
            peo_after[gi] = peo.n
            for qs in range(4):
                w = [(peo, peo_after[gi])]
                if gi >= 2 and qs == 0:
                    w.append((stm[s], stm_after[gi - 2]))
                p.do("vector", (lambda qs, s: lambda e: e.reciprocal(
                    rec[s][:], ps_o[s][:, qs * 65 + 64: qs * 65 + 65]))(qs, s), wait=w)
                p.do("vector", (lambda qs, s: lambda e: e.tensor_scalar(
                    stg[s][:, qs * 64:(qs + 1) * 64], ps_o[s][:, qs * 65: qs * 65 + 64],
                    rec[s][:], None, ALU.mult))(qs, s), inc=vdv)
            vdv_after[gi] = vdv.n
            for qs in range(4):
                p.dma("gpsimd", ao[512 * n + qs * 128: 512 * n + (qs + 1) * 128,
                                   h * 64:(h + 1) * 64],
                      stg[s][:, qs * 64:(qs + 1) * 64],
                      wait=[(vdv, vdv_after[gi])], inc=stm[s])
            stm_after[gi] = stm[s].n
            gi += 1
    p.do("sync", lambda e: e.nop(), wait=[(s_, s_.n) for s_ in stm])
    return p


def build_l1c(p=None):
    if p is None:
        p = Prog("l1c")
    aoin = p.inp("attnout", [T, D], BF16)
    wo_in = p.inp("w_out", [D, D])
    bo_in = p.inp("b_out", [D])
    x = p.inp("x", [T, D])
    idn_in = p.inp("idn", [128, 128], BF16)
    r1d = p.out("r1", [T, D])

    idn = p.sb("idn_s", [128, 128], BF16)
    wof = [p.sb(f"wof{i}", [128, D], F32) for i in range(2)]
    wob = p.sb("wob", [128, 8 * D], BF16)
    aot = [p.sb(f"aot{i}", [128, D], BF16) for i in range(2)]
    aodt = [p.sb(f"aodt{i}", [128, 8 * 128], BF16) for i in range(2)]
    xt = [p.sb(f"xt{i}", [128, D], F32) for i in range(2)]
    ot = [p.sb(f"ot{i}", [128, D], F32) for i in range(2)]
    brep = p.sb("brep_c", [128, D], F32)
    ps_t = [p.ps(f"ps_t{i}", [128, 1024], BF16) for i in range(2)]
    ps_o = [p.ps(f"ps_o{i}", [128, 1024], F32) for i in range(2)]
    li = p.sem("li")
    p.dma("sync", idn[:], idn_in, inc=li)
    p.dma("sync", brep[:], bo_in.unsqueeze(0).partition_broadcast(128).squeeze(1), inc=li)
    li_n = li.n

    lwo = [p.sem(f"lwo{i}") for i in range(2)]
    cwo = p.sem("cwo")
    for k in range(8):
        w = [(cwo, k - 1)] if k >= 2 else []
        p.dma("sync", wof[k % 2][:], wo_in[k * 128:(k + 1) * 128, :], wait=w, inc=lwo[k % 2])
        p.do("gpsimd", (lambda k: lambda e: e.tensor_copy(
            wob[:, k * D:(k + 1) * D], wof[k % 2][:]))(k),
            wait=[(lwo[k % 2], lwo[k % 2].n)], inc=cwo)
    lao = [p.sem(f"lao{i}") for i in range(2)]
    ldx = [p.sem(f"ldx{i}") for i in range(2)]
    peT = p.sem("peT"); evT = p.sem("evT"); pem = p.sem("pem"); evo = p.sem("evo")
    str_ = [p.sem(f"str{i}") for i in range(2)]
    peT_after = {}; evT_after = {}; pem_after = {}; evo_after = {}; str_after = {}
    wobv = wob[:].rearrange("p (k o) -> p k o", o=D)
    for tt in range(NTT):
        s = tt % 2
        w = [(evT, evT_after[tt - 2])] if tt >= 2 else []
        p.dma("sync", aot[s][:], aoin[tt * 128:(tt + 1) * 128, :], wait=w, inc=lao[s])
        p.dma("sync", xt[s][:], x[tt * 128:(tt + 1) * 128, :],
              wait=([(evo, evo_after[tt - 2])] if tt >= 2 else []), inc=ldx[s])
        for j in range(8):
            w = [(lao[s], lao[s].n), (li, li_n)]
            if tt >= 2 and j == 0:
                w.append((evT, evT_after[tt - 2]))
            p.do("tensor", (lambda s, j: lambda e: e.transpose(
                ps_t[s][:, j * 128:(j + 1) * 128], aot[s][:, j * 128:(j + 1) * 128],
                idn[:]))(s, j), wait=w, inc=(peT if j == 7 else None))
        peT_after[tt] = peT.n
        p.do("vector", (lambda s: lambda e: e.tensor_copy(
            aodt[s][:], ps_t[s][:]))(s), wait=[(peT, peT_after[tt])], inc=evT)
        evT_after[tt] = evT.n
        for oc in range(2):
            bank = ps_o[oc]
            for k in range(8):
                w = []
                if k == 0:
                    w = [(evT, evT_after[tt]), (cwo, 8)]
                    if tt >= 1:
                        w.append((evo, evo_after.get(tt - 1, 0)))
                p.do("tensor", (lambda s, k, oc, bank: lambda e: e.matmul(
                    bank[:, 0:512], aodt[s][:, k * 128:(k + 1) * 128],
                    wobv[:, k, oc * 512:(oc + 1) * 512], start=(k == 0), stop=(k == 7)))(
                    s, k, oc, bank), wait=w, inc=(pem if k == 7 else None))
        pem_after[tt] = pem.n
        w = [(pem, pem_after[tt]), (ldx[s], ldx[s].n)]
        if tt >= 2:
            w.append((str_[s], str_after[tt - 2]))
        p.do("vector", (lambda s: lambda e: e.tensor_tensor(
            ot[s][:, 0:512], ps_o[0][:, 0:512], brep[:, 0:512], ALU.add))(s), wait=w)
        p.do("vector", (lambda s: lambda e: e.tensor_tensor(
            ot[s][:, 512:1024], ps_o[1][:, 0:512], brep[:, 512:1024], ALU.add))(s))
        p.do("vector", (lambda s: lambda e: e.tensor_tensor(
            ot[s][:], ot[s][:], xt[s][:], ALU.add))(s), inc=evo)
        evo_after[tt] = evo.n
        p.dma("gpsimd", r1d[tt * 128:(tt + 1) * 128, :], ot[s][:],
              wait=[(evo, evo_after[tt])], inc=str_[s])
        str_after[tt] = str_[s].n

    p.do("sync", lambda e: e.nop(), wait=[(s_, s_.n) for s_ in str_])
    return p


def build_l1d(p=None):
    if p is None:
        p = Prog("l1d")
    xnd_in = p.inp("xn_d", [D, T], BF16)
    w1T = p.inp("W1T", [D, 4096])
    b1r = p.inp("b1_re", [128, 32])
    w2T = p.inp("W2T", [4096, D])
    b2_in = p.inp("b2", [D])
    r1d = p.inp("r1", [T, D])
    x2 = p.out("x2", [T, D])
    csd = p.nc.dram_tensor("csd", [128 * 32], F32, kind="Internal").ap()

    w1b = p.sb("w1b", [128, 8 * 4096], BF16)
    w2b = p.sb("w2b", [128, 32 * 1024], BF16)
    hb = p.sb("hb", [128, 32 * 512], BF16)
    xnr = [p.sb(f"xnr{i}", [128, 8 * 512], BF16) for i in range(2)]
    ot = [p.sb(f"ot{i}", [128, D], F32) for i in range(2)]
    rt = [p.sb(f"rt{i}", [128, D], F32) for i in range(2)]
    brep = p.sb("brep_c", [128, D], BF16)
    b1s = p.sb("b1s", [128, 32], F32)
    cols = p.sb("cols", [128, 32], F32)
    sct = [p.sb(f"sct{i}", [128, 1], F32) for i in range(2)]
    tot11 = p.sb("tot11", [1, 2], F32)
    i32s = p.sb("i32s", [128, 1024], I32)
    ring = [p.sb(f"tring{i}", [128, 1024], F32) for i in range(2)]
    ps_m = [p.ps(f"ps_m{i}", [128, 512], F32) for i in range(2)]
    ps_o = [p.ps(f"ps_o{i}", [128, 512], F32) for i in range(2)]
    li = p.sem("li")
    p.dma("sync", b1s[:], b1r, inc=li)
    li_n = li.n

    def ternarize(wT, nkt, wide, dst, pfx, sct_t, barrier):
        SW = min(wide, 1024)
        nsub = wide // SW
        nch = nkt * nsub
        lds = [p.sem(f"{pfx}ld{i}") for i in range(2)]
        vred = p.sem(f"{pfx}vred"); tern = p.sem(f"{pfx}tern")
        sttr = p.sem(f"{pfx}sttr"); lsc = p.sem(f"{pfx}lsc")
        def chunk_ap(c):
            k, sub = divmod(c, nsub)
            return wT[k * 128:(k + 1) * 128, sub * SW:(sub + 1) * SW]
        ld_after = {}
        for c in range(nch):
            w = list(barrier) if c < 2 else [(vred, c - 1)]
            p.dma("sync", ring[c % 2][:, 0:SW], chunk_ap(c), wait=w, inc=lds[c % 2])
            ld_after[c] = lds[c % 2].n
        for c in range(nch):
            p.do("vector", (lambda c: lambda e: e.tensor_reduce(
                cols[:, c:c + 1], ring[c % 2][:, 0:SW], AX.X, ALU.add,
                apply_absolute_value=True))(c), wait=[(lds[c % 2], ld_after[c])], inc=vred)
        p.dma("gpsimd", csd[0:128 * nch].rearrange("(p k) -> p k", p=128),
              cols[:, 0:nch], wait=[(vred, vred.n)], inc=sttr)
        tot = 128 * nch
        nld = (tot + 1023) // 1024
        for ci in range(nld):
            c0 = ci * 1024
            cw = min(1024, tot - c0)
            w = [(sttr, sttr.n)] if ci == 0 else [(vred, vred.n)]
            p.dma("sync", ring[0][0:1, 0:cw], csd[c0:c0 + cw].unsqueeze(0),
                  wait=w, inc=lsc)
            p.do("vector", (lambda ci, cw: lambda e: e.tensor_reduce(
                tot11[0:1, 1:2], ring[0][0:1, 0:cw], AX.X, ALU.add))(ci, cw),
                wait=[(lsc, lsc.n)])
            if ci == 0:
                p.do("vector", lambda e: e.tensor_copy(tot11[0:1, 0:1], tot11[0:1, 1:2]),
                     inc=vred)
            else:
                p.do("vector", lambda e: e.tensor_tensor(
                    tot11[0:1, 0:1], tot11[0:1, 0:1], tot11[0:1, 1:2], ALU.add), inc=vred)
        p.dma("gpsimd", csd[0:1].unsqueeze(0), tot11[0:1, 0:1],
              wait=[(vred, vred.n)], inc=sttr)
        p.dma("sync", sct_t[:], csd[0:1].unsqueeze(0).partition_broadcast(128).squeeze(1),
              wait=[(sttr, sttr.n)], inc=lsc)
        p.do("vector", (lambda nkt, wide: lambda e: e.tensor_scalar(
            sct_t[:], sct_t[:], 1.0 / (nkt * 128 * wide), 1e-8, ALU.mult, ALU.max))(nkt, wide),
            wait=[(lsc, lsc.n)])
        p.do("vector", (lambda: lambda e: e.reciprocal(sct_t[:], sct_t[:]))(), inc=tern)
        t0 = tern.n
        ld2 = {}
        for c in range(nch):
            w = [(tern, t0)] if c < 2 else [(tern, t0 + c - 1)]
            p.dma("sync", ring[c % 2][:, 0:SW], chunk_ap(c), wait=w, inc=lds[c % 2])
            ld2[c] = lds[c % 2].n
        for c in range(nch):
            k, sub = divmod(c, nsub)
            dof = k * wide + sub * SW
            p.do("vector", (lambda c: lambda e: e.tensor_scalar(
                ring[c % 2][:, 0:SW], ring[c % 2][:, 0:SW], sct_t[:], None,
                ALU.mult))(c), wait=[(lds[c % 2], ld2[c])])
            p.do("vector", (lambda c: lambda e: e.tensor_copy(
                i32s[:, 0:SW], ring[c % 2][:, 0:SW]))(c))
            p.do("vector", (lambda c: lambda e: e.tensor_copy(
                ring[c % 2][:, 0:SW], i32s[:, 0:SW]))(c))
            p.do("vector", (lambda c, dof: lambda e: e.tensor_scalar(
                dst[:, dof:dof + SW], ring[c % 2][:, 0:SW], 1.0, -1.0,
                ALU.min, ALU.max))(c, dof), inc=tern)
        return tern, tern.n

    barrier = []
    t1s, t1n = ternarize(w1T, 8, 4096, w1b[:], "t1_", sct[0], barrier)
    t2s, t2n = ternarize(w2T, 32, 1024, w2b[:], "t2_", sct[1], [(t1s, t1n)])

    lxn = [p.sem(f"lxn{i}") for i in range(2)]
    lr1 = [p.sem(f"lr{i}") for i in range(2)]
    peh = p.sem("peh"); evh = p.sem("evh"); peo2 = p.sem("peo2"); evf = p.sem("evf")
    stx = [p.sem(f"stx{i}") for i in range(2)]
    b2rep = brep
    lb2 = p.sem("lb2")
    p.dma("gpsimd", b2rep[:], b2_in.unsqueeze(0).partition_broadcast(128).squeeze(1), inc=lb2)
    w1v = w1b[:].rearrange("p (k f) -> p k f", f=4096)
    w2v = w2b[:].rearrange("p (ft o) -> p ft o", o=1024)
    peh_after = {}; evh_after = {}; peo2_after = {}; evf_after = {}; stx_after = {}
    lxn_after = {}; lr1_after = {}
    for tc in range(8):
        s = tc % 2
        w = [(peh, peh_after[(tc - 2, 31)])] if tc >= 2 else []
        for k in range(8):
            p.dma("sync", xnr[s][:, k * 512:(k + 1) * 512],
                  xnd_in[k * 128:(k + 1) * 128, tc * 512:(tc + 1) * 512],
                  wait=(w if k == 0 else []), inc=lxn[s])
        lxn_after[tc] = lxn[s].n
        for ft in range(32):
            bank = ps_m[ft % 2]
            for k in range(8):
                w = []
                if k == 0:
                    w = [(lxn[s], lxn_after[tc]), (t2s, t2n)]
                    if tc * 32 + ft >= 2:
                        w.append((evh, evh_after[tc * 32 + ft - 2]))
                p.do("tensor", (lambda s, k, ft, bank: lambda e: e.matmul(
                    bank[:], w1v[:, k, ft * 128:(ft + 1) * 128],
                    xnr[s][:, k * 512:(k + 1) * 512], start=(k == 0), stop=(k == 7)))(
                    s, k, ft, bank), wait=w, inc=(peh if k == 7 else None))
            peh_after[(tc, ft)] = peh.n
            p.do("scalar", (lambda ft, bank: lambda e: e.activation(
                hb[:, ft * 512:(ft + 1) * 512], bank[:], AF.Gelu,
                bias=b1s[:, ft:ft + 1], scale=1.0))(ft, bank),
                wait=[(peh, peh_after[(tc, ft)]), (li, li_n)], inc=evh)
            evh_after[tc * 32 + ft] = evh.n
        p.dma("sync", rt[0][:], r1d[tc * 512:tc * 512 + 128, :],
              wait=([(evf, evf_after[(tc - 1, 2)])] if tc >= 1 else []), inc=lr1[0])
        for ts in range(4):
            if ts > 0:
                if ts >= 2:
                    wrt = [(evf, evf_after[(tc, ts - 2)])]
                elif tc >= 1:
                    wrt = [(evf, evf_after[(tc - 1, 3)])]
                else:
                    wrt = []
                p.dma("sync", rt[ts % 2][:],
                      r1d[tc * 512 + ts * 128: tc * 512 + (ts + 1) * 128, :],
                      wait=wrt, inc=lr1[ts % 2])
            lr1_after[(tc, ts)] = lr1[ts % 2].n
            for half in range(2):
                bank = ps_o[half]
                for ft in range(32):
                    w = []
                    if ft == 0:
                        w = [(evh, evh_after[tc * 32 + 31])]
                        if (tc, ts) != (0, 0):
                            prev = (tc, ts - 1) if ts > 0 else (tc - 1, 3)
                            w.append((evf, evf_after[prev]))
                    p.do("tensor", (lambda ts, ft, half, bank: lambda e: e.matmul(
                        bank[:], hb[:, ft * 512 + ts * 128: ft * 512 + (ts + 1) * 128],
                        w2v[:, ft, half * 512:(half + 1) * 512],
                        start=(ft == 0), stop=(ft == 31)))(ts, ft, half, bank),
                        wait=w, inc=(peo2 if ft == 31 else None))
            peo2_after[(tc, ts)] = peo2.n
            w = [(peo2, peo2_after[(tc, ts)]), (lr1[ts % 2], lr1_after[(tc, ts)]), (lb2, 16)]
            if (tc * 4 + ts) >= 2:
                pv = (tc, ts - 2) if ts >= 2 else (tc - 1, ts + 2)
                w.append((stx[ts % 2], stx_after[pv]))
            p.do("vector", (lambda ts: lambda e: e.tensor_tensor(
                ot[ts % 2][:, 0:512], ps_o[0][:, 0:512], b2rep[:, 0:512], ALU.add))(ts),
                wait=w)
            p.do("vector", (lambda ts: lambda e: e.tensor_tensor(
                ot[ts % 2][:, 512:1024], ps_o[1][:, 0:512], b2rep[:, 512:1024], ALU.add))(ts))
            p.do("vector", (lambda ts: lambda e: e.tensor_tensor(
                ot[ts % 2][:], ot[ts % 2][:], rt[ts % 2][:], ALU.add))(ts), inc=evf)
            evf_after[(tc, ts)] = evf.n
            p.dma("gpsimd", x2[tc * 512 + ts * 128: tc * 512 + (ts + 1) * 128, :],
                  ot[ts % 2][:], wait=[(evf, evf_after[(tc, ts)])], inc=stx[ts % 2])
            stx_after[(tc, ts)] = stx[ts % 2].n
    p.do("sync", lambda e: e.nop(), wait=[(s_, s_.n) for s_ in stx])
    return p


def build_l2b(p=None):
    if p is None:
        p = Prog("l2b")
    x = p.inp("x", [T, D])
    wk_in = p.inp("Wk", [D, D])
    wq_in = p.inp("Wq", [D, D])
    wo_in = p.inp("Wo", [D, D])
    bo_in = p.inp("bo", [D])
    idn_in = p.inp("idn", [128, 128], BF16)
    idnf_in = p.inp("idnf", [128, 128])
    xn_tok_o = p.out("xn_tok", [T, D], BF16)
    r2 = p.out("r2", [T, D])

    idn = p.sb("idn_s", [128, 128], BF16)
    idnf = p.sb("idnf_s", [128, 128], F32)
    xn_d = p.sb("xn_d_s", [128, NDT * T], BF16)
    ao = p.sb("ao", [128, NTT * D], BF16)
    kp = [p.sb("kp0", [128, T], BF16)]
    qp = [p.sb("qp0", [128, T], BF16)]
    vx = [p.sb("vx0", [128, NTT * 130], BF16)]
    wcb = [p.sb(f"wcb{i}", [128, 8 * 128], BF16) for i in range(2)]
    pT = [p.sb(f"pT{i}", [128, 1024], BF16) for i in range(2)]
    o65 = p.sb("o65", [65, 512], F32)
    rec = [p.sb(f"rec{i}", [128, 1], F32) for i in range(2)]
    stg = [p.sb(f"stg{i}", [128, 4 * 64], BF16) for i in range(2)]
    borep = p.sb("borep", [128, D], BF16)
    aodt = [p.sb(f"aodt{i}", [128, 8 * 128], BF16) for i in range(2)]
    xstg = [aodt[0], aodt[1]]
    xt = [p.sb(f"xt{i}", [128, D], F32) for i in range(2)]
    otl = [p.sb(f"otl{i}", [128, D], F32) for i in range(2)]
    ps_d = [p.ps(f"ps_d{i}", [128, 1024], F32) for i in range(2)]
    ps_v = [p.ps(f"ps_v{i}", [65, 512], F32) for i in range(2)]
    ps_oo = [p.ps("ps_oo0", [128, 4 * 65], F32), None]
    ps_oo[1] = ps_oo[0]
    ps_s = [ps_d[0][:, 0:512], ps_d[0][:, 512:1024]]
    li = p.sem("li")
    lig = p.sem("lig")
    p.dma("sync", idn[:], idn_in, inc=li)
    p.dma("sync", idnf[:], idnf_in, inc=li)
    p.dma("gpsimd", borep[:], bo_in.unsqueeze(0).partition_broadcast(128).squeeze(1), inc=lig)
    li_n = li.n
    p.cbias(-3.0)

    evq, evq_n, _ = emit_ln(p, x, xn_d, idn, "ln_", idn_wait=[(li, li.n)], ps_t_bufs=1)
    ps_t = p._ln_ps_t
    ps_m = None
    xnd = xn_d[:].rearrange("p (dt t) -> p dt t", t=T)
    sxn = [p.sem(f"sxn{i}") for i in range(2)]
    pex = p.sem("pex"); evx = p.sem("evx")
    pex_after = {}; evx_after = {}; sxn_after = {}
    for i in range(NTT):
        s = i % 2
        for j in range(NDT):
            w = []
            if j == 0:
                w = [(evq, evq_n), (li, li_n)]
                if i >= 1:
                    w.append((evx, evx_after[i - 1]))
            p.do("tensor", (lambda i, j: lambda e: e.transpose(
                ps_t[i % 2][:, j * 128:(j + 1) * 128],
                xnd[:, j, i * 128:(i + 1) * 128], idn[:]))(i, j),
                wait=w, inc=(pex if j == NDT - 1 else None))
        pex_after[i] = pex.n
        w = [(pex, pex_after[i])]
        if i >= 2:
            w.append((sxn[s], sxn_after[i - 2]))
        p.do("vector", (lambda s: lambda e: e.tensor_copy(
            xstg[s][:], ps_t[s][:]))(s), wait=w, inc=evx)
        evx_after[i] = evx.n
        p.dma("gpsimd", xn_tok_o[i * 128:(i + 1) * 128, :], xstg[s][:],
              wait=[(evx, evx_after[i])], inc=sxn[s])
        sxn_after[i] = sxn[s].n

    lw = [p.sem(f"lw{i}") for i in range(4)]
    pep = p.sem("pep"); evp = p.sem("evp")
    pes = p.sem("pes"); sex = p.sem("sex"); pev = p.sem("pev")
    evv = p.sem("evv"); peo = p.sem("peo"); vdv = p.sem("vdv")
    sao = p.sem("sao")
    lw_after = {}; evp_after = {}; pes_after = {}; sex_after = {}
    pev_after = {}; evv_after = {}; peo_after = {}; vdv_after = {}
    NKS = 32
    gq = 0
    pes_end = {}; pev_end = {}; sex_end = {}
    sexi = 0
    sex_at = {}
    for hp in range(8):
        s = 0
        w = [(lw[0], lw[0].n), (lw[1], lw[1].n)] if False else []
        if hp >= 1:
            w.append((pes, pes_end[hp - 1]))
        p.dma("gpsimd", wcb[0][:].rearrange("p (k c) -> p k c", c=128),
              wk_in[:, hp * 128:(hp + 1) * 128].rearrange("(k p) c -> p k c", p=128),
              wait=w, inc=lw[0])
        p.dma("gpsimd", wcb[1][:].rearrange("p (k c) -> p k c", c=128),
              wq_in[:, hp * 128:(hp + 1) * 128].rearrange("(k p) c -> p k c", p=128),
              inc=lw[1])
        lwk, lwq = lw[0].n, lw[1].n
        kp_ev = {}; qp_ev = {}; vx_ev = {}
        for f in range(8):
            for k in range(8):
                w = []
                if k == 0:
                    w = [(lw[0], lwk), (evq, evq_n)]
                    if f > 0:
                        w.append((evp, kp_ev[f - 1]))
                    elif hp >= 1:
                        w.append((pes, pes_end[hp - 1]))
                        w.append((sex, sex_end[hp - 1]))
                p.do("tensor", (lambda k, f: lambda e: e.matmul(
                    ps_s[0][:], wcb[0][:, k * 128:(k + 1) * 128],
                    xnd[:, k, f * 512:(f + 1) * 512], start=(k == 0), stop=(k == 7)))(k, f),
                    wait=w, inc=(pep if k == 7 else None))
            p.do("scalar", (lambda f: lambda e: e.activation(
                kp[0][:, f * 512:(f + 1) * 512], ps_s[0][:], AF.Copy))(f),
                wait=[(pep, pep.n)], inc=evp)
            kp_ev[f] = evp.n
            for k in range(8):
                w = []
                if k == 0:
                    w = [(lw[1], lwq)]
                    if f > 0:
                        w.append((evp, qp_ev[f - 1]))
                p.do("tensor", (lambda k, f: lambda e: e.matmul(
                    ps_s[1][:], wcb[1][:, k * 128:(k + 1) * 128],
                    xnd[:, k, f * 512:(f + 1) * 512], start=(k == 0), stop=(k == 7)))(k, f),
                    wait=w, inc=(pep if k == 7 else None))
            p.do("scalar", (lambda f: lambda e: e.activation(
                qp[0][:, f * 512:(f + 1) * 512], ps_s[1][:], AF.Copy))(f),
                wait=[(pep, pep.n)], inc=evp)
            qp_ev[f] = evp.n
        for tt in range(NTT):
            for k in range(8):
                w = []
                if k == 0:
                    w = [(lw[0], lwk), (evq, evq_n)]
                    w.append((evp, kp_ev[7] if tt == 0 else vx_ev[tt - 1]))
                    if hp >= 1 and tt == 0:
                        w.append((pev, pev_end[hp - 1]))
                p.do("tensor", (lambda k, tt: lambda e: e.matmul(
                    ps_s[0][:, 0:128], xnd[:, k, tt * 128:(tt + 1) * 128],
                    wcb[0][:, k * 128:(k + 1) * 128], start=(k == 0), stop=(k == 7)))(k, tt),
                    wait=w, inc=(pep if k == 7 else None))
            p.do("scalar", (lambda tt: lambda e: e.activation(
                vx[0][:].rearrange("p (t h c) -> p t h c", h=2, c=65)[:, tt, :, 0:64],
                ps_s[0][:, 0:128].rearrange("p (h c) -> p h c", c=64), AF.Copy))(tt),
                wait=[(pep, pep.n)], inc=evp)
            vx_ev[tt] = evp.n
        if hp == 0:
            msv = p.sem("msv")
            p.do("vector", lambda e: e.memset(
                vx[0][:].rearrange("p (t h c) -> p t h c", h=2, c=65)[:, :, :, 64:65], 1.0),
                inc=msv)
        evp_c = evp.n
        vxv = vx[0][:].rearrange("p (t c) -> p t c", c=130)
        for hh in range(2):
            h = hp * 2 + hh
            dro = hh * 64
            for qcp in range(4):
                qa, qb = 2 * qcp, 2 * qcp + 1
                for k in range(NKS + 1):
                    if k < NKS:
                        sl = sexi % 2
                        for si, qq in ((0, qa), (1, qb)):
                            w = [(evp, evp_c), (msv, 1)]
                            if sexi >= 2 and si == 0:
                                w.append((sex, sex_at[sexi - 2]))
                            p.do("tensor", (lambda dro, k, qq, sl, si: lambda e: e.matmul(
                                ps_d[sl][:, si * 512:(si + 1) * 512],
                                kp[0][:, k * 128:(k + 1) * 128][dro:dro + 64, :],
                                qp[0][:, qq * 512:(qq + 1) * 512][dro:dro + 64, :],
                                start=True, stop=True))(dro, k, qq, sl, si),
                                wait=w, inc=(pes if si == 1 else None))
                        wexp = [(pes, pes.n)] + p.cwait()
                        if sexi >= 2:
                            wexp.append((pev, pev.n))
                        p.do("scalar", (lambda sl: lambda e: e.activation(
                            pT[sl][:], ps_d[sl][:], AF.Exp, bias=-3.0,
                            scale=1.0 / 32))(sl), wait=wexp, inc=sex)
                        sex_at[sexi] = sex.n
                    if k >= 1:
                        kv_ = k - 1
                        slv = (sexi - 1) % 2
                        w = [(sex, sex_at[sexi - 1])]
                        if kv_ == 0 and gq >= 1:
                            w.append((evv, evv_after.get(gq - 1, 0)))
                        for si in (0, 1):
                            p.do("tensor", (lambda hh, kv_, slv, si: lambda e: e.matmul(
                                ps_v[si][:], vxv[:, kv_, hh * 65:(hh + 1) * 65],
                                pT[slv][:, si * 512:(si + 1) * 512],
                                start=(kv_ == 0), stop=(kv_ == NKS - 1)))(hh, kv_, slv, si),
                                wait=(w if si == 0 else []),
                                inc=(pev if si == 1 else None))
                    if k < NKS:
                        sexi += 1
                pevc = pev.n
                for si, qq in ((0, qa), (1, qb)):
                    w = [(pev, pevc)]
                    if gq >= 1:
                        w.append((peo, peo_after.get(gq - 1, 0)))
                    p.do("vector", (lambda si: lambda e: e.tensor_copy(
                        o65[:], ps_v[si][:]))(si), wait=w, inc=evv)
                    evv_after[gq] = evv.n
                    for qs in range(4):
                        w = [(evv, evv_after[gq]), (li, li_n)]
                        if qs == 0 and gq >= 1:
                            w.append((vdv, vdv_after.get(gq - 1, 0)))
                        p.do("tensor", (lambda qs: lambda e: e.transpose(
                            ps_oo[0][:, qs * 65:(qs + 1) * 65],
                            o65[:, qs * 128:(qs + 1) * 128], idnf[0:65, 0:65]))(qs),
                            wait=w, inc=(peo if qs == 3 else None))
                    peo_after[gq] = peo.n
                    for qs in range(4):
                        w = [(peo, peo_after[gq])]
                        p.do("vector", (lambda qs: lambda e: e.reciprocal(
                            rec[0][:], ps_oo[0][:, qs * 65 + 64:qs * 65 + 65]))(qs),
                            wait=w)
                        p.do("vector", (lambda qs, h, qq: lambda e: e.tensor_scalar(
                            ao[:].rearrange("p (t d) -> p t d", d=D)
                            [:, qq * 4 + qs, h * 64:(h + 1) * 64],
                            ps_oo[0][:, qs * 65:qs * 65 + 64],
                            rec[0][:], None, ALU.mult))(qs, h, qq), inc=vdv)
                    vdv_after[gq] = vdv.n
                    gq += 1
        pes_end[hp] = pes.n
        pev_end[hp] = pev.n
        sex_end[hp] = sex.n
    lwo = p.sem("lwo")
    p.dma("gpsimd", kp[0][:].rearrange("p (k o) -> p k o", o=D),
          wo_in[0:512, :].rearrange("(k p) o -> p k o", p=128),
          wait=[(pes, pes_end[7]), (pev, pev_end[7])], inc=lwo)
    p.dma("gpsimd", qp[0][:].rearrange("p (k o) -> p k o", o=D),
          wo_in[512:1024, :].rearrange("(k p) o -> p k o", p=128), inc=lwo)
    lwo_n = lwo.n

    def wob_ap(k, oc):
        t = kp[0] if k < 4 else qp[0]
        return t[:, (k % 4) * 1024 + oc * 512:(k % 4) * 1024 + oc * 512 + 512]
    ldx = [p.sem(f"ldx2{i}") for i in range(2)]
    peT = p.sem("peT"); evT = p.sem("evT"); pem = p.sem("pem"); evo = p.sem("evo")
    str_ = [p.sem(f"str{i}") for i in range(2)]
    peT_after = {}; evT_after = {}; pem_after = {}; evo_after = {}; str_after = {}
    aov = ao[:].rearrange("p (t d) -> p t d", d=D)
    for tt in range(NTT):
        s = tt % 2
        p.dma("sync", xt[s][:], x[tt * 128:(tt + 1) * 128, :],
              wait=([(evo, evo_after[tt - 2])] if tt >= 2 else []), inc=ldx[s])
        for j in range(8):
            w = []
            if j == 0:
                w = [(vdv, vdv.n), (li, li_n)]
                if tt >= 1:
                    w.append((evT, evT_after[tt - 1]))
            p.do("tensor", (lambda s, j, tt: lambda e: e.transpose(
                ps_t[s][:, j * 128:(j + 1) * 128],
                aov[:, tt, j * 128:(j + 1) * 128], idn[:]))(s, j, tt),
                wait=w, inc=(peT if j == 7 else None))
        peT_after[tt] = peT.n
        w_ev = [(peT, peT_after[tt])]
        if tt < 2:
            w_ev += [(sxn[0], sxn[0].n), (sxn[1], sxn[1].n)]
        p.do("vector", (lambda s: lambda e: e.tensor_copy(
            aodt[s][:], ps_t[s][:]))(s), wait=w_ev, inc=evT)
        evT_after[tt] = evT.n
        for oc in range(2):
            bank = ps_s[oc]
            for k in range(8):
                w = []
                if k == 0:
                    w = [(evT, evT_after[tt]), (li, li_n), (lwo, lwo_n)]
                    if tt >= 1:
                        w.append((evo, evo_after.get(tt - 1, 0)))
                p.do("tensor", (lambda s, k, oc, bank: lambda e: e.matmul(
                    bank[:], aodt[s][:, k * 128:(k + 1) * 128],
                    wob_ap(k, oc), start=(k == 0), stop=(k == 7)))(
                    s, k, oc, bank), wait=w, inc=(pem if k == 7 else None))
        pem_after[tt] = pem.n
        w = [(pem, pem_after[tt]), (ldx[s], ldx[s].n)]
        if tt >= 2:
            w.append((str_[s], str_after[tt - 2]))
        p.do("vector", (lambda s: lambda e: e.tensor_tensor(
            otl[s][:, 0:512], ps_s[0][:], borep[:, 0:512], ALU.add))(s),
            wait=w + [(lig, 16)])
        p.do("vector", (lambda s: lambda e: e.tensor_tensor(
            otl[s][:, 512:1024], ps_s[1][:], borep[:, 512:1024], ALU.add))(s))
        p.do("vector", (lambda s: lambda e: e.tensor_tensor(
            otl[s][:], otl[s][:], xt[s][:], ALU.add))(s), inc=evo)
        evo_after[tt] = evo.n
        p.dma("gpsimd", r2[tt * 128:(tt + 1) * 128, :], otl[s][:],
              wait=[(evo, evo_after[tt])], inc=str_[s])
        str_after[tt] = str_[s].n
    p.do("sync", lambda e: e.nop(),
         wait=[(s_, s_.n) for s_ in str_ + sxn])
    return p


def build_l2c(p=None):
    if p is None:
        p = Prog("l2c")
    xn_in = p.inp("xn_tok", [T, D], BF16)
    nsc = p.inp("norm_scale", [D])
    wrT = p.inp("w_router_T", [D, 8])
    wupT = p.inp("w_upT", [8, D, 5632])
    wdnT = p.inp("w_downT", [8, 2816, D])
    r2in = p.inp("r2", [T, D])
    idn_in = p.inp("idn", [128, 128], BF16)
    x3 = p.out("x3", [T, D])

    idn = p.sb("idn_s", [128, 128], BF16)
    xnr_d = p.sb("xnr_d", [128, NDT * T], BF16)
    yh = p.sb("yh", [128, 16 * D], F32)
    xnt = [p.sb(f"xnt{i}", [128, D], BF16) for i in range(2)]
    rt1 = p.sb("rt1", [128, D], F32)
    xrt = [rt1[:, 0:512].bitcast(BF16), rt1[:, 512:1024].bitcast(BF16)]
    sq = p.sb("sq", [128, D], F32)
    ssc = p.sb("ssc", [128, NTT], F32)
    rstd = p.sb("rstd", [128, NTT], F32)
    ot2 = p.sb("ot2", [128, D], F32)
    screp = ot2[:, 0:512].bitcast(BF16)
    coef = p.sb("coef", [128, NTT * 8], F32)
    scs = p.sb("scs", [128, 8], F32)
    m1 = p.sb("m1", [128, 1], F32)
    m2 = p.sb("m2", [128, 1], F32)
    eqm = p.sb("eqm", [128, 8], F32)
    prb = p.sb("prb", [128, 8], F32)
    wrb = p.sb("wrb", [128, 8 * 8], BF16)
    wval = [p.sb(f"wval{i}", [128, 8 * 512], BF16) for i in range(2)]
    wgat = [p.sb(f"wgat{i}", [128, 8 * 512], BF16) for i in range(2)]
    wdnb = [p.sb(f"wdnb{i}", [128, 4 * D], BF16) for i in range(2)]
    gbufs = [p.sb(f"gb{i}", [128, 4 * 512], BF16) for i in range(2)]
    sgb = p.sb("sgb", [128, 512], BF16)
    rt0 = p.sb("rt0", [128, D], F32)
    rt = [rt0, rt1]

    ps_u = [p.ps(f"ps_u{i}", [128, 512], F32) for i in range(2)]
    pu = None
    ps_y = [p.ps(f"ps_y{i}", [128, 512], F32) for i in range(2)]
    ps_t = [p.ps(f"ps_t{i}", [128, 1024], BF16) for i in range(2)]
    ps_r = p.ps("ps_r", [128, 8], F32)
    pu = [[ps_u[0], ps_u[1]],
          [ps_t[0][:].bitcast(F32), ps_t[1][:].bitcast(F32)]]

    ot = [sq, ot2]
    li = p.sem("li")
    lig = p.sem("lig")
    p.dma("sync", idn[:], idn_in, inc=li)
    p.dma("gpsimd", screp[:], nsc.unsqueeze(0).partition_broadcast(128).squeeze(1), inc=lig)
    p.dma("gpsimd", wrb[:].rearrange("p (k c) -> p k c", c=8),
          wrT.rearrange("(k p) c -> p k c", p=128), inc=lig)
    li_n = li.n
    eps_ap = p.cbias(1e-6)

    ldx = [p.sem(f"ldx{i}") for i in range(2)]
    vrm = p.sem("vrm"); srm = p.sem("srm"); pet = p.sem("pet"); evq = p.sem("evq")
    ldx_after = {}; vrm_after = {}; pet_after = {}; evq_after = {}
    xndv = xnr_d[:].rearrange("p (dt t) -> p dt t", t=T)
    for i in range(NTT):
        s = i % 2
        w = [(vrm, vrm_after[i - 2])] if i >= 2 else []
        p.dma("sync", xnt[s][:], xn_in[i * 128:(i + 1) * 128, :], wait=w, inc=ldx[s])
        ldx_after[i] = ldx[s].n
        p.do("scalar", (lambda s, i: lambda e: e.activation(
            sq[:], xnt[s][:], AF.Square, accum_out=ssc[:, i:i + 1]))(s, i),
            wait=[(ldx[s], ldx_after[i])], inc=srm)
        p.do("scalar", (lambda i: lambda e: e.activation(
            rstd[:, i:i + 1], ssc[:, i:i + 1], AF.Sqrt, bias=eps_ap, scale=1.0 / D))(i),
            wait=p.cwait(), inc=srm)
        p.do("vector", (lambda i: lambda e: e.reciprocal(
            rstd[:, i:i + 1], rstd[:, i:i + 1]))(i), wait=[(srm, srm.n)])
        p.do("vector", (lambda s, i: lambda e: e.scalar_tensor_tensor(
            xrt[s][:], xnt[s][:], rstd[:, i:i + 1], screp[:], ALU.mult, ALU.mult))(s, i),
            wait=[(lig, 32)], inc=vrm)
        vrm_after[i] = vrm.n
        for j in range(NDT):
            w = [(vrm, vrm_after[i]), (li, li_n)]
            if i >= 2 and j == 0:
                w.append((evq, evq_after[i - 2]))
            p.do("tensor", (lambda s, i, j: lambda e: e.transpose(
                ps_t[i % 2][:, j * 128:(j + 1) * 128], xrt[s][:, j * 128:(j + 1) * 128],
                idn[:]))(s, i, j), wait=(w if j == 0 else []), inc=(pet if j == NDT - 1 else None))
        pet_after[i] = pet.n
        p.do("vector", (lambda i: lambda e: e.tensor_copy(
            xndv[:, :, i * 128:(i + 1) * 128],
            ps_t[i % 2][:].rearrange("p (dt c) -> p dt c", c=128)))(i),
            wait=[(pet, pet_after[i])], inc=evq)
        evq_after[i] = evq.n
    evq_n = evq.n

    per = p.sem("per"); vco = p.sem("vco"); sac = p.sem("sac")
    vco_tile = {}
    for i in range(NTT):
        for k in range(8):
            w = [(evq, evq_after[i]), (li, li_n), (lig, 32)] if k == 0 else []
            if k == 0 and i >= 1:
                w.append((vco, vco_tile[i - 1]))
            p.do("tensor", (lambda i, k: lambda e: e.matmul(
                ps_r[:], xndv[:, k, i * 128:(i + 1) * 128],
                wrb[:, k * 8:(k + 1) * 8], start=(k == 0), stop=(k == 7)))(i, k),
                wait=w, inc=(per if k == 7 else None))
        pc = per.n
        p.do("vector", (lambda i: lambda e: e.tensor_copy(scs[:], ps_r[:]))(i),
             wait=[(per, pc)], inc=vco)
        p.do("vector", lambda e: e.tensor_reduce(m1[:], scs[:], AX.X, ALU.max), inc=vco)
        p.do("vector", lambda e: e.tensor_scalar(eqm[:], scs[:], m1[:], None, ALU.is_ge), inc=vco)
        p.do("vector", lambda e: e.scalar_tensor_tensor(
            eqm[:], eqm[:], -1e9, scs[:], ALU.mult, ALU.add), inc=vco)
        p.do("vector", lambda e: e.tensor_reduce(m2[:], eqm[:], AX.X, ALU.max), inc=vco)
        vco_c5 = vco.n
        p.do("scalar", (lambda i: lambda e: e.activation(
            prb[:], scs[:], AF.Exp, bias=0.0, scale=1.0))(i), wait=[(vco, vco_c5)])
        p.do("scalar", (lambda i: lambda e: e.activation(
            sq[:, 0:8], prb[:], AF.Copy, accum_out=m1[:]))(i), inc=sac)
        p.do("vector", lambda e: e.reciprocal(m1[:], m1[:]), wait=[(sac, sac.n)])
        p.do("vector", lambda e: e.tensor_scalar(prb[:], prb[:], m1[:], None, ALU.mult))
        p.do("vector", lambda e: e.tensor_scalar(eqm[:], scs[:], m2[:], None, ALU.is_ge))
        p.do("vector", (lambda i: lambda e: e.tensor_tensor(
            coef[:, i * 8:(i + 1) * 8], prb[:], eqm[:], ALU.mult))(i), inc=vco)
        vco_tile[i] = vco.n
    vco_n = vco.n

    CH = [(0, 512), (512, 512), (1024, 512), (1536, 512), (2048, 512), (2560, 256)]
    lwv = [p.sem(f"lwv{i}") for i in range(2)]
    peu = p.sem("peu"); evg = p.sem("evg"); ssl = p.sem("ssl")
    pey = p.sem("pey"); evy = p.sem("evy")
    gi = 0
    ci = 0
    evy_after = {}; peu_after = {}; evg_after = {}; pey_after = {}
    lwv_after = {}
    lr = [p.sem(f"lr{i}") for i in range(2)]
    lx2 = [p.sem(f"lx2{i}") for i in range(2)]
    stx = [p.sem(f"stx{i}") for i in range(2)]
    vfa = p.sem("vfa")
    vfa_after = {}; stx_after = {}
    evy_half = {}

    def emit_fadd(half, t):
        s = t % 2
        tt = half * 16 + t
        g = tt
        w = [(vfa, vfa_after[g - 2])] if g >= 2 else [(evq, evq_n)]
        p.dma("sync", rt[s][:], r2in[tt * 128:(tt + 1) * 128, :], wait=w, inc=lr[s])
        wx = [(vrm, vrm.n)] if g < 2 else []
        p.dma("sync", xnt[s][:], xn_in[tt * 128:(tt + 1) * 128, :], wait=wx, inc=lx2[s])
        w = [(lr[s], lr[s].n), (lx2[s], lx2[s].n), (evy, evy_half[half])]
        if g >= 2:
            w.append((stx[s], stx_after[g - 2]))
        p.do("gpsimd", (lambda s, t: lambda e: e.tensor_tensor(
            ot[s][:], rt[s][:], yh[:].rearrange("p (t d) -> p t d", d=D)[:, t, :],
            ALU.add))(s, t), wait=w)
        p.do("gpsimd", (lambda s: lambda e: e.tensor_tensor(
            ot[s][:], ot[s][:], xnt[s][:], ALU.add))(s), inc=vfa)
        vfa_after[g] = vfa.n
        p.dma("sync", x3[tt * 128:(tt + 1) * 128, :], ot[s][:],
              wait=[(vfa, vfa_after[g])], inc=stx[s])
        stx_after[g] = stx[s].n

    dcnt = 0
    pey_tg = {}
    gtg = 0

    def emit_downs(pend):
        nonlocal dcnt
        ci_, s_, tg_, gtg_, evg_l, nfs_, first_, half_, _e2 = pend
        if half_ == 1 and first_:
            for ts4 in range(4):
                emit_fadd(0, tg_ * 4 + ts4)
        tb_ = half_ * 16
        for chh in range(2):
            for ts in range(4):
                bank = ps_y[(chh * 4 + ts) % 2]
                for fs in range(nfs_):
                    w = []
                    if fs == 0:
                        w = [(evg, evg_l)]
                        if dcnt >= 2:
                            w.append((evy, evy.n - 1))
                    p.do("tensor", (lambda s_, fs, chh, ts, bank, nfs_, gtg_:
                                    lambda e: e.matmul(
                        bank[:], gbufs[gtg_ % 2][:, fs * 512 + ts * 128: fs * 512 + (ts + 1) * 128],
                        wdnb[s_][:].rearrange("p (f c) -> p f c", c=D)[:, fs, chh * 512:(chh + 1) * 512],
                        start=(fs == 0), stop=(fs == nfs_ - 1)))(s_, fs, chh, ts, bank, nfs_, gtg_),
                        wait=w, inc=(pey if fs == nfs_ - 1 else None))
                dcnt += 1
                pyc = pey.n
                ti_g = tb_ + tg_ * 4 + ts
                wv = [(pey, pyc), (vco, vco_tile[ti_g])]
                if first_ and half_ == 1:
                    wv.append((vfa, vfa_after[tg_ * 4 + ts]))
                if first_:
                    p.do("vector", (lambda bank, ti_g, e2, tg_, ts, chh:
                                    lambda e: e.tensor_scalar(
                        yh[:].rearrange("p (t d) -> p t d", d=D)[:, tg_ * 4 + ts, chh * 512:(chh + 1) * 512],
                        bank[:], coef[:, ti_g * 8 + e2: ti_g * 8 + e2 + 1],
                        None, ALU.mult))(bank, ti_g, pend[8], tg_, ts, chh),
                        wait=wv, inc=evy)
                else:
                    p.do("vector", (lambda bank, ti_g, e2, tg_, ts, chh:
                                    lambda e: e.scalar_tensor_tensor(
                        yh[:].rearrange("p (t d) -> p t d", d=D)[:, tg_ * 4 + ts, chh * 512:(chh + 1) * 512],
                        bank[:], coef[:, ti_g * 8 + e2: ti_g * 8 + e2 + 1],
                        yh[:].rearrange("p (t d) -> p t d", d=D)[:, tg_ * 4 + ts, chh * 512:(chh + 1) * 512],
                        ALU.mult, ALU.add))(bank, ti_g, pend[8], tg_, ts, chh),
                        wait=wv, inc=evy)
        pey_tg[gtg_] = pey.n

    for half in range(2):
        tb = half * 16
        pend = None
        for e_ in range(8):
            for (c0, cw) in CH:
                s = ci % 2
                nfs = cw // 128
                w = []
                if ci >= 2:
                    w.append((peu, peu_after[ci - 2]))
                    w.append((pey, pey_tg[gtg - 5]))
                p.dma("gpsimd", wval[s][:, 0:8 * cw].rearrange("p (k c) -> p k c", c=cw),
                      wupT[e_][:, c0:c0 + cw].rearrange("(k p) c -> p k c", p=128),
                      wait=w, inc=lwv[s])
                p.dma("gpsimd", wgat[s][:, 0:8 * cw].rearrange("p (k c) -> p k c", c=cw),
                      wupT[e_][:, 2816 + c0:2816 + c0 + cw].rearrange("(k p) c -> p k c", p=128),
                      inc=lwv[s])
                p.dma("gpsimd", wdnb[s][:, 0:nfs * D].rearrange("p (f c) -> p f c", c=D),
                      wdnT[e_][c0:c0 + cw, :].rearrange("(f p) c -> p f c", p=128),
                      inc=lwv[s])
                lwv_after[ci] = lwv[s].n
                for tg in range(4):
                    tok0 = half * 2048 + tg * 512
                    for fs in range(nfs):
                        sel = gi % 2
                        for which, wt in [(0, wval), (1, wgat)]:
                            bank = pu[sel][which]
                            for k in range(8):
                                w = []
                                if k == 0:
                                    w = [(lwv[s], lwv_after[ci]),
                                         (evq, evq_after[half * 16 + tg * 4 + 3])]
                                    if gi >= 2:
                                        w.append((evg, evg_after[gi - 2]))
                                p.do("tensor", (lambda s, k, fs, cw, tok0, wt, bank:
                                                lambda e: e.matmul(
                                    bank[:], wt[s][:].rearrange("p (k c) -> p k c", c=cw)[:, k, fs * 128:(fs + 1) * 128],
                                    xndv[:, k, tok0:tok0 + 512],
                                    start=(k == 0), stop=(k == 7)))(s, k, fs, cw, tok0, wt, bank),
                                    wait=w, inc=(peu if k == 7 else None))
                        peu_after[ci] = peu.n
                        wg = [(peu, peu.n)]
                        p.do("scalar", (lambda sel: lambda e: e.activation(
                            sgb[:], pu[sel][1][:], AF.Silu))(sel),
                            wait=wg, inc=ssl)
                        wm = [(ssl, ssl.n)]
                        if fs == 0 and gtg >= 2:
                            wm.append((pey, pey_tg[gtg - 2]))
                        p.do("vector", (lambda fs, sel, gtg: lambda e: e.tensor_tensor(
                            gbufs[gtg % 2][:, fs * 512:(fs + 1) * 512], pu[sel][0][:], sgb[:],
                            ALU.mult))(fs, sel, gtg), wait=wm, inc=evg)
                        evg_after[gi] = evg.n
                        gi += 1
                    if pend is not None:
                        emit_downs(pend)
                    pend = (ci, s, tg, gtg, evg_after[gi - 1], nfs,
                            (e_ == 0 and c0 == 0), half, e_)
                    gtg += 1
                pey_after[ci] = pey.n
                ci += 1
        emit_downs(pend)
        pend = None
        evy_half[half] = evy.n
    for t in range(16):
        emit_fadd(1, t)
    p.do("sync", lambda e: e.nop(), wait=[(s_, s_.n) for s_ in stx])
    return p
L3A_PAIRS = 8
L3A_PROJ = True
L3A_STAGE = "full"
L3A_DEBUG = False


def build_l3a(p=None):
    if p is None:
        p = Prog("l3a")
    x = p.inp("x", [T, D])
    wqkvT = p.inp("w_qkvT", [D, 3 * D])
    wprT = p.inp("w_projT", [D, D])
    qnw = p.inp("q_norm_w", [64])
    knw = p.inp("k_norm_w", [64])
    costab = p.inp("costab", [T, 32])
    sintab = p.inp("sintab", [T, 32])
    cmask = p.inp("cmask", [128, 896])
    idn_in = p.inp("idn", [128, 128], BF16)
    idnf_in = p.inp("idnf", [128, 128])
    xnd_o = p.out("xn_d", [D, T], BF16)
    r3 = p.out("r3", [T, D])
    if L3A_DEBUG:
        dbg_qd = p.out("dbg_qd", [128, T], BF16)
        dbg_kd = p.out("dbg_kd", [128, T], BF16)
        dbg_ao = p.out("dbg_ao", [T, D], BF16)

    idn = p.sb("idn_s", [128, 128], BF16)
    idnf = p.sb("idnf_s", [128, 128], F32)
    xn_d = p.sb("xn_d_s", [128, NDT * T], BF16)
    ao = p.sb("ao", [128, NTT * D], BF16)
    cosb = p.sb("cosb", [128, NTT * 32], BF16)
    sinb = p.sb("sinb", [128, NTT * 32], BF16)
    cmb = p.sb("cmb", [128, 896], BF16)
    wnr = p.sb("wnr", [128, 256], F32)
    wcb = [p.sb(f"wcb{i}", [128, 8 * 128], BF16) for i in range(3)]
    qtok = p.sb("qtok", [128, NTT * 128], BF16)
    ktok = p.sb("ktok", [128, NTT * 128], BF16)
    qro = p.sb("qro", [128, NTT * 128], BF16)
    kro = p.sb("kro", [128, NTT * 128], BF16)
    q_d = p.sb("q_d", [128, T], BF16)
    k_d = p.sb("k_d", [128, T], BF16)
    vx = p.sb("vx", [128, NTT * 130], BF16)
    sscr = p.sb("sscr", [128, 512], BF16)
    rs2 = p.sb("rs2", [128, 8], F32)
    pT = [p.sb(f"pT{i}", [128, 1024], BF16) for i in range(2)]
    o65 = p.sb("o65", [65, 512], F32)
    rec = [p.sb(f"rec{i}", [128, 1], F32) for i in range(2)]
    aodt = [None, None]
    xt = [None, None]
    otl = [None, None]
    ps_d = [p.ps(f"ps_d{i}", [128, 1024], F32) for i in range(2)]
    ps_s = [ps_d[0][:, 0:512], ps_d[0][:, 512:1024]]
    ps_v = [p.ps(f"ps_v{i}", [65, 512], F32) for i in range(2)]
    ps_oo = [p.ps("ps_oo0", [128, 4 * 65], F32), None]
    ps_oo[1] = ps_oo[0]

    li = p.sem("li"); lig = p.sem("lig")
    p.dma("sync", idn[:], idn_in, inc=li)
    p.dma("sync", idnf[:], idnf_in, inc=li)
    p.dma("gpsimd", cmb[:], cmask, inc=lig)
    p.dma("gpsimd", cosb[:].rearrange("p (t c) -> p t c", c=32),
          costab.rearrange("(t p) c -> p t c", p=128), inc=lig)
    p.dma("gpsimd", sinb[:].rearrange("p (t c) -> p t c", c=32),
          sintab.rearrange("(t p) c -> p t c", p=128), inc=lig)
    for rep in range(2):
        p.dma("gpsimd", wnr[:, rep * 64:(rep + 1) * 64],
              qnw.unsqueeze(0).partition_broadcast(128).squeeze(1), inc=lig)
        p.dma("gpsimd", wnr[:, 128 + rep * 64: 128 + (rep + 1) * 64],
              knw.unsqueeze(0).partition_broadcast(128).squeeze(1), inc=lig)
    li_n, lig_n = li.n, lig.n
    p.cbias(-10.0)
    eps6_ap = p.cbias(1e-6)

    ln_bufs = ([qro[:, 0:2048].bitcast(F32), qro[:, 2048:4096].bitcast(F32)],
               [kro[:, 0:2048].bitcast(F32), kro[:, 2048:4096].bitcast(F32)],
               [qtok[:, 0:1024], qtok[:, 1024:2048]])
    evq, evq_n, _ = emit_ln(p, x, xn_d, idn, "ln_", idn_wait=[(li, li_n)],
                            bufs=ln_bufs, ps_t_bufs=1)
    ps_t = p._ln_ps_t
    xnd = xn_d[:].rearrange("p (dt t) -> p dt t", t=T)
    sxn = [p.sem(f"sxn{i}") for i in range(2)]
    for k in range(NDT):
        p.dma("gpsimd", xnd_o[k * 128:(k + 1) * 128, :], xnd[:, k, :],
              wait=[(evq, evq_n)], inc=sxn[k % 2])

    lw = [p.sem(f"lw{i}") for i in range(3)]
    pep = p.sem("pep"); evp = p.sem("evp"); vnr = p.sem("vnr"); snr = p.sem("snr")
    peq = p.sem("peq"); evdq = p.sem("evdq")
    pes = p.sem("pes"); sex = p.sem("sex"); pev = p.sem("pev")
    evv = p.sem("evv"); peo = p.sem("peo"); vdv = p.sem("vdv")
    vmk = p.sem("vmk")
    msv = p.sem("msv")
    p.do("vector", lambda e: e.memset(
        vx[:].rearrange("p (t h c) -> p t h c", h=2, c=65)[:, :, :, 64:65], 1.0), inc=msv)

    cosv = cosb[:].rearrange("p (t c) -> p t c", c=32)
    sinv = sinb[:].rearrange("p (t c) -> p t c", c=32)
    pes_end = {}; pev_end = {}
    sexi = 0
    sex_at = {}
    gq = 0
    evv_after = {}; peo_after = {}; vdv_after = {}
    evdq_slot = {}; evdq_prev_pair = 0
    for hp in range(L3A_PAIRS):
        wq_l = []
        if hp >= 1:
            wq_l = [(pes, pes_end[hp - 1]), (pev, pev_end[hp - 1])]
        for wi, coff in [(0, hp * 128), (1, 1024 + hp * 128), (2, 2048 + hp * 128)]:
            p.dma("gpsimd", wcb[wi][:].rearrange("p (k c) -> p k c", c=128),
                  wqkvT[:, coff:coff + 128].rearrange("(k p) c -> p k c", p=128),
                  wait=(wq_l if wi == 0 else []), inc=lw[wi])
        lw_n = [lw[i].n for i in range(3)]
        pj = 0
        evp_at = {}
        for tt in range(NTT):
            for wi in range(3):
                sl_ = pj % 2
                for k in range(8):
                    w = []
                    if k == 0:
                        w = [(lw[wi], lw_n[wi]), (evq, evq_n)]
                        if hp >= 1 and tt == 0 and wi == 0:
                            w.append((pev, pev_end[hp - 1]))
                        if pj >= 2:
                            w.append((evp, evp_at[pj - 2]))
                    p.do("tensor", (lambda k, tt, wi, sl_: lambda e: e.matmul(
                        ps_d[sl_][:, 0:128], xnd[:, k, tt * 128:(tt + 1) * 128],
                        wcb[wi][:, k * 128:(k + 1) * 128], start=(k == 0), stop=(k == 7)))(k, tt, wi, sl_),
                        wait=w, inc=(pep if k == 7 else None))
                dst = [qtok, ktok, None][wi]
                if wi < 2:
                    p.do("scalar", (lambda dst, tt, sl_: lambda e: e.activation(
                        dst[:, tt * 128:(tt + 1) * 128], ps_d[sl_][:, 0:128], AF.Copy))(dst, tt, sl_),
                        wait=[(pep, pep.n)], inc=evp)
                else:
                    p.do("scalar", (lambda tt, sl_: lambda e: e.activation(
                        vx[:].rearrange("p (t h c) -> p t h c", h=2, c=65)[:, tt, :, 0:64],
                        ps_d[sl_][:, 0:128].rearrange("p (h c) -> p h c", c=64), AF.Copy))(tt, sl_),
                        wait=[(pep, pep.n), (msv, 1)], inc=evp)
                evp_at[pj] = evp.n
                pj += 1
        evp_c = evp.n
        if L3A_STAGE == "proj":
            pes_end[hp] = pes.n; pev_end[hp] = pev.n
            continue
        for tg4 in range(NTT // 4):
            for qi, (tok, ro, woff) in enumerate([(qtok, qro, 0), (ktok, kro, 128)]):
                tv = tok[:, tg4 * 512:(tg4 + 1) * 512]
                rv = ro[:, tg4 * 512:(tg4 + 1) * 512]
                w = [(evp, evp_c), (lig, lig_n)]
                p.do("vector", (lambda tv: lambda e: e.tensor_tensor(
                    sscr[:], tv, tv, ALU.mult))(tv), wait=w)
                p.do("vector", (lambda: lambda e: e.tensor_reduce(
                    rs2[:], sscr[:].rearrange("p (g c) -> p g c", c=64), AX.X, ALU.add))(),
                    inc=vnr)
                p.do("scalar", (lambda: lambda e: e.activation(
                    rs2[:], rs2[:], AF.Sqrt, bias=eps6_ap, scale=1.0 / 64))(),
                    wait=[(vnr, vnr.n)] + p.cwait(), inc=snr)
                p.do("vector", (lambda: lambda e: e.reciprocal(rs2[:], rs2[:]))(),
                     wait=[(snr, snr.n)])
                p.do("vector", (lambda tv: lambda e: e.tensor_tensor(
                    sscr[:].rearrange("p (g c) -> p g c", c=64),
                    tv.rearrange("p (g c) -> p g c", c=64),
                    rs2[:].unsqueeze(2).broadcast_to((128, 8, 64)), ALU.mult))(tv))
                p.do("vector", (lambda woff: lambda e: e.tensor_tensor(
                    sscr[:].rearrange("p (t c) -> p t c", c=128),
                    sscr[:].rearrange("p (t c) -> p t c", c=128),
                    wnr[:, woff:woff + 128].unsqueeze(1).broadcast_to((128, 4, 128)),
                    ALU.mult))(woff))
                sv = sscr[:].rearrange("p (t h x) -> p t h x", h=2, x=64)
                rvv = rv.rearrange("p (t h x) -> p t h x", h=2, x=64)
                cb = cosv[:, tg4 * 4:(tg4 + 1) * 4, :].unsqueeze(2).broadcast_to((128, 4, 2, 32))
                sb2 = sinv[:, tg4 * 4:(tg4 + 1) * 4, :].unsqueeze(2).broadcast_to((128, 4, 2, 32))
                q1 = sv[:, :, :, 0:32]; q2 = sv[:, :, :, 32:64]
                o1 = rvv[:, :, :, 0:32]; o2 = rvv[:, :, :, 32:64]
                p.do("vector", (lambda o1, q1, cb: lambda e: e.tensor_tensor(
                    o1, q1, cb, ALU.mult))(o1, q1, cb))
                p.do("vector", (lambda o2, q1, sb2: lambda e: e.tensor_tensor(
                    o2, q1, sb2, ALU.mult))(o2, q1, sb2))
                p.do("vector", (lambda q1, q2, cb: lambda e: e.tensor_tensor(
                    q1, q2, cb, ALU.mult))(q1, q2, cb))
                p.do("vector", (lambda o2, q1: lambda e: e.tensor_tensor(
                    o2, q1, o2, ALU.add))(o2, q1))
                p.do("vector", (lambda q1, q2, sb2: lambda e: e.tensor_tensor(
                    q1, q2, sb2, ALU.mult))(q1, q2, sb2))
                p.do("vector", (lambda o1, q1: lambda e: e.tensor_tensor(
                    o1, o1, q1, ALU.subtract))(o1, q1), inc=vnr)
        vnr_c = vnr.n
        if L3A_STAGE == "rope":
            pes_end[hp] = pes.n; pev_end[hp] = pev.n
            continue
        for qi, (ro, dmat) in enumerate([(qro, q_d), (kro, k_d)]):
            for tg4 in range(NTT // 4):
                for j4 in range(4):
                    tt = tg4 * 4 + j4
                    w = []
                    if j4 == 0:
                        w = [(vnr, vnr_c), (li, li_n)]
                        if hp >= 1 and tg4 == 0 and qi == 0:
                            w.append((pes, pes_end[hp - 1]))
                        if tg4 >= 1:
                            w.append((evdq, evdq_slot[(qi, tg4 - 1)]))
                        elif qi == 1:
                            w.append((evdq, evdq_slot[(0, NTT // 4 - 1)]))
                        elif hp >= 1:
                            w.append((evdq, evdq_prev_pair))
                    p.do("tensor", (lambda ro, tt, qi, j4: lambda e: e.transpose(
                        ps_t[qi][:, j4 * 128:(j4 + 1) * 128],
                        ro[:, tt * 128:(tt + 1) * 128], idn[:]))(ro, tt, qi, j4),
                        wait=w, inc=(peq if j4 == 3 else None))
                pq = peq.n
                p.do("vector", (lambda dmat, tg4, qi: lambda e: e.tensor_copy(
                    dmat[:, tg4 * 512:(tg4 + 1) * 512],
                    ps_t[qi][:, 0:512]))(dmat, tg4, qi),
                    wait=[(peq, pq)], inc=evdq)
                evdq_slot[(qi, tg4)] = evdq.n
        evdq_c = evdq.n
        evdq_prev_pair = evdq.n
        if L3A_STAGE == "trans":
            pes_end[hp] = pes.n; pev_end[hp] = pev.n
            continue
        vxv = vx[:].rearrange("p (t c) -> p t c", c=130)
        for hh in range(2):
            h = hp * 2 + hh
            dro = hh * 64
            for qcp in range(4):
                qa, qb = 2 * qcp, 2 * qcp + 1
                ksa, ksb = 4 * qa + 4, 4 * qb + 4
                for k in range(ksb + 1):
                    if k < ksb:
                        sl = sexi % 2
                        both = k < ksa
                        w = [(evdq, evdq_c)]
                        if sexi >= 2:
                            w.append((sex, sex_at[sexi - 2]))
                        if both:
                            p.do("tensor", (lambda dro, k, qa, sl: lambda e: e.matmul(
                                ps_d[sl][:, 0:512],
                                k_d[:, k * 128:(k + 1) * 128][dro:dro + 64, :],
                                q_d[:, qa * 512:(qa + 1) * 512][dro:dro + 64, :],
                                start=True, stop=True))(dro, k, qa, sl),
                                wait=w)
                            w = []
                        p.do("tensor", (lambda dro, k, qb, sl: lambda e: e.matmul(
                            ps_d[sl][:, 512:1024],
                            k_d[:, k * 128:(k + 1) * 128][dro:dro + 64, :],
                            q_d[:, qb * 512:(qb + 1) * 512][dro:dro + 64, :],
                            start=True, stop=True))(dro, k, qb, sl),
                            wait=w, inc=pes)
                        pes_c = pes.n
                        wexp = [(pes, pes_c)]
                        mq = qa if (both and k >= 4 * qa) else (qb if k >= 4 * qb else None)
                        if mq is not None:
                            off = 0 if mq == qa else 512
                            o_ = k * 128 - mq * 512
                            p.do("vector", (lambda sl, o_, off: lambda e: e.tensor_tensor(
                                ps_d[sl][:, off:off + 512], ps_d[sl][:, off:off + 512],
                                cmb[:, 384 - o_:384 - o_ + 512], ALU.add))(sl, o_, off),
                                wait=[(pes, pes_c), (lig, lig_n)], inc=vmk)
                            wexp = [(vmk, vmk.n)]
                        wexp = wexp + p.cwait()
                        if sexi >= 2:
                            wexp.append((pev, pev.n))
                        if both:
                            p.do("scalar", (lambda sl: lambda e: e.activation(
                                pT[sl][:], ps_d[sl][:], AF.Exp, bias=-10.0,
                                scale=0.125))(sl), wait=wexp, inc=sex)
                        else:
                            p.do("scalar", (lambda sl: lambda e: e.activation(
                                pT[sl][:, 512:1024], ps_d[sl][:, 512:1024], AF.Exp,
                                bias=-10.0, scale=0.125))(sl), wait=wexp, inc=sex)
                        sex_at[sexi] = sex.n
                    if k >= 1:
                        kv = k - 1
                        slv = (sexi - 1) % 2
                        bothv = kv < ksa
                        w = [(sex, sex_at[sexi - 1]), (msv, 1)]
                        if kv == 0 and gq >= 2:
                            w.append((evv, gq))
                        if bothv:
                            p.do("tensor", (lambda hh, kv, slv, st_, sp_: lambda e: e.matmul(
                                ps_v[0][:], vxv[:, kv, hh * 65:(hh + 1) * 65],
                                pT[slv][:, 0:512], start=st_, stop=sp_))(
                                hh, kv, slv, kv == 0, kv == ksa - 1),
                                wait=w)
                            w = []
                        p.do("tensor", (lambda hh, kv, slv, st_, sp_: lambda e: e.matmul(
                            ps_v[1][:], vxv[:, kv, hh * 65:(hh + 1) * 65],
                            pT[slv][:, 512:1024], start=st_, stop=sp_))(
                            hh, kv, slv, kv == 0, kv == ksb - 1),
                            wait=w, inc=(pev if kv == ksb - 1 else None))
                    if k < ksb:
                        sexi += 1
                pevc = pev.n
                for si, qq in ((0, qa), (1, qb)):
                    w = [(pev, pevc)]
                    if gq >= 1:
                        w.append((peo, peo_after[gq - 1]))
                    p.do("vector", (lambda si: lambda e: e.tensor_copy(
                        o65[:], ps_v[si][:]))(si), wait=w, inc=evv)
                    evv_after[gq] = evv.n
                    for qs in range(4):
                        w = [(evv, evv_after[gq]), (li, li_n)]
                        if gq >= 1 and qs == 0:
                            w.append((vdv, vdv_after[gq - 1]))
                        p.do("tensor", (lambda qs: lambda e: e.transpose(
                            ps_oo[0][:, qs * 65:(qs + 1) * 65],
                            o65[:, qs * 128:(qs + 1) * 128], idnf[0:65, 0:65]))(qs),
                            wait=w, inc=(peo if qs == 3 else None))
                    peo_after[gq] = peo.n
                    for qs in range(4):
                        w = [(peo, peo_after[gq])]
                        p.do("vector", (lambda qs: lambda e: e.reciprocal(
                            rec[0][:], ps_oo[0][:, qs * 65 + 64:qs * 65 + 65]))(qs),
                            wait=w)
                        p.do("vector", (lambda qs, h, qq: lambda e: e.tensor_scalar(
                            ao[:].rearrange("p (t d) -> p t d", d=D)
                            [:, qq * 4 + qs, h * 64:(h + 1) * 64],
                            ps_oo[0][:, qs * 65:qs * 65 + 64],
                            rec[0][:], None, ALU.mult))(qs, h, qq), inc=vdv)
                    vdv_after[gq] = vdv.n
                    gq += 1
        pes_end[hp] = pes.n
        pev_end[hp] = pev.n

    if not L3A_PROJ:
        fin = p.sem("fin")
        p.do("vector", lambda e: e.tensor_copy(ao[:, 0:4], ao[:, 0:4]),
             wait=[(vdv, vdv.n)], inc=fin)
        p.dma("gpsimd", r3[0:128, 0:64], ao[:, 0:128].bitcast(F32), wait=[(fin, 1)],
              inc=p.sem("strx"))
        p.do("sync", lambda e: e.nop(), wait=[(s_, s_.n) for s_ in sxn])
        return p
    lwp = p.sem("lwp")
    p.dma("gpsimd", qtok[:].rearrange("p (k o) -> p k o", o=D),
          wprT[0:512, :].rearrange("(k p) o -> p k o", p=128),
          wait=[(pes, pes.n), (pev, pev.n), (vnr, vnr.n)], inc=lwp)
    p.dma("gpsimd", ktok[:].rearrange("p (k o) -> p k o", o=D),
          wprT[512:1024, :].rearrange("(k p) o -> p k o", p=128), inc=lwp)
    lwp_n = lwp.n

    def wpv_ap(k, oc):
        t = qtok if k < 4 else ktok
        return t[:, (k % 4) * 1024 + oc * 512:(k % 4) * 1024 + oc * 512 + 512]
    aodt[0] = k_d[:, 0:1024]
    aodt[1] = k_d[:, 1024:2048]
    xt[0] = qro[:, 0:2048].bitcast(F32)
    xt[1] = qro[:, 2048:4096].bitcast(F32)
    otl[0] = kro[:, 0:2048].bitcast(F32)
    otl[1] = kro[:, 2048:4096].bitcast(F32)
    evdq_fin = evdq.n
    ldx = [p.sem(f"ldx2{i}") for i in range(2)]
    peT = p.sem("peT"); evT = p.sem("evT"); pem = p.sem("pem"); evo = p.sem("evo")
    str_ = [p.sem(f"str{i}") for i in range(2)]
    peT_after = {}; evT_after = {}; pem_after = {}; evo_after = {}; str_after = {}
    aov = ao[:].rearrange("p (t d) -> p t d", d=D)
    for tt in range(NTT):
        s = tt % 2
        p.dma("sync", xt[s], x[tt * 128:(tt + 1) * 128, :],
              wait=([(evo, evo_after[tt - 2])] if tt >= 2 else [(evdq, evdq_fin)]),
              inc=ldx[s])
        for j in range(8):
            w = []
            if j == 0:
                w = [(vdv, vdv.n), (li, li_n), (lig, lig_n)]
                if tt >= 1:
                    w.append((evT, evT_after[tt - 1]))
            p.do("tensor", (lambda s, j, tt: lambda e: e.transpose(
                ps_t[s][:, j * 128:(j + 1) * 128],
                aov[:, tt, j * 128:(j + 1) * 128], idn[:]))(s, j, tt),
                wait=w, inc=(peT if j == 7 else None))
        peT_after[tt] = peT.n
        p.do("vector", (lambda s: lambda e: e.tensor_copy(
            aodt[s], ps_t[s][:]))(s), wait=[(peT, peT_after[tt])], inc=evT)
        evT_after[tt] = evT.n
        for oc in range(2):
            bank = ps_s[oc]
            for k in range(8):
                w = []
                if k == 0:
                    w = [(evT, evT_after[tt]), (sex, sexi), (lwp, lwp_n)]
                    if tt >= 1:
                        w.append((evo, evo_after.get(tt - 1, 0)))
                p.do("tensor", (lambda s, k, oc, bank: lambda e: e.matmul(
                    bank[:], aodt[s][:, k * 128:(k + 1) * 128],
                    wpv_ap(k, oc), start=(k == 0), stop=(k == 7)))(
                    s, k, oc, bank), wait=w, inc=(pem if k == 7 else None))
        pem_after[tt] = pem.n
        w = [(pem, pem_after[tt]), (ldx[s], ldx[s].n)]
        if tt >= 2:
            w.append((str_[s], str_after[tt - 2]))
        p.do("vector", (lambda s: lambda e: e.tensor_tensor(
            otl[s][:, 0:512], ps_s[0][:], xt[s][:, 0:512], ALU.add))(s), wait=w)
        p.do("vector", (lambda s: lambda e: e.tensor_tensor(
            otl[s][:, 512:1024], ps_s[1][:], xt[s][:, 512:1024], ALU.add))(s), inc=evo)
        evo_after[tt] = evo.n
        p.dma("gpsimd", r3[tt * 128:(tt + 1) * 128, :], otl[s],
              wait=[(evo, evo_after[tt])], inc=str_[s])
        str_after[tt] = str_[s].n
    p.do("sync", lambda e: e.nop(),
         wait=[(s_, s_.n) for s_ in str_ + sxn])
    return p


def build_l3b(p=None):
    if p is None:
        p = Prog("l3b")
    xnd_in = p.inp("xn_d", [D, T], BF16)
    wr_in = p.inp("Wr", [D, 8])
    w1_in = p.inp("W1", [8, D, 4096])
    w2_in = p.inp("W2", [8, 4096, D])
    r3in = p.inp("r3", [T, D])
    x4 = p.out("x4", [T, D])

    xn_d = p.sb("xn_d_s", [128, NDT * T], BF16)
    yh = p.sb("yh", [128, 16 * D], F32)
    coef = p.sb("coef", [128, NTT * 8], F32)
    scs = p.sb("scs", [128, 8], F32)
    m1 = p.sb("m1", [128, 1], F32)
    m2 = p.sb("m2", [128, 1], F32)
    eqm = p.sb("eqm", [128, 8], F32)
    prb = p.sb("prb", [128, 8], F32)
    sqs = p.sb("sqs", [128, 8], F32)
    wrb = p.sb("wrb", [128, 8 * 8], BF16)
    w1b = [p.sb(f"w1b{i}", [128, 8 * 512], BF16) for i in range(2)]
    w2b = [p.sb(f"w2b{i}", [128, 4 * D], BF16) for i in range(2)]
    hbufs = [p.sb(f"hb{i}", [128, 4 * 512], BF16) for i in range(2)]
    rt = [p.sb(f"rt{i}", [128, D], F32) for i in range(2)]
    ot = [p.sb(f"ot{i}", [128, D], F32) for i in range(2)]
    ps_u = [p.ps(f"ps_u{i}", [128, 512], F32) for i in range(2)]
    ps_y = [p.ps(f"ps_y{i}", [128, 512], F32) for i in range(2)]
    ps_r = p.ps("ps_r", [128, 8], F32)

    lig = p.sem("lig"); lxd = p.sem("lxd")
    p.dma("gpsimd", wrb[:].rearrange("p (k c) -> p k c", c=8),
          wr_in.rearrange("(k p) c -> p k c", p=128), inc=lig)
    for k in range(NDT):
        p.dma("sync", xn_d[:].rearrange("p (dt t) -> p dt t", t=T)[:, k, :],
              xnd_in[k * 128:(k + 1) * 128, :], inc=lxd)
    lig_n, lxd_n = lig.n, lxd.n
    xndv = xn_d[:].rearrange("p (dt t) -> p dt t", t=T)

    per = p.sem("per"); vco = p.sem("vco"); sac = p.sem("sac")
    vco_tile = {}
    for i in range(NTT):
        for k in range(8):
            w = [(lxd, lxd_n), (lig, lig_n)] if k == 0 else []
            if k == 0 and i >= 1:
                w.append((vco, vco_tile[i - 1]))
            p.do("tensor", (lambda i, k: lambda e: e.matmul(
                ps_r[:], xndv[:, k, i * 128:(i + 1) * 128],
                wrb[:, k * 8:(k + 1) * 8], start=(k == 0), stop=(k == 7)))(i, k),
                wait=w, inc=(per if k == 7 else None))
        pc = per.n
        p.do("vector", (lambda i: lambda e: e.tensor_copy(scs[:], ps_r[:]))(i),
             wait=[(per, pc)], inc=vco)
        p.do("vector", lambda e: e.tensor_reduce(m1[:], scs[:], AX.X, ALU.max), inc=vco)
        p.do("vector", lambda e: e.tensor_scalar(eqm[:], scs[:], m1[:], None, ALU.is_ge), inc=vco)
        p.do("vector", lambda e: e.scalar_tensor_tensor(
            eqm[:], eqm[:], -1e9, scs[:], ALU.mult, ALU.add), inc=vco)
        p.do("vector", lambda e: e.tensor_reduce(m2[:], eqm[:], AX.X, ALU.max), inc=vco)
        vco_c5 = vco.n
        p.do("scalar", (lambda i: lambda e: e.activation(
            prb[:], scs[:], AF.Exp, bias=0.0, scale=1.0))(i), wait=[(vco, vco_c5)])
        p.do("scalar", (lambda i: lambda e: e.activation(
            sqs[:], prb[:], AF.Copy, accum_out=m1[:]))(i), inc=sac)
        p.do("vector", lambda e: e.reciprocal(m1[:], m1[:]), wait=[(sac, sac.n)])
        p.do("vector", lambda e: e.tensor_scalar(prb[:], prb[:], m1[:], None, ALU.mult))
        p.do("vector", lambda e: e.tensor_scalar(eqm[:], scs[:], m2[:], None, ALU.is_ge))
        p.do("vector", lambda e: e.tensor_tensor(prb[:], prb[:], eqm[:], ALU.mult))
        p.do("vector", (lambda: lambda e: e.tensor_reduce(
            m2[:], prb[:], AX.X, ALU.add))())
        p.do("vector", lambda e: e.reciprocal(m2[:], m2[:]))
        p.do("vector", (lambda i: lambda e: e.tensor_scalar(
            coef[:, i * 8:(i + 1) * 8], prb[:], m2[:], None, ALU.mult))(i), inc=vco)
        vco_tile[i] = vco.n
    vco_n = vco.n

    lwv = [p.sem(f"lwv{i}") for i in range(2)]
    peu = p.sem("peu"); srl = p.sem("srl")
    pey = p.sem("pey"); evy = p.sem("evy")
    lr = [p.sem(f"lr{i}") for i in range(2)]
    stx = [p.sem(f"stx{i}") for i in range(2)]
    vfa = p.sem("vfa")
    gi = 0; ci = 0
    peu_after = {}; srl_after = {}; pey_after = {}; lwv_after = {}
    vfa_after = {}; stx_after = {}
    evy_half = {}
    dcnt = 0; pey_tg = {}; gtg = 0

    def emit_fadd(half_, t):
        s = t % 2
        tt = half_ * 16 + t
        g = tt
        w = [(vfa, vfa_after[g - 2])] if g >= 2 else []
        p.dma("sync", rt[s][:], r3in[tt * 128:(tt + 1) * 128, :], wait=w, inc=lr[s])
        w = [(lr[s], lr[s].n), (evy, evy_half[half_])]
        if g >= 2:
            w.append((stx[s], stx_after[g - 2]))
        p.do("gpsimd", (lambda s, t: lambda e: e.tensor_tensor(
            ot[s][:], rt[s][:], yh[:].rearrange("p (t d) -> p t d", d=D)[:, t, :],
            ALU.add))(s, t), wait=w, inc=vfa)
        vfa_after[g] = vfa.n
        p.dma("sync", x4[tt * 128:(tt + 1) * 128, :], ot[s][:],
              wait=[(vfa, vfa_after[g])], inc=stx[s])
        stx_after[g] = stx[s].n

    def emit_downs(pend):
        nonlocal dcnt
        ci_, s_, tg_, gtg_, srl_l, first_, half_, e2 = pend
        if half_ == 1 and first_:
            for ts4 in range(4):
                emit_fadd(0, tg_ * 4 + ts4)
        tb_ = half_ * 16
        for chh in range(2):
            for ts in range(4):
                bank = ps_y[ts % 2]
                for fs in range(4):
                    w = []
                    if fs == 0:
                        w = [(srl, srl_l)]
                        if dcnt >= 2:
                            w.append((evy, evy.n - 1))
                    p.do("tensor", (lambda s_, fs, chh, ts, bank, gtg_: lambda e: e.matmul(
                        bank[:], hbufs[gtg_ % 2][:, fs * 512 + ts * 128: fs * 512 + (ts + 1) * 128],
                        w2b[s_][:].rearrange("p (f c) -> p f c", c=D)[:, fs, chh * 512:(chh + 1) * 512],
                        start=(fs == 0), stop=(fs == 3)))(s_, fs, chh, ts, bank, gtg_),
                        wait=w, inc=(pey if fs == 3 else None))
                dcnt += 1
                pyc = pey.n
                ti_g = tb_ + tg_ * 4 + ts
                wv = [(pey, pyc), (vco, vco_tile[ti_g])]
                if first_ and half_ == 1:
                    wv.append((vfa, vfa_after[tg_ * 4 + ts]))
                if first_:
                    p.do("vector", (lambda bank, ti_g, e2, tg_, ts, chh:
                                    lambda e: e.tensor_scalar(
                        yh[:].rearrange("p (t d) -> p t d", d=D)[:, tg_ * 4 + ts, chh * 512:(chh + 1) * 512],
                        bank[:], coef[:, ti_g * 8 + e2: ti_g * 8 + e2 + 1],
                        None, ALU.mult))(bank, ti_g, e2, tg_, ts, chh),
                        wait=wv, inc=evy)
                else:
                    p.do("vector", (lambda bank, ti_g, e2, tg_, ts, chh:
                                    lambda e: e.scalar_tensor_tensor(
                        yh[:].rearrange("p (t d) -> p t d", d=D)[:, tg_ * 4 + ts, chh * 512:(chh + 1) * 512],
                        bank[:], coef[:, ti_g * 8 + e2: ti_g * 8 + e2 + 1],
                        yh[:].rearrange("p (t d) -> p t d", d=D)[:, tg_ * 4 + ts, chh * 512:(chh + 1) * 512],
                        ALU.mult, ALU.add))(bank, ti_g, e2, tg_, ts, chh),
                        wait=wv, inc=evy)
        pey_tg[gtg_] = pey.n

    for half in range(2):
        tb = half * 16
        pend = None
        for e_ in range(8):
            for fc in range(8):
                s = ci % 2
                w = []
                if ci >= 2:
                    w.append((peu, peu_after[ci - 2]))
                    w.append((pey, pey_tg[gtg - 5]))
                p.dma("gpsimd", w1b[s][:].rearrange("p (k c) -> p k c", c=512),
                      w1_in[e_][:, fc * 512:(fc + 1) * 512].rearrange("(k p) c -> p k c", p=128),
                      wait=w, inc=lwv[s])
                p.dma("gpsimd", w2b[s][:].rearrange("p (f c) -> p f c", c=D),
                      w2_in[e_][fc * 512:(fc + 1) * 512, :].rearrange("(f p) c -> p f c", p=128),
                      inc=lwv[s])
                lwv_after[ci] = lwv[s].n
                for tg in range(4):
                    tok0 = half * 2048 + tg * 512
                    for fs in range(4):
                        bank = ps_u[fs % 2]
                        for k in range(8):
                            w = []
                            if k == 0:
                                w = [(lwv[s], lwv_after[ci]), (lxd, lxd_n)]
                                if gi >= 2:
                                    w.append((srl, srl_after[gi - 2]))
                            p.do("tensor", (lambda s, k, fs, tok0, bank: lambda e: e.matmul(
                                bank[:], w1b[s][:].rearrange("p (k c) -> p k c", c=512)[:, k, fs * 128:(fs + 1) * 128],
                                xndv[:, k, tok0:tok0 + 512],
                                start=(k == 0), stop=(k == 7)))(s, k, fs, tok0, bank),
                                wait=w, inc=(peu if k == 7 else None))
                        peu_after[ci] = peu.n
                        wr_ = [(peu, peu.n)]
                        if fs == 0 and gtg >= 2:
                            wr_.append((pey, pey_tg[gtg - 2]))
                        p.do("scalar", (lambda fs, bank, gtg: lambda e: e.activation(
                            hbufs[gtg % 2][:, fs * 512:(fs + 1) * 512], bank[:], AF.Relu))(fs, bank, gtg),
                            wait=wr_, inc=srl)
                        srl_after[gi] = srl.n
                        gi += 1
                    if pend is not None:
                        emit_downs(pend)
                    pend = (ci, s, tg, gtg, srl_after[gi - 1],
                            (e_ == 0 and fc == 0), half, e_)
                    gtg += 1
                pey_after[ci] = pey.n
                ci += 1
        emit_downs(pend)
        pend = None
        evy_half[half] = evy.n
    for t in range(16):
        emit_fadd(1, t)
    p.do("sync", lambda e: e.nop(), wait=[(s_, s_.n) for s_ in stx])
    return p


def build_fused():
    p = Prog("mk")
    nc = p.nc
    I = lambda name, shape, dt_=BF16: nc.dram_tensor(name, list(shape), dt_, kind="Internal").ap()
    h = {
        "xn0": I("h_xn0", [T, D], F32), "xc_d": I("h_xc", [2048, T]),
        "zsil_d": I("h_zs", [2048, T]), "u_d": I("h_u", [2048, T]),
        "dt_d": I("h_dt", [2048, T]), "dtu_d": I("h_dtu", [2048, T]),
        "Bm": I("h_bm", [16, T], F32), "Cm": I("h_cm", [16, T], F32),
        "y_d": I("h_y", [2048, T]), "x1": I("h_x1", [T, D], F32),
        "xn1d": I("h_xn1d", [D, T]), "q1d": I("h_q1d", [D, TQ]),
        "k1d": I("h_k1d", [D, TQ]), "v1": I("h_v1", [TQ, D]),
        "ao1": I("h_ao1", [T, D]), "r1": I("h_r1", [T, D], F32),
        "x2": I("h_x2", [T, D], F32), "xn2": I("h_xn2", [T, D]),
        "r2": I("h_r2", [T, D], F32), "x3": I("h_x3", [T, D], F32),
        "xn3d": I("h_xn3d", [D, T]), "r3": I("h_r3", [T, D], F32),
    }
    phases = [
        (build_l0a, {"xn_tok": "xn0", "xc_d": "xc_d", "zsil_d": "zsil_d"}),
        (build_l0b1, {"xc_d": "xc_d", "u_d": "u_d"}),
        (build_l0b2, {"u_d": "u_d", "dt_d": "dt_d", "dtu_d": "dtu_d",
                      "Bm": "Bm", "Cm": "Cm"}),
        (build_l0c, {"dt_d": "dt_d", "dtu_d": "dtu_d", "u_d": "u_d",
                     "zsil_d": "zsil_d", "Bm": "Bm", "Cm": "Cm", "y_d": "y_d"}),
        (build_l0d, {"y_d": "y_d", "xn_tok": "xn0", "x1": "x1"}),
        (build_l1a, {"x": "x1", "xn_d": "xn1d", "q_d": "q1d", "k_d": "k1d",
                     "v_tok": "v1"}),
        (build_l1b, {"q_d": "q1d", "k_d": "k1d", "v_tok": "v1",
                     "attnout": "ao1"}),
        (build_l1c, {"attnout": "ao1", "x": "x1", "r1": "r1"}),
        (build_l1d, {"xn_d": "xn1d", "r1": "r1", "x2": "x2"}),
        (build_l2b, {"x": "x2", "xn_tok": "xn2", "r2": "r2"}),
        (build_l2c, {"xn_tok": "xn2", "r2": "r2", "x3": "x3"}),
        (build_l3a, {"x": "x3", "xn_d": "xn3d", "r3": "r3"}),
        (build_l3b, {"xn_d": "xn3d", "r3": "r3", "x4": None}),
    ]
    x4 = nc.dram_tensor("x4", [T, D], F32, kind="ExternalOutput").ap()
    p.outs["x4"] = x4
    for cv in (1e-5, 1e-6, -3.0, -10.0):
        p.cbias(cv)
    for bi, (builder, iomap) in enumerate(phases):
        p.begin_phase()
        p.io_map = {k: (x4 if v is None else h[v]) for k, v in iomap.items()}
        builder(p)
        p.io_map = {}
        p.end_phase()
    return p


import ml_dtypes
from concourse.bass_utils import run_bass_kernel_spmd

_FUSED = [None]


def kernel(**inp):
    NB = 8
    C = np.ascontiguousarray
    x = np.asarray(inp["L0_attn_x"], dtype=np.float32)
    idn = np.eye(128).astype(ml_dtypes.bfloat16)
    idnf = np.eye(128, dtype=np.float32)
    freqs = 32.0 ** (-np.arange(0, 64, 2, dtype=np.float32) / 64)
    ang = np.arange(T, dtype=np.float32)[:, None] * freqs[None, :]
    COS = C(np.cos(ang).astype(np.float32))
    SIN = C(np.sin(ang).astype(np.float32))
    U = np.arange(896)[None, :] - 384
    CM = C(np.where(U < np.arange(128)[:, None], -1e30, 0.0).astype(np.float32))

    w = dict(
        in_proj_wT=C(inp["L0_attn_in_proj_w"].T),
        conv_wT=C(np.asarray(inp["L0_attn_conv_w"]).transpose(2, 1, 0)),
        conv_b_re=C(np.asarray(inp["L0_attn_conv_b"]).reshape(16, 128).T),
        dt_wT=C(inp["L0_attn_dt_w"].T),
        dt_b_re=C(np.asarray(inp["L0_attn_dt_b"]).reshape(16, 128).T),
        B_wT=C(inp["L0_attn_B_w"].T),
        C_wT=C(inp["L0_attn_C_w"].T),
        A_re=C(np.asarray(inp["L0_attn_A_log"]).reshape(16, 128, 16).transpose(1, 0, 2).reshape(128, 256)),
        D_re=C(np.asarray(inp["L0_attn_D"]).reshape(16, 128).T),
        out_wT=C(inp["L0_attn_out_w"].T),
        ffn_w=C(inp["L0_ffn_weight"]),
        ffn_b_re=C(np.asarray(inp["L0_ffn_bias"]).reshape(8, 128).T),
        w_qkv=C(inp["L1_attn_w_qkv"]),
        b_qkv_re=C(np.asarray(inp["L1_attn_b_qkv"]).reshape(24, 128).T),
        b_v=C(np.asarray(inp["L1_attn_b_qkv"])[2048:]),
        w_out=C(inp["L1_attn_w_out"]),
        b_out=C(inp["L1_attn_b_out"]),
        W1T=C(inp["L1_ffn_W1"].T),
        b1_re=C(np.asarray(inp["L1_ffn_b1"]).reshape(32, 128).T),
        W2T=C(inp["L1_ffn_W2"].T),
        b2=C(inp["L1_ffn_b2"]),
        Wk=C(inp["L2_attn_Wk"]),
        Wq=C(inp["L2_attn_Wq"]),
        Wo=C(inp["L2_attn_Wo"]),
        bo=C(inp["L2_attn_bo"]),
        norm_scale=C(inp["L2_ffn_norm_scale"]),
        w_router_T=C(inp["L2_ffn_w_router"].T),
        w_upT=C(np.asarray(inp["L2_ffn_w_up"]).transpose(0, 2, 1)),
        w_downT=C(np.asarray(inp["L2_ffn_w_down"]).transpose(0, 2, 1)),
        w_qkvT=C(inp["L3_attn_w_qkv"].T),
        w_projT=C(inp["L3_attn_w_proj"].T),
        q_norm_w=C(inp["L3_attn_q_norm_w"]),
        k_norm_w=C(inp["L3_attn_k_norm_w"]),
        Wr=C(inp["L3_ffn_Wr"]),
        W1=C(inp["L3_ffn_W1"]),
        W2=C(inp["L3_ffn_W2"]),
        idn=idn, idnf=idnf, costab=COS, sintab=SIN, cmask=CM,
    )

    if _FUSED[0] is None:
        p = build_fused()
        _FUSED[0] = p.finalize()
    nc = _FUSED[0]
    maps = [dict(w, x=C(x[b])) for b in range(NB)]
    res = run_bass_kernel_spmd(nc, maps, core_ids=list(range(NB)))
    return np.stack([np.asarray(res.results[b]["x4"]) for b in range(NB)], axis=0)
